# version 12
# baseline (speedup 1.0000x reference)
import numpy as np
import ml_dtypes
import concourse.bass as bass
import concourse.mybir as mybir
from concourse.bass_utils import run_bass_kernel_spmd
from contextlib import ExitStack

F32 = mybir.dt.float32
BF16 = mybir.dt.bfloat16
I16 = mybir.dt.int16
F16 = mybir.dt.float16
AF = mybir.ActivationFunctionType
ALU = mybir.AluOpType
AX = mybir.AxisListType

H = 2048
NH = 16
HD = 128
TOWN = 1024
TALL = 2048
IM = 5632
EPS_LN = 1e-5
EPS_RMS = 1e-6
SCALE_S = float(1.0 / np.sqrt(HD))
NEG = -1e30
NKC_DN = 44

BUILD_PHASES = 99


class Sem:
    def __init__(self, nc, name):
        self.h = nc.alloc_semaphore(name=name)
        self.n = 0

    def inc(self, inst, k=1):
        inst.then_inc(self.h, k)
        self.n += k
        return self.n


def build(debug=()):
    nc = bass.Bass("TRN2", target_bir_lowering=False)

    xh = nc.dram_tensor("xh", [16, 128, H], F32, kind="ExternalInput").ap()
    wqkv = nc.dram_tensor("wqkv", [48, 128, NH, 128], F32, kind="ExternalInput").ap()
    wo = nc.dram_tensor("wo", [16, 128, NH, 128], F32, kind="ExternalInput").ap()
    qw = nc.dram_tensor("qw", [128, NH], F32, kind="ExternalInput").ap()
    kw = nc.dram_tensor("kw", [128, NH], F32, kind="ExternalInput").ap()
    gup16m = nc.dram_tensor("gup16m", [1536, 512], I16, kind="ExternalInput").ap()
    gusm = nc.dram_tensor("gusm", [16, 1536], F16, kind="ExternalInput").ap()
    dnp16m = nc.dram_tensor("dnp16m", [256, 1408], I16, kind="ExternalInput").ap()
    dnsm = nc.dram_tensor("dnsm", [44, 256], F16, kind="ExternalInput").ap()
    cosd = nc.dram_tensor("cosd", [128, TALL], BF16, kind="ExternalInput").ap()
    sind = nc.dram_tensor("sind", [128, TALL], BF16, kind="ExternalInput").ap()
    identb = nc.dram_tensor("identb", [128, 128], BF16, kind="ExternalInput").ap()
    identf = nc.dram_tensor("identf", [128, 128], F32, kind="ExternalInput").ap()
    fmask = nc.dram_tensor("fmask", [128, 128], BF16, kind="ExternalInput").ap()
    ctri = nc.dram_tensor("ctri", [128, 128], BF16, kind="ExternalInput").ap()
    vm01 = nc.dram_tensor("vm01", [128, 128], BF16, kind="ExternalInput").ap()
    vones = nc.dram_tensor("vones", [128, 9 * NH], BF16, kind="ExternalInput").ap()
    onesrow = nc.dram_tensor("onesrow", [1, 128], F32, kind="ExternalInput").ap()
    onescol = nc.dram_tensor("onescol", [128, 1], BF16, kind="ExternalInput").ap()
    bcast4 = nc.dram_tensor("bcast4", [4, 128], BF16, kind="ExternalInput").ap()
    constv = nc.dram_tensor("constv", [128, 4], F32, kind="ExternalInput").ap()

    out = nc.dram_tensor("out", [8, 128, H], F32, kind="ExternalOutput").ap()

    v_sp = nc.dram_tensor("v_sp", [16, 128, H], BF16, kind="Internal").ap()
    wgu_cin = nc.dram_tensor("wgu_cin", [3 * 16 * 128, 512], BF16, kind="Internal")
    wgu_g = nc.dram_tensor("wgu_g", [8 * 3 * 16 * 128, 512], BF16, kind="Internal",
                           addr_space="Shared")
    wdn_cin = nc.dram_tensor("wdn_cin", [44 * 128, 256], BF16, kind="Internal")
    wdn_g = nc.dram_tensor("wdn_g", [8 * 44 * 128, 256], BF16, kind="Internal",
                           addr_space="Shared")
    x2_sp = nc.dram_tensor("x2_sp", [8, 128, H], F32, kind="Internal").ap()
    h_sp = nc.dram_tensor("h_sp", [NKC_DN, 128, TOWN], BF16, kind="Internal").ap()

    dbg_outs = {}

    root = ExitStack()
    with root:
        def sb(name, shape, dtype, side="left"):
            return root.enter_context(nc.sbuf_tensor(name, shape, dtype, side=side))

        _phase_sems = []

        def sem(name, persistent=False):
            s = Sem(nc, name)
            if not persistent:
                _phase_sems.append(s)
            return s

        def free_phase_sems():
            if not _phase_sems:
                return
            with nc.Block() as blk:
                @blk.sync
                def _(sync):
                    for s in _phase_sems:
                        sync.sem_clear(s.h)
            for s in _phase_sems:
                nc.release_semaphore(s.h)
            _phase_sems.clear()
            last.clear()

        s_identb = sb("c_identb", [128, 128], BF16)
        s_identf = sb("c_identf", [128, 128], F32)
        s_cv = sb("c_cv", [128, 4], F32)
        s_onesrow = sb("c_onesrow", [1, 128], F32)
        s_onescol = sb("c_onescol", [128, 1], BF16)
        kwsb = sb("c_kwsb", [128, NH], F32)
        qwsb = sb("c_qwsb", [128, NH], F32)
        cossb = sb("c_cossb", [128, TALL], BF16)
        sinsb = sb("c_sinsb", [128, TALL], BF16)

        s_cons = sem("s_cons", persistent=True)
        s_dbg = sem("s_dbg", persistent=True)
        s_vsp = [sem("s_vsp0", persistent=True), sem("s_vsp1", persistent=True)]
        s_x2sp = sem("s_x2sp", persistent=True)
        s_hsp = [sem("s_hsp0", persistent=True), sem("s_hsp1", persistent=True)]
        s_outst = [sem("s_outst0", persistent=True), sem("s_outst1", persistent=True)]

        selfs = {e: sem(f"self_{e}", persistent=True) for e in ("v", "s", "t", "g")}
        last = {}

        def mark(e, inst, s=None, k=1):
            s = s if s is not None else selfs[e]
            s.inc(inst, k)
            last[e] = (s.h, s.n)
            return inst

        def bar(e, eng):
            if e in last:
                eng.wait_ge(*last[e])

        def tap(name, src_ap, shape, dtype):
            if name not in debug:
                return
            t = nc.dram_tensor("dbg_" + name, shape, dtype, kind="ExternalOutput").ap()
            dbg_outs[name] = t
            with nc.Block() as blk:
                @blk.gpsimd
                def _(gp):
                    s_dbg.inc(gp.dma_start(t, src_ap), 16)
                    gp.wait_ge(s_dbg.h, s_dbg.n)

        with nc.Block() as blk:
            @blk.sync
            def _(sync):
                for t, d in [(s_identb, identb), (s_identf, identf), (s_cv, constv),
                             (s_onesrow, onesrow), (s_onescol, onescol),
                             (kwsb, kw), (qwsb, qw), (cossb, cosd), (sinsb, sind)]:
                    s_cons.inc(sync.dma_start(t[:], d), 16)
                sync.wait_ge(s_cons.h, s_cons.n)

        EPSLN = s_cv[:, 0:1]
        EPSRMS = s_cv[:, 1:2]

        s_ccg = sem("s_ccg", persistent=True)
        s_ccd = sem("s_ccd", persistent=True)
        ln1T_ctx = ExitStack()
        ln1T = ln1T_ctx.enter_context(nc.sbuf_tensor("ln1T", [128, NH, TALL], BF16))
        pdq_ctx = ExitStack()
        pdq_env = dict(nc=nc, sem=sem, mark=mark, bar=bar, pdq_ctx=pdq_ctx,
                       gup16m=gup16m, gusm=gusm, dnp16m=dnp16m, dnsm=dnsm,
                       bcast4=bcast4, wgu_cin=wgu_cin, wgu_g=wgu_g,
                       wdn_cin=wdn_cin, wdn_g=wdn_g, s_ccg=s_ccg, s_ccd=s_ccd)
        if BUILD_PHASES >= 5:
            build_pdq(pdq_env)
        PB = pdq_env.get("pdq_bufs") if BUILD_PHASES >= 5 else None
        s_dqg = sem("s_dqg", persistent=True)
        s_dqd = sem("s_dqd", persistent=True)
        s_spq = [sem("s_spq0", persistent=True), sem("s_spq1", persistent=True)]

        def gate_plane(vector, j):
            hb = j // 8
            Pl = (j % 8) // 4
            i_ = j % 4
            b, Ph = hb // 2, (hb % 2) * 2
            P_ = Ph + Pl
            if j % 8 == 0 and hb >= 2:
                vector.wait_ge(s_spq[hb % 2].h, 16 * (hb // 2))
            mark("v", _dequant8(vector, mark, bar,
                                PB["pkq"][:, 4 * b + P_, :],
                                PB["sclq"][:, 4 * b + P_, :],
                                PB["wsl"][:, hb % 2, 4 * Pl + i_, :],
                                PB["dqs"], i_), s_dqg)

        with ExitStack() as pctx:
            psb = lambda n, s, d: pctx.enter_context(nc.sbuf_tensor(n, s, d))
            pps = lambda n, s, d: pctx.enter_context(nc.psum_tensor(n, s, d))
            x_in = [pctx.enter_context(nc.sbuf_tensor(f"p0_x{i}", [128, H], F32,
                                                       side="right")) for i in range(4)]
            xsq = pctx.enter_context(nc.sbuf_tensor("p0_xsq", [128, H], BF16, side="right"))
            ln1tm = [pctx.enter_context(nc.sbuf_tensor(f"p0_ln1tm{i}", [128, H], BF16,
                                                       side="right")) for i in range(2)]
            srow = pctx.enter_context(nc.sbuf_tensor("p0_srow", [128, 16], F32, side="right"))
            tr = [pps(f"p0_tr{i}", [128, 512], BF16) for i in range(2)]

            s_xld = [sem(f"p0_xld{i}") for i in range(4)]
            s_sq = sem("p0_sq")
            s_var = sem("p0_var")
            s_sd = sem("p0_sd")
            s_app = sem("p0_app")
            s_tr = sem("p0_tr")
            s_ev = sem("p0_ev")

            with nc.Block() as blk:
                @blk.sync
                def _(sync):
                    for tt in range(16):
                        b = tt % 4
                        if tt >= 4:
                            sync.wait_ge(s_app.h, tt - 3)
                            sync.wait_ge(s_sq.h, tt - 3)
                        s_xld[b].inc(sync.dma_start(x_in[b][:], xh[tt]), 16)

                @blk.scalar
                def _(scalar):
                    for tt in range(16):
                        b = tt % 4
                        o = 8 * (tt % 2)
                        scalar.wait_ge(s_xld[b].h, 16 * (tt // 4 + 1))
                        if tt >= 2:
                            scalar.wait_ge(s_var.h, tt - 1)
                        mark("s", scalar.activation(xsq[:], x_in[b][:], AF.Square,
                                                    accum_out=srow[:, o + 1:o + 2]), s_sq)
                        scalar.wait_ge(s_var.h, tt + 1)
                        mark("s", scalar.activation(srow[:, o + 5:o + 6], srow[:, o + 4:o + 5],
                                                    AF.Sqrt, bias=EPSLN), s_sd)
                        for g in range(4):
                            scalar.wait_ge(s_tr.h, 4 * tt + g + 1)
                            mark("s", scalar.copy(
                                ln1T[:, 4 * g:4 * g + 4, 128 * tt:128 * (tt + 1)],
                                tr[g % 2][:].rearrange("p (a t) -> p a t", a=4)), s_ev)

                @blk.vector
                def _(vector):
                    ndq = 0
                    if PB is not None:
                        vector.wait_ge(pdq_env["s_pkt"].h, 16 * 23)
                        vector.wait_ge(pdq_env["s_bce"].h, 23)
                    for tt in range(16):
                        b = tt % 4
                        o = 8 * (tt % 2)
                        vector.wait_ge(s_xld[b].h, 16 * (tt // 4 + 1))
                        bar("v", vector)
                        mark("v", vector.reduce_sum(srow[:, o + 0:o + 1], x_in[b][:], axis=AX.X))
                        vector.wait_ge(s_sq.h, tt + 1)
                        bar("v", vector)
                        mark("v", vector.tensor_scalar(srow[:, o + 2:o + 3], srow[:, o + 0:o + 1],
                                                       1.0 / H, None, ALU.mult))
                        bar("v", vector)
                        mark("v", vector.tensor_tensor(srow[:, o + 3:o + 4], srow[:, o + 2:o + 3],
                                                       srow[:, o + 2:o + 3], ALU.mult))
                        bar("v", vector)
                        mark("v", vector.scalar_tensor_tensor(
                            srow[:, o + 4:o + 5], srow[:, o + 1:o + 2], 1.0 / H,
                            srow[:, o + 3:o + 4], ALU.mult, ALU.subtract), s_var)
                        vector.wait_ge(s_sd.h, tt + 1)
                        bar("v", vector)
                        mark("v", vector.reciprocal(srow[:, o + 6:o + 7], srow[:, o + 5:o + 6]))
                        bar("v", vector)
                        mark("v", vector.scalar_tensor_tensor(
                            srow[:, o + 7:o + 8], srow[:, o + 2:o + 3], -1.0,
                            srow[:, o + 6:o + 7], ALU.mult, ALU.mult))
                        if tt >= 2:
                            vector.wait_ge(s_tr.h, 4 * (tt - 1))
                        bar("v", vector)
                        mark("v", vector.tensor_scalar(ln1tm[tt % 2][:], x_in[b][:],
                                                       srow[:, o + 6:o + 7], srow[:, o + 7:o + 8],
                                                       ALU.mult, ALU.add), s_app)
                        if PB is not None:
                            want = (tt + 1) * 24 // 16
                            while ndq < want:
                                gate_plane(vector, ndq)
                                ndq += 1

                @blk.tensor
                def _(tensor):
                    for tt in range(16):
                        b = tt % 2
                        tensor.wait_ge(s_app.h, tt + 1)
                        for g in range(4):
                            gi = 4 * tt + g
                            if gi >= 2:
                                tensor.wait_ge(s_ev.h, gi - 1)
                            for j in range(4):
                                ht = 4 * g + j
                                inst = tensor.transpose(
                                    tr[gi % 2][:, 128 * j:128 * (j + 1)],
                                    ln1tm[b][:, 128 * ht:128 * (ht + 1)], s_identb[:])
                            mark("t", inst, s_tr)

                @blk.gpsimd
                def _(g):
                    if PB is not None:
                        for hb in range(2):
                            g.wait_ge(s_dqg.h, 8 * (hb + 1))
                            s_spq[hb % 2].inc(g.dma_start(
                                wgu_cin.ap()[1024 * hb:1024 * (hb + 1), :]
                                .rearrange("(k p) f -> p k f", p=128),
                                PB["wsl"][:, hb % 2, :, :]), 16)
                    g.wait_ge(s_ev.h, 64)

        free_phase_sems()
        tap("ln1T", ln1T[:], [128, NH, TALL], BF16)
        if BUILD_PHASES < 1:
            ln1T_ctx.close()
            return nc, dbg_outs

        with ExitStack() as pctx:
            psb = lambda n, s, d: pctx.enter_context(nc.sbuf_tensor(n, s, d))
            pps = lambda n, s, d: pctx.enter_context(nc.psum_tensor(n, s, d))
            wvb = [pctx.enter_context(nc.sbuf_tensor(f"p1a_wvb{i}", [128, NH, 512], BF16,
                                                      side="right")) for i in range(2)]
            vstage = [pctx.enter_context(nc.sbuf_tensor(f"p1a_vst{i}", [128, NH, 512], BF16,
                                                        side="right")) for i in range(2)]
            acc = [pps(f"p1a_acc{i}", [128, 512], F32) for i in range(2)]

            s_wvb = [sem(f"p1a_wvb{i}") for i in range(2)]
            s_mm = sem("p1a_mm")
            s_evv = sem("p1a_evv")

            with nc.Block() as blk:
                @blk.gpsimd
                def _(g):
                    for fg in range(4):
                        b = fg % 2
                        if fg >= 2:
                            g.wait_ge(s_mm.h, 16 * (fg - 1))
                        for j in range(4):
                            s_wvb[b].inc(g.dma_start(
                                wvb[b][:].rearrange("p a (j f) -> p a j f", j=4)[:, :, j, :],
                                wqkv[32 + 4 * fg + j]), 16)
                    if PB is not None:
                        for hb in range(2, 6):
                            g.wait_ge(s_dqg.h, 8 * (hb + 1))
                            s_spq[hb % 2].inc(g.dma_start(
                                wgu_cin.ap()[1024 * hb:1024 * (hb + 1), :]
                                .rearrange("(k p) f -> p k f", p=128),
                                PB["wsl"][:, hb % 2, :, :]), 16)
                        for tp in range(11):
                            g.wait_ge(s_dqd.h, 4 * (tp + 1))
                            it = 6 + tp
                            s_spq[it % 2].inc(g.dma_start(
                                wdn_cin.ap()[512 * tp:512 * (tp + 1), :]
                                .rearrange("(k p) f -> p k f", p=128),
                                PB["wsl"][:, it % 2, 0:4, 0:256]), 16)
                        g.wait_ge(s_spq[0].h, s_spq[0].n)
                        g.wait_ge(s_spq[1].h, s_spq[1].n)

                @blk.sync
                def _(sync):
                    for fg in range(4):
                        sync.wait_ge(s_evv.h, 16 * (fg + 1))
                        sbf = fg % 2
                        s_vsp[sbf].inc(sync.dma_start(
                            v_sp[:, :, 512 * fg:512 * (fg + 1)].rearrange("t p f -> p t f"),
                            vstage[sbf][:]), 16)
                    sync.wait_ge(s_vsp[0].h, s_vsp[0].n)
                    sync.wait_ge(s_vsp[1].h, s_vsp[1].n)

                @blk.vector
                def _(vector):
                    if PB is not None:
                        for j in range(24, 48):
                            gate_plane(vector, j)
                        for tp in range(11):
                            it = 6 + tp
                            vector.wait_ge(s_spq[it % 2].h, 16 * (it // 2))
                            for i_ in range(4):
                                mark("v", _dequant8(vector, mark, bar,
                                                    PB["dpkq"][:, tp, :],
                                                    PB["scld"][:, tp, :],
                                                    PB["wsl"][:, it % 2, i_, 0:256],
                                                    PB["dqs"], i_), s_dqd)

                @blk.tensor
                def _(tensor):
                    nmm = 0
                    for fg in range(4):
                        b = fg % 2
                        tensor.wait_ge(s_wvb[b].h, 64 * (fg // 2 + 1))
                        for tt in range(16):
                            pb = nmm % 2
                            if nmm >= 2:
                                tensor.wait_ge(s_evv.h, nmm - 1)
                            for hc in range(NH):
                                inst = tensor.matmul(acc[pb][:],
                                                     ln1T[:, hc, 128 * tt:128 * (tt + 1)],
                                                     wvb[b][:, hc, :],
                                                     start=(hc == 0), stop=(hc == NH - 1))
                            mark("t", inst, s_mm)
                            nmm += 1

                @blk.scalar
                def _(scalar):
                    nmm = 0
                    for fg in range(4):
                        sbf = fg % 2
                        for tt in range(16):
                            pb = nmm % 2
                            scalar.wait_ge(s_mm.h, nmm + 1)
                            if fg >= 2 and tt == 0:
                                scalar.wait_ge(s_vsp[fg % 2].h, 16 * (fg // 2))
                            mark("s", scalar.copy(vstage[sbf][:, tt, :], acc[pb][:]), s_evv)
                            nmm += 1

        free_phase_sems()
        pdq_ctx.close()
        if BUILD_PHASES < 2:
            ln1T_ctx.close()
            return nc, dbg_outs

        kq_ctx = ExitStack()
        kT = kq_ctx.enter_context(nc.sbuf_tensor("kT", [128, NH, TALL], BF16, side="right"))
        qT = kq_ctx.enter_context(nc.sbuf_tensor("qT", [128, NH, TOWN], BF16, side="right"))
        rsbk = kq_ctx.enter_context(nc.sbuf_tensor("rsbk", [128, TALL], F32, side="right"))
        rsbq = kq_ctx.enter_context(nc.sbuf_tensor("rsbq", [128, TOWN], F32, side="right"))

        p1b_ctx = ExitStack()
        psb = lambda n, s, d: p1b_ctx.enter_context(nc.sbuf_tensor(n, s, d))
        pps = lambda n, s, d: p1b_ctx.enter_context(nc.psum_tensor(n, s, d))
        wkb = [psb(f"p1b_wkb{i}", [128, NH, 128], BF16) for i in range(2)]
        wkf = [psb(f"p1b_wkf{i}", [128, 8, 128], F32) for i in range(2)]
        sqv = psb("p1b_sqv", [128, TALL], BF16)
        tmpA = psb("p1b_tmpA", [128, 1024], BF16)
        tmpB = psb("p1b_tmpB", [128, 1024], BF16)
        acc = [pps(f"p1b_acc{i}", [128, 512], F32) for i in range(2)]
        ssqk = pps("p1b_ssqk", [1, TALL], F32)
        ssqq = pps("p1b_ssqq", [1, TOWN], F32)

        wl = [("k", ft) for ft in range(16)] + [("q", ft) for ft in range(16)]
        items = []
        for kind, ft in wl:
            for g in range(4 if kind == "k" else 2):
                items.append((kind, ft, g))
        NIT = len(items)

        s_wkb = [sem(f"p1b_wkb{i}") for i in range(2)]
        s_cvt2 = sem("p1b_cvt")
        s_mm2 = sem("p1b_mm")
        s_evk = sem("p1b_evk")
        s_sq2 = sem("p1b_sq2")
        s_ones = sem("p1b_ones")
        s_ropeK = sem("p1b_ropeK")
        s_ropeQ = sem("p1b_ropeQ")

        def groups_done_through(li):
            return sum(4 if k == "k" else 2 for k, _ in wl[:li + 1])

        def sqv_slice(k2, g2, ft2):
            s = g2 if k2 == "k" else (g2 + 2 * (ft2 % 2))
            return sqv[:, 512 * s:512 * (s + 1)]

        def ones_mm(tensor, lag):
            k2, f2, g2 = items[lag]
            tensor.wait_ge(s_sq2.h, lag + 1)
            tgt = ssqk[:, 512 * g2:512 * (g2 + 1)] if k2 == "k" \
                else ssqq[:, 512 * g2:512 * (g2 + 1)]
            inst = tensor.matmul(tgt, s_onescol[:], sqv_slice(k2, g2, f2),
                                 start=(f2 == 0), stop=(f2 == 15))
            if f2 == 15:
                mark("t", inst, s_ones)

        with nc.Block() as blk:
            @blk.sync
            def _(sync):
                if BUILD_PHASES >= 5:
                    sync.wait_ge(pdq_env["s_pkt"].h, 16 * 23)
                for li, (kind, ft) in enumerate(wl):
                    tile = (16 if kind == "k" else 0) + ft
                    for hh in range(2):
                        j = 2 * li + hh
                        if j >= 2:
                            sync.wait_ge(s_cvt2.h, j - 1)
                        s_wkb[j % 2].inc(sync.dma_start(
                            wkf[j % 2][:], wqkv[tile, :, 8 * hh:8 * (hh + 1), :]), 16)

            @blk.tensor
            def _(tensor):
                ni = 0
                for li, (kind, ft) in enumerate(wl):
                    b = li % 2
                    tensor.wait_ge(s_cvt2.h, 2 * (li + 1))
                    ngr = 4 if kind == "k" else 2
                    for g in range(ngr):
                        pb = ni % 2
                        if ni >= 2:
                            tensor.wait_ge(s_evk.h, ni - 1)
                        src0 = 0 if kind == "k" else TOWN
                        for hc in range(NH):
                            inst = tensor.matmul(
                                acc[pb][:], wkb[b][:, hc, :],
                                ln1T[:, hc, src0 + 512 * g:src0 + 512 * (g + 1)],
                                start=(hc == 0), stop=(hc == NH - 1))
                        mark("t", inst, s_mm2)
                        ni += 1
                        if ni - 4 >= 0:
                            ones_mm(tensor, ni - 4)
                for lag in range(NIT - 3, NIT):
                    ones_mm(tensor, lag)

            @blk.scalar
            def _(scalar):
                ni = 0
                for li, (kind, ft) in enumerate(wl):
                    ngr = 4 if kind == "k" else 2
                    for g in range(ngr):
                        pb = ni % 2
                        scalar.wait_ge(s_mm2.h, ni + 1)
                        dst = kT[:, ft, 512 * g:512 * (g + 1)] if kind == "k" \
                            else qT[:, ft, 512 * g:512 * (g + 1)]
                        wcol = kwsb[:, ft:ft + 1] if kind == "k" else qwsb[:, ft:ft + 1]
                        mark("s", scalar.activation(sqv_slice(kind, g, ft), acc[pb][:],
                                                    AF.Square), s_sq2)
                        mark("s", scalar.activation(dst, acc[pb][:], AF.Copy,
                                                    scale=wcol), s_evk)
                        ni += 1

            @blk.vector
            def _(vector):
                def rope(kind2, ft2):
                    if kind2 == "k":
                        kTT, W, c0 = kT, TALL, 0
                        done = 4 * ft2 + 4
                        rsem = s_ropeK
                    else:
                        kTT, W, c0 = qT, TOWN, TOWN
                        done = 64 + 2 * ft2 + 2
                        rsem = s_ropeQ
                    vector.wait_ge(s_evk.h, done)
                    for hf in range(W // 1024):
                        c1 = c0 + 1024 * hf
                        x1 = kTT[0:64, ft2, 1024 * hf:1024 * (hf + 1)]
                        x2 = kTT[64:128, ft2, 1024 * hf:1024 * (hf + 1)]
                        cl = cossb[0:64, c1:c1 + 1024]
                        ch = cossb[64:128, c1:c1 + 1024]
                        sl = sinsb[0:64, c1:c1 + 1024]
                        sh = sinsb[64:128, c1:c1 + 1024]
                        bar("v", vector)
                        mark("v", vector.tensor_mul(tmpA[0:64, :], x1, cl))
                        mark("v", vector.tensor_mul(tmpB[0:64, :], x2, sh))
                        mark("v", vector.tensor_mul(tmpA[64:128, :], x2, ch))
                        mark("v", vector.tensor_mul(tmpB[64:128, :], x1, sl))
                        bar("v", vector)
                        mark("v", vector.tensor_sub(x1, tmpA[0:64, :], tmpB[0:64, :]))
                        mark("v", vector.tensor_add(x2, tmpA[64:128, :], tmpB[64:128, :]))
                    rsem.inc(vector.engine_nop())

                for li, (kind, ft) in enumerate(wl):
                    b = li % 2
                    for hh in range(2):
                        j = 2 * li + hh
                        vector.wait_ge(s_wkb[j % 2].h, 16 * (j // 2 + 1))
                        if li >= 2 and hh == 0:
                            vector.wait_ge(s_mm2.h, groups_done_through(li - 2))
                        bar("v", vector)
                        mark("v", vector.tensor_copy(
                            wkb[b][:, 8 * hh:8 * (hh + 1), :].rearrange("p a f -> p (a f)"),
                            wkf[j % 2][:].rearrange("p a f -> p (a f)")), s_cvt2)
                    if 0 <= li - 2 < 16:
                        rope("k", li - 2)
                    elif li - 2 >= 16:
                        rope("q", li - 18)
                for ft2 in (14, 15):
                    rope("q", ft2)

            @blk.sync
            def _(sync):
                sync.wait_ge(s_ones.h, 6)

        c_row = sem("p1c_row")
        c_rec = sem("p1c_rec")
        c_bc = sem("p1c_bc")
        c_bev = sem("p1c_bev")
        rrowk = rsbk[0:1, :]
        rrowq = rsbq[0:1, :]
        with nc.Block() as blk:
            @blk.scalar
            def _(scalar):
                mark("s", scalar.activation(rrowk, ssqk[0:1, :], AF.Sqrt,
                                            bias=EPSRMS[0:1, :], scale=1.0 / H), c_row)
                mark("s", scalar.activation(rrowq, ssqq[0:1, :], AF.Sqrt,
                                            bias=EPSRMS[0:1, :], scale=1.0 / H), c_row)
                for gi in range(6):
                    scalar.wait_ge(c_bc.h, gi + 1)
                    dst = rsbk[:, 512 * gi:512 * (gi + 1)] if gi < 4 \
                        else rsbq[:, 512 * (gi - 4):512 * (gi - 3)]
                    mark("s", scalar.copy(dst, acc[0][:]), c_bev)

            @blk.vector
            def _(vector):
                vector.wait_ge(c_row.h, 2)
                bar("v", vector)
                mark("v", vector.reciprocal(rrowk, rrowk))
                bar("v", vector)
                mark("v", vector.reciprocal(rrowq, rrowq))
                bar("v", vector)
                mark("v", vector.tensor_scalar(rrowq, rrowq, SCALE_S, None, ALU.mult), c_rec)

            @blk.tensor
            def _(tensor):
                tensor.wait_ge(c_rec.h, 1)
                for gi in range(6):
                    if gi >= 1:
                        tensor.wait_ge(c_bev.h, gi)
                    row = rsbk[0:1, 512 * gi:512 * (gi + 1)] if gi < 4 \
                        else rsbq[0:1, 512 * (gi - 4):512 * (gi - 3)]
                    mark("t", tensor.matmul(acc[0][:], s_onesrow[:], row,
                                            start=True, stop=True), c_bc)

            @blk.gpsimd
            def _(g):
                g.wait_ge(c_bev.h, 6)

        free_phase_sems()
        p1b_ctx.close()
        ln1T_ctx.close()

        tap("kTraw", kT[:], [128, NH, TALL], BF16)
        tap("qTraw", qT[:], [128, NH, TOWN], BF16)
        tap("rsbk", rsbk[:], [128, TALL], F32)
        tap("rsbq", rsbq[:], [128, TOWN], F32)

        if BUILD_PHASES < 3:
            kq_ctx.close()
            return nc, dbg_outs

        env = dict(nc=nc, sem=sem, mark=mark, bar=bar, root=root,
                   kT=kT, qT=qT, rsbk=rsbk, rsbq=rsbq,
                   v_sp=v_sp, s_identb=s_identb, s_identf=s_identf,
                   fmask=fmask, ctri=ctri, vm01=vm01, vones=vones,
                   s_vsp=s_vsp, EPSRMS=EPSRMS,
                   wgu_cin=wgu_cin, wgu_g=wgu_g, wdn_cin=wdn_cin, wdn_g=wdn_g,
                   s_ccg=s_ccg, s_ccd=s_ccd, do_cc=(BUILD_PHASES >= 5))
        oT_ctx = ExitStack()
        oT = oT_ctx.enter_context(nc.sbuf_tensor("oT", [128, NH, TOWN], BF16))
        env["oT"] = oT
        build_p2(env)
        free_phase_sems()

        tap("kT", kT[:], [128, NH, TALL], BF16)
        tap("qT", qT[:], [128, NH, TOWN], BF16)
        tap("oT", oT[:], [128, NH, TOWN], BF16)

        kq_ctx.close()
        if BUILD_PHASES < 4:
            oT_ctx.close()
            return nc, dbg_outs

        ln2_ctx = ExitStack()
        ln2pT = ln2_ctx.enter_context(nc.sbuf_tensor("ln2pT", [128, NH, TOWN], BF16, side="right"))
        env.update(dict(wo=wo, xh=xh, x2_sp=x2_sp, s_x2sp=s_x2sp, ln2pT=ln2pT,
                        EPSLN=EPSLN))
        build_p3(env)
        free_phase_sems()
        oT_ctx.close()

        tap("ln2pT", ln2pT[:], [128, NH, TOWN], BF16)

        if BUILD_PHASES < 5:
            ln2_ctx.close()
            return nc, dbg_outs

        env.update(dict(wgu_g=wgu_g, wdn_g=wdn_g, s_ccg=s_ccg, s_ccd=s_ccd,
                        h_sp=h_sp, s_hsp=s_hsp, out=out, s_outst=s_outst))
        build_p4(env)
        free_phase_sems()
        ln2_ctx.close()
        tap("h_sp", h_sp[:], [NKC_DN, 128, TOWN], BF16)
        build_p5(env)

    return nc, dbg_outs


def build_p2(env):
    nc = env["nc"]
    sem, mark, bar, root = env["sem"], env["mark"], env["bar"], env["root"]
    kT, qT, rsbk, rsbq = env["kT"], env["qT"], env["rsbk"], env["rsbq"]
    v_sp, s_identb = env["v_sp"], env["s_identb"]
    fmask, ctri, vm01, vones = env["fmask"], env["ctri"], env["vm01"], env["vones"]
    s_vsp, oT = env["s_vsp"], env["oT"]

    p2 = ExitStack()
    psb = lambda n, s, d: p2.enter_context(nc.sbuf_tensor(n, s, d))
    pps = lambda n, s, d: p2.enter_context(nc.psum_tensor(n, s, d))
    vroll = psb("p2_vroll", [128, 9, NH * 129], BF16)
    PT_sb = [psb(f"p2_P{i}", [128, 1152], BF16) for i in range(2)]
    o_sb = psb("p2_osb", [128, 2, 128], BF16)
    rrq = psb("p2_rr", [128, 2], F32)
    fmask_sb = psb("p2_fmask", [128, 128], BF16)
    ctri_sb = psb("p2_ctri", [128, 128], BF16)
    vm_sb = psb("p2_vm", [128, 128], BF16)
    vones_sb = psb("p2_vones", [128, 9 * NH], BF16)
    S_ps = [pps(f"p2_S{i}", [128, 1152], F32) for i in range(2)]
    o_ps = pps("p2_ops", [128, 129], F32)
    oTr_ps = pps("p2_otrp", [128, 128], BF16)

    s_msk_ld = sem("p2_mskld")
    s_von = sem("p2_von")
    s_vl = [sem(f"p2_vl{i}") for i in range(9)]
    s_kap = sem("p2_kap")
    s_qap = sem("p2_qap")
    s_S = sem("p2_S")
    s_exp = sem("p2_exp")
    s_trim = sem("p2_trim")
    s_pv = sem("p2_pv")
    s_rr = sem("p2_rr")
    s_oev = sem("p2_oev")
    s_otr = sem("p2_otr")
    s_ote = sem("p2_ote")

    NT = 128

    def qi_h(n):
        return n // 16, n % 16

    def vre(sl):
        return vroll[:, sl, :].rearrange("p (h c) -> p h c", c=129)[:, :, 0:128]

    with nc.Block() as blk:
        @blk.sync
        def _(sync):
            sync.dma_start(fmask_sb[:], fmask).then_inc(s_msk_ld.h, 16)
            sync.dma_start(ctri_sb[:], ctri).then_inc(s_msk_ld.h, 16)
            sync.dma_start(vm_sb[:], vm01).then_inc(s_msk_ld.h, 16)
            sync.dma_start(vones_sb[:], vones).then_inc(s_msk_ld.h, 16)
            s_msk_ld.n += 64
            sync.wait_ge(s_vsp[0].h, s_vsp[0].n)
            sync.wait_ge(s_vsp[1].h, s_vsp[1].n)
            for t in range(9):
                s_vl[t % 9].inc(sync.dma_start(
                    vre(t % 9), v_sp[t].rearrange("p (h c) -> p h c", c=128)), 16)
            for qi in range(1, 8):
                t = qi + 8
                sync.wait_ge(s_pv.h, 16 * qi)
                s_vl[t % 9].inc(sync.dma_start(
                    vre(t % 9), v_sp[t].rearrange("p (h c) -> p h c", c=128)), 16)

        @blk.vector
        def _(vector):
            def rs_apply(h):
                bar("v", vector)
                mark("v", vector.tensor_mul(kT[:, h, :], kT[:, h, :], rsbk[:]), s_kap)
                bar("v", vector)
                mark("v", vector.tensor_mul(qT[:, h, :], qT[:, h, :], rsbq[:]), s_qap)

            vector.wait_ge(s_msk_ld.h, 64)
            mark("v", vector.tensor_scalar(
                vroll[:].rearrange("p s (h c) -> p (s h) c", c=129)[:, :, 128:129],
                vones_sb[:].rearrange("p (x c) -> p x c", c=1),
                1.0, None, ALU.mult), s_von)
            rs_apply(0)
            for n in range(NT):
                if n + 1 < 16:
                    rs_apply(n + 1)
                mpar = n % 2
                vector.wait_ge(s_pv.h, n + 1)
                if n >= 2:
                    vector.wait_ge(s_oev.h, n - 1)
                bar("v", vector)
                mark("v", vector.reciprocal(rrq[:, mpar:mpar + 1], o_ps[:, 128:129]), s_rr)

        @blk.tensor
        def _(tensor):
            tensor.wait_ge(s_von.h, 1)
            for n in range(NT + 2):
                if n < NT:
                    qi, h = qi_h(n)
                    par = n % 2
                    if n < 16:
                        tensor.wait_ge(s_kap.h, h + 1)
                        tensor.wait_ge(s_qap.h, h + 1)
                    if n >= 2:
                        tensor.wait_ge(s_exp.h, n - 1)
                    rhs_q = qT[:, h, 128 * qi:128 * (qi + 1)]
                    for b in range(9):
                        tb = qi + b
                        inst = tensor.matmul(S_ps[par][:, 128 * b:128 * (b + 1)],
                                             kT[:, h, 128 * tb:128 * (tb + 1)], rhs_q,
                                             start=True, stop=True)
                    mark("t", inst, s_S)
                if n >= 2:
                    j = n - 2
                    tensor.wait_ge(s_oev.h, j + 1)
                    if j >= 1:
                        tensor.wait_ge(s_ote.h, j)
                    mark("t", tensor.transpose(oTr_ps[:], o_sb[:, j % 2, :],
                                               s_identb[:]), s_otr)
                if 1 <= n <= NT:
                    m = n - 1
                    mq, mh = qi_h(m)
                    mpar = m % 2
                    if mh == 0:
                        if mq == 0:
                            for t in range(9):
                                tensor.wait_ge(s_vl[t % 9].h, 16)
                        else:
                            tensor.wait_ge(s_vl[(mq + 8) % 9].h, 32)
                    tensor.wait_ge(s_trim.h, m + 1)
                    if m >= 1:
                        tensor.wait_ge(s_oev.h, m)
                    for b in range(9):
                        sl = (mq + b) % 9
                        inst = tensor.matmul(o_ps[:, 0:129],
                                             PT_sb[mpar][:, 128 * b:128 * (b + 1)],
                                             vroll[:, sl, 129 * mh:129 * mh + 129],
                                             start=(b == 0), stop=(b == 8))
                    mark("t", inst, s_pv)

        @blk.scalar
        def _(scalar):
            def oev(j):
                ppar = j % 2
                scalar.wait_ge(s_rr.h, j + 1)
                if j >= 2:
                    scalar.wait_ge(s_otr.h, j - 1)
                mark("s", scalar.activation(o_sb[:, ppar, :], o_ps[:, 0:128], AF.Copy,
                                            scale=rrq[:, ppar:ppar + 1]), s_oev)

            def otev(j):
                pq, ph = qi_h(j)
                scalar.wait_ge(s_otr.h, j + 1)
                mark("s", scalar.copy(oT[:, ph, 128 * pq:128 * (pq + 1)], oTr_ps[:]),
                     s_ote)

            for m in range(NT):
                mpar = m % 2
                scalar.wait_ge(s_S.h, m + 1)
                if m >= 2:
                    scalar.wait_ge(s_pv.h, m - 1)
                mark("s", scalar.activation(PT_sb[mpar][:], S_ps[mpar][:], AF.Exp), s_exp)
                if m >= 1:
                    oev(m - 1)
                if m >= 2:
                    otev(m - 2)
            oev(NT - 1)
            otev(NT - 2)
            otev(NT - 1)

        @blk.gpsimd
        def _(g):
            if env.get("do_cc"):
                g.collective_compute("AllGather", ALU.bypass,
                                     replica_groups=[list(range(8))],
                                     ins=[env["wgu_cin"].ap().opt()],
                                     outs=[env["wgu_g"].ap().opt()]).then_inc(env["s_ccg"].h, 1)
                env["s_ccg"].n += 1
                g.collective_compute("AllGather", ALU.bypass,
                                     replica_groups=[list(range(8))],
                                     ins=[env["wdn_cin"].ap().opt()],
                                     outs=[env["wdn_g"].ap().opt()]).then_inc(env["s_ccd"].h, 1)
                env["s_ccd"].n += 1
            g.wait_ge(s_msk_ld.h, 64)
            for m in range(NT):
                mq, mh = qi_h(m)
                mpar = m % 2
                g.wait_ge(s_exp.h, m + 1)
                bar("g", g)
                mark("g", g.tensor_mul(PT_sb[mpar][:, 0:128], PT_sb[mpar][:, 0:128],
                                       fmask_sb[:]))
                for b in range(1, 8):
                    if mq + b < 8:
                        mark("g", g.tensor_mul(PT_sb[mpar][:, 128 * b:128 * (b + 1)],
                                               PT_sb[mpar][:, 128 * b:128 * (b + 1)],
                                               vm_sb[:]))
                mark("g", g.tensor_mul(PT_sb[mpar][:, 1024:1152],
                                       PT_sb[mpar][:, 1024:1152], ctri_sb[:]), s_trim)
            g.wait_ge(s_ote.h, NT)

    p2.close()


def build_p3(env):
    nc = env["nc"]
    sem, mark, bar = env["sem"], env["mark"], env["bar"]
    oT, wo, xh = env["oT"], env["wo"], env["xh"]
    x2_sp, s_x2sp, ln2pT = env["x2_sp"], env["s_x2sp"], env["ln2pT"]
    s_identb, s_identf = env["s_identb"], env["s_identf"]
    EPSLN = env["EPSLN"]

    p3 = ExitStack()
    psb = lambda n, s, d: p3.enter_context(nc.sbuf_tensor(n, s, d))
    pps = lambda n, s, d: p3.enter_context(nc.psum_tensor(n, s, d))
    wob = [psb(f"p3_wob{i}", [128, NH, 128], BF16) for i in range(2)]
    y1T = [psb(f"p3_y1T{i}", [128, 512], F32) for i in range(2)]
    x2b = psb("p3_x2b", [128, 8, H], F32)
    xtile = [psb(f"p3_xt{i}", [128, H], F32) for i in range(2)]
    ln2tm = [psb(f"p3_ln2tm{i}", [128, H], BF16) for i in range(2)]
    srow3 = psb("p3_srow", [128, 16], F32)
    xsq3 = psb("p3_xsq", [128, H], BF16)
    wo_acc = [pps(f"p3_acc{i}", [128, 512], F32) for i in range(2)]
    tr_ps = [pps(f"p3_tr{i}", [128, 512], F32) for i in range(2)]
    pt3 = [pps(f"p3_pt{i}", [128, 512], BF16) for i in range(2)]

    s_wob = [sem(f"p3_wob{i}") for i in range(2)]
    s_mm3 = sem("p3_mm")
    s_y1ev = sem("p3_y1ev")
    s_tr3 = sem("p3_tr")
    s_x2w = sem("p3_x2w")
    s_xt = [sem(f"p3_xt{i}") for i in range(2)]
    s_add = sem("p3_add")
    s_sq3 = sem("p3_sq3")
    s_var3 = sem("p3_var")
    s_sd3 = sem("p3_sd")
    s_app3 = sem("p3_app")
    s_ptr = sem("p3_ptr")
    s_pev = sem("p3_pev")

    NIT3 = 32

    with nc.Block() as blk:
        @blk.gpsimd
        def _(g):
            for ft in range(16):
                b = ft % 2
                if ft >= 2:
                    g.wait_ge(s_mm3.h, 2 * (ft - 1))
                s_wob[b].inc(g.dma_start(wob[b][:], wo[ft]), 16)
            for tt in range(8):
                g.wait_ge(s_add.h, tt + 1)
                s_x2sp.inc(g.dma_start(x2_sp[tt], x2b[:, tt, :]), 16)
            g.wait_ge(s_x2sp.h, s_x2sp.n)

        @blk.tensor
        def _(tensor):
            for it in range(NIT3 + 1):
                if it < NIT3:
                    ft, tg = it // 2, it % 2
                    pb = it % 2
                    if tg == 0:
                        tensor.wait_ge(s_wob[ft % 2].h, 16 * (ft // 2 + 1))
                    if it >= 2:
                        tensor.wait_ge(s_y1ev.h, it - 1)
                    for hc in range(NH):
                        inst = tensor.matmul(wo_acc[pb][:], wob[ft % 2][:, hc, :],
                                             oT[:, hc, 512 * tg:512 * (tg + 1)],
                                             start=(hc == 0), stop=(hc == NH - 1))
                    mark("t", inst, s_mm3)
                if it >= 1:
                    m = it - 1
                    mpb = m % 2
                    tensor.wait_ge(s_y1ev.h, m + 1)
                    if m >= 2:
                        tensor.wait_ge(s_x2w.h, m - 1)
                    for st in range(4):
                        inst = tensor.transpose(tr_ps[mpb][:, 128 * st:128 * (st + 1)],
                                                y1T[mpb][:, 128 * st:128 * (st + 1)],
                                                s_identf[:])
                    mark("t", inst, s_tr3)
            for tt in range(8):
                tensor.wait_ge(s_app3.h, tt + 1)
                for gg in range(4):
                    gi = 4 * tt + gg
                    if gi >= 2:
                        tensor.wait_ge(s_pev.h, gi - 1)
                    for jj in range(4):
                        c4 = 4 * gg + jj
                        P_, i_ = c4 // 4, c4 % 4
                        pv = ln2tm[tt % 2][:].rearrange("p (P c b i) -> p P i (c b)",
                                                        P=4, c=64, b=2, i=4)
                        inst = tensor.transpose(pt3[gi % 2][:, 128 * jj:128 * (jj + 1)],
                                                pv[:, P_, i_, :], s_identb[:])
                    mark("t", inst, s_ptr)

        @blk.vector
        def _(vector):
            for it in range(NIT3):
                pb = it % 2
                vector.wait_ge(s_mm3.h, it + 1)
                if it >= 2:
                    vector.wait_ge(s_tr3.h, it - 1)
                bar("v", vector)
                mark("v", vector.tensor_copy(y1T[pb][:], wo_acc[pb][:]), s_y1ev)
            for tt in range(8):
                b = tt % 2
                o = 8 * (tt % 2)
                vector.wait_ge(s_xt[b].h, 16 * (tt // 2 + 1))
                if tt == 0:
                    vector.wait_ge(s_x2w.h, NIT3)
                bar("v", vector)
                mark("v", vector.tensor_add(x2b[:, tt, :], x2b[:, tt, :], xtile[b][:]), s_add)
                bar("v", vector)
                mark("v", vector.reduce_sum(srow3[:, o + 0:o + 1], x2b[:, tt, :], axis=AX.X))
                vector.wait_ge(s_sq3.h, tt + 1)
                bar("v", vector)
                mark("v", vector.tensor_scalar(srow3[:, o + 2:o + 3], srow3[:, o + 0:o + 1],
                                               1.0 / H, None, ALU.mult))
                bar("v", vector)
                mark("v", vector.tensor_tensor(srow3[:, o + 3:o + 4], srow3[:, o + 2:o + 3],
                                               srow3[:, o + 2:o + 3], ALU.mult))
                bar("v", vector)
                mark("v", vector.scalar_tensor_tensor(
                    srow3[:, o + 4:o + 5], srow3[:, o + 1:o + 2], 1.0 / H,
                    srow3[:, o + 3:o + 4], ALU.mult, ALU.subtract), s_var3)
                vector.wait_ge(s_sd3.h, tt + 1)
                bar("v", vector)
                mark("v", vector.reciprocal(srow3[:, o + 6:o + 7], srow3[:, o + 5:o + 6]))
                bar("v", vector)
                mark("v", vector.scalar_tensor_tensor(
                    srow3[:, o + 7:o + 8], srow3[:, o + 2:o + 3], -1.0,
                    srow3[:, o + 6:o + 7], ALU.mult, ALU.mult))
                if tt >= 2:
                    vector.wait_ge(s_ptr.h, 4 * (tt - 1))
                bar("v", vector)
                mark("v", vector.tensor_scalar(ln2tm[b][:], x2b[:, tt, :],
                                               srow3[:, o + 6:o + 7], srow3[:, o + 7:o + 8],
                                               ALU.mult, ALU.add), s_app3)

        @blk.scalar
        def _(scalar):
            for it in range(NIT3):
                mit = it % 2
                ft, tg = it // 2, it % 2
                scalar.wait_ge(s_tr3.h, it + 1)
                mark("s", scalar.copy(
                    x2b[:, 4 * tg:4 * tg + 4, 128 * ft:128 * (ft + 1)],
                    tr_ps[mit][:].rearrange("p (a t) -> p a t", a=4)), s_x2w)
            for tt in range(8):
                b = tt % 2
                o = 8 * (tt % 2)
                scalar.wait_ge(s_add.h, tt + 1)
                if tt >= 2:
                    scalar.wait_ge(s_var3.h, tt - 1)
                mark("s", scalar.activation(xsq3[:], x2b[:, tt, :], AF.Square,
                                            accum_out=srow3[:, o + 1:o + 2]), s_sq3)
                scalar.wait_ge(s_var3.h, tt + 1)
                mark("s", scalar.activation(srow3[:, o + 5:o + 6], srow3[:, o + 4:o + 5],
                                            AF.Sqrt, bias=EPSLN), s_sd3)
                for gg in range(4):
                    gi = 4 * tt + gg
                    scalar.wait_ge(s_ptr.h, gi + 1)
                    mark("s", scalar.copy(
                        ln2pT[:, 4 * gg:4 * gg + 4, 128 * tt:128 * (tt + 1)],
                        pt3[gi % 2][:].rearrange("p (a t) -> p a t", a=4)), s_pev)

        @blk.sync
        def _(sync):
            for tt in range(8):
                b = tt % 2
                if tt >= 2:
                    sync.wait_ge(s_app3.h, tt - 1)
                    sync.wait_ge(s_sq3.h, tt - 1)
                s_xt[b].inc(sync.dma_start(xtile[b][:], xh[8 + tt]), 16)
            sync.wait_ge(s_pev.h, 32)

    p3.close()


def build_pdq(env):
    nc = env["nc"]
    sem, mark, bar = env["sem"], env["mark"], env["bar"]
    gup16m, gusm, dnp16m, dnsm = env["gup16m"], env["gusm"], env["dnp16m"], env["dnsm"]
    bcast4 = env["bcast4"]

    pdq = env["pdq_ctx"]
    stg = ExitStack()
    psb = lambda n, s, d: pdq.enter_context(nc.sbuf_tensor(n, s, d))
    ssb = lambda n, s, d: stg.enter_context(nc.sbuf_tensor(n, s, d, side="right"))
    pps = lambda n, s, d: stg.enter_context(nc.psum_tensor(n, s, d))
    B = env["pdq_bufs"] = {}
    B["pkq"] = psb("pdq_pkq", [128, 12, 512], I16)
    B["dpkq"] = psb("pdq_dpkq", [128, 11, 256], I16)
    gsc = ssb("pdq_gsc", [4, 4, 1536], BF16)
    dsc = ssb("pdq_dsc", [4, 11, 256], BF16)
    bcsb = ssb("pdq_bcsb", [4, 128], BF16)
    B["sclq"] = psb("pdq_sclq", [128, 12, 512], BF16)
    B["scld"] = psb("pdq_scld", [128, 11, 256], BF16)
    B["wsl"] = psb("pdq_wsl", [128, 2, 8, 512], BF16)
    B["dqs"] = psb("pdq_dqs", [128, 5, 512], I16)
    bcps = pps("pdq_bc", [128, 512], F32)

    s_pkt = env["s_pkt"] = sem("pdq_pkt", persistent=True)
    s_sc = sem("pdq_sc", persistent=True)
    s_bcm = sem("pdq_bcm", persistent=True)
    s_bce = env["s_bce"] = sem("pdq_bce", persistent=True)

    gus_v = gusm.rearrange("(P j) f -> j P f", j=4)
    dns_v = dnsm.rearrange("(t j) f -> j t f", j=4)

    with nc.Block() as blk:
        @blk.sync
        def _(sync):
            for b in range(3):
                for P_ in range(4):
                    s_pkt.inc(sync.dma_start(
                        B["pkq"][:, 4 * b + P_, :],
                        gup16m[512 * b:512 * (b + 1), 128 * P_:128 * (P_ + 1)],
                        transpose=True), 16)
            for tp in range(11):
                s_pkt.inc(sync.dma_start(
                    B["dpkq"][:, tp, :], dnp16m[:, 128 * tp:128 * (tp + 1)],
                    transpose=True), 16)

        @blk.gpsimd
        def _(g):
            s_sc.inc(g.dma_start(bcsb[:], bcast4), 16)
            s_sc.inc(g.dma_start(gsc[:], gus_v), 16)
            s_sc.inc(g.dma_start(dsc[:], dns_v), 16)

        @blk.tensor
        def _(tensor):
            tensor.wait_ge(s_sc.h, 48)
            for s in range(12):
                if s >= 1:
                    tensor.wait_ge(s_bce.h, s)
                b, P_ = s // 4, s % 4
                mark("t", tensor.matmul(bcps[:], bcsb[:],
                                        gsc[0:4, P_, 512 * b:512 * (b + 1)],
                                        start=True, stop=True), s_bcm)
            for tp in range(11):
                s = 12 + tp
                tensor.wait_ge(s_bce.h, s)
                mark("t", tensor.matmul(bcps[:, 0:256], bcsb[:],
                                        dsc[0:4, tp, :],
                                        start=True, stop=True), s_bcm)

        @blk.scalar
        def _(scalar):
            for s in range(12):
                scalar.wait_ge(s_bcm.h, s + 1)
                mark("s", scalar.copy(B["sclq"][:, s, :], bcps[:]), s_bce)
            for tp in range(11):
                scalar.wait_ge(s_bcm.h, 12 + tp + 1)
                mark("s", scalar.copy(B["scld"][:, tp, :], bcps[:, 0:256]), s_bce)

    stg.close()


def _dequant8(vector, mark, bar, v, scl, dst, scr, i):
    ALUm = ALU
    W = v.shape[-1]
    m = scr[:, 0, 0:W]
    a = scr[:, 1, 0:W]
    b2 = scr[:, 2, 0:W]
    c = scr[:, 3, 0:W]
    s01 = scr[:, 4, 0:W]
    bar("v", vector)
    if i == 0:
        mark("v", vector.tensor_scalar(m, v, 7, None, ALUm.bitwise_and))
    else:
        mark("v", vector.tensor_scalar(m, v, 4 * i, 7, ALUm.logical_shift_right,
                                       ALUm.bitwise_and))
    mark("v", vector.tensor_scalar(s01, v, 4 * i + 3, 1, ALUm.logical_shift_right,
                                   ALUm.bitwise_and))
    bar("v", vector)
    mark("v", vector.scalar_tensor_tensor(a, m, 4, m, ALUm.max, ALUm.add))
    mark("v", vector.tensor_scalar(b2, m, 2, 12, ALUm.mult, ALUm.max))
    bar("v", vector)
    mark("v", vector.scalar_tensor_tensor(c, b2, -16, a, ALUm.add, ALUm.add))
    bar("v", vector)
    mark("v", vector.tensor_tensor(a, c, s01, ALUm.mult))
    bar("v", vector)
    mark("v", vector.scalar_tensor_tensor(m, a, -2, c, ALUm.mult, ALUm.add))
    bar("v", vector)
    return vector.tensor_tensor(dst, m, scl, ALUm.mult)


def build_p4(env):
    nc = env["nc"]
    sem, mark, bar = env["sem"], env["mark"], env["bar"]
    ln2pT = env["ln2pT"]
    wgu_g = env["wgu_g"]
    h_sp, s_hsp = env["h_sp"], env["s_hsp"]
    s_ccg = env["s_ccg"]

    p4 = ExitStack()
    psb = lambda n, s, d: p4.enter_context(nc.sbuf_tensor(n, s, d))
    pps = lambda n, s, d: p4.enter_context(nc.psum_tensor(n, s, d))
    wgu = [psb(f"p4_wgu{i}", [128, 2, 16, 512], BF16) for i in range(2)]
    sgate = psb("p4_sgate", [128, 2, 512], BF16)
    hstage = [psb(f"p4_hst{i}", [128, 4, TOWN], BF16) for i in range(2)]
    gps = [pps(f"p4_g{i}", [128, 512], F32) for i in range(2)]
    ups = [pps(f"p4_u{i}", [128, 512], F32) for i in range(2)]

    s_wg = [sem(f"p4_wg{i}") for i in range(2)]
    s_gu = sem("p4_gu")
    s_sil = sem("p4_sil")
    s_hh = sem("p4_hh")

    def gpos(u):
        return 3 * (u % 8) + u // 8


    def _p4_mms(tensor, tp):
        sl = tp % 2
        tensor.wait_ge(s_wg[sl].h, 32 * (tp // 2 + 1))
        for ip in range(4):
            for tg in range(2):
                it = 8 * tp + 2 * ip + tg
                pb = it % 2
                if it >= 2:
                    tensor.wait_ge(s_sil.h, it - 1)
                    tensor.wait_ge(s_hh.h, it - 1)
                for kc in range(16):
                    wv = wgu[sl][:, 0, kc, :].rearrange("p (c b i) -> p i (c b)",
                                                        c=64, b=2, i=4)[:, ip, :]
                    inst = tensor.matmul(gps[pb][:], wv,
                                         ln2pT[:, kc, 512 * tg:512 * (tg + 1)],
                                         start=(kc == 0), stop=(kc == 15))
                mark("t", inst, s_gu)
                for kc in range(16):
                    wv = wgu[sl][:, 1, kc, :].rearrange("p (c b i) -> p i (c b)",
                                                        c=64, b=2, i=4)[:, ip, :]
                    inst = tensor.matmul(ups[pb][:], wv,
                                         ln2pT[:, kc, 512 * tg:512 * (tg + 1)],
                                         start=(kc == 0), stop=(kc == 15))
                mark("t", inst, s_gu)

    def _p4_hcomb(vector, tp):
        hs = tp % 2
        for ii in range(8):
            it = 8 * tp + ii
            ip, tg = ii // 2, ii % 2
            pb = it % 2
            vector.wait_ge(s_sil.h, it + 1)
            vector.wait_ge(s_gu.h, 2 * it + 2)
            if tp >= 2 and ii == 0:
                vector.wait_ge(s_hsp[hs].h, 16 * (tp // 2))
            bar("v", vector)
            mark("v", vector.tensor_mul(hstage[hs][:, ip, 512 * tg:512 * (tg + 1)],
                                        sgate[:, it % 2, :], ups[pb][:]), s_hh)

    with nc.Block() as blk:
        @blk.sync
        def _(sync):
            sync.wait_ge(s_ccg.h, 1)
            for tp in range(11):
                sl = tp % 2
                if tp >= 2:
                    sync.wait_ge(s_gu.h, 16 * (tp - 1))
                for bb in range(2):
                    u = tp + 11 * bb
                    r0 = gpos(u) * 2048
                    s_wg[sl].inc(sync.dma_start(
                        wgu[sl][:, bb, :, :],
                        wgu_g.ap()[r0:r0 + 2048, :].rearrange("(k p) f -> p k f", p=128)),
                        16)

        @blk.gpsimd
        def _(g):
            for tp in range(11):
                if tp >= 2:
                    g.wait_ge(s_hh.h, 8 * (tp - 1))
                    hs = (tp - 2) % 2
                    s_hsp[hs].inc(g.dma_start(
                        h_sp[4 * (tp - 2):4 * (tp - 2) + 4].rearrange("k p t -> p k t"),
                        hstage[hs][:]), 16)
            for tp in (9, 10):
                g.wait_ge(s_hh.h, 8 * (tp + 1))
                hs = tp % 2
                s_hsp[hs].inc(g.dma_start(
                    h_sp[4 * tp:4 * tp + 4].rearrange("k p t -> p k t"),
                    hstage[hs][:]), 16)
            g.wait_ge(s_hsp[0].h, s_hsp[0].n)
            g.wait_ge(s_hsp[1].h, s_hsp[1].n)

        @blk.tensor
        def _(tensor):
            for tp in range(11):
                _p4_mms(tensor, tp)

        @blk.scalar
        def _(scalar):
            for tp in range(11):
                for ii in range(8):
                    it = 8 * tp + ii
                    pb = it % 2
                    scalar.wait_ge(s_gu.h, 2 * it + 1)
                    if it >= 2:
                        scalar.wait_ge(s_hh.h, it - 1)
                    mark("s", scalar.activation(sgate[:, it % 2, :], gps[pb][:],
                                                AF.Silu), s_sil)

        @blk.vector
        def _(vector):
            for tp in range(11):
                _p4_hcomb(vector, tp)

    p4.close()


def build_p5(env):
    nc = env["nc"]
    sem, mark, bar = env["sem"], env["mark"], env["bar"]
    wdn_g = env["wdn_g"]
    h_sp = env["h_sp"]
    s_hsp, s_x2sp = env["s_hsp"], env["s_x2sp"]
    x2_sp, out, s_outst = env["x2_sp"], env["out"], env["s_outst"]
    s_identf, EPSRMS = env["s_identf"], env["EPSRMS"]
    s_ccd = env["s_ccd"]

    p5b = ExitStack()
    psb = lambda n, s, d: p5b.enter_context(nc.sbuf_tensor(n, s, d))
    pps = lambda n, s, d: p5b.enter_context(nc.psum_tensor(n, s, d))
    hT = psb("p5_hT", [128, NKC_DN, TOWN], BF16)
    wdnc = [psb(f"p5_wdnc{i}", [128, NKC_DN, 128], BF16) for i in range(2)]
    yst = [psb(f"p5_yst{i}", [128, 512], F32) for i in range(2)]
    ybuf = psb("p5_ybuf", [128, 4, H], F32)
    x2t = [psb(f"p5_x2t{i}", [128, H], F32) for i in range(2)]
    outb = [psb(f"p5_outb{i}", [128, H], F32) for i in range(2)]
    ysq = psb("p5_ysq", [128, H], BF16)
    srow5 = psb("p5_srow", [128, 8], F32)
    dacc = [pps(f"p5_dacc{i}", [128, 512], F32) for i in range(2)]
    typs = [pps(f"p5_ty{i}", [128, 512], F32) for i in range(2)]

    s_hld = sem("p5_hld")
    s_wdc = [sem(f"p5_wdc{i}") for i in range(2)]
    s_dmm = sem("p5_dmm")
    s_yev = sem("p5_yev")
    s_ty = sem("p5_ty")
    s_yw = sem("p5_yw")
    s_x2t = [sem(f"p5_x2t{i}") for i in range(2)]
    s_sq5 = sem("p5_sq5")
    s_sd5 = sem("p5_sd5")
    s_fin = sem("p5_fin")

    with nc.Block() as blk:
        @blk.sync
        def _(sync):
            sync.wait_ge(s_hsp[0].h, 96)
            sync.wait_ge(s_hsp[1].h, 80)
            for pc in range(4):
                s_hld.inc(sync.dma_start(
                    hT[:, 11 * pc:11 * (pc + 1), :],
                    h_sp[11 * pc:11 * (pc + 1)].rearrange("k p t -> p k t")), 16)
            sync.wait_ge(s_ccd.h, 1)
            for it in range(32):
                ho = it % 16
                sl = it % 2
                rb, co = ho // 2, (ho % 2) * 128
                if it >= 2:
                    sync.wait_ge(s_dmm.h, it - 1)
                for pc in range(4):
                    s_wdc[sl].inc(sync.dma_start(
                        wdnc[sl][:, 11 * pc:11 * (pc + 1), :],
                        wdn_g.ap()[rb * 5632 + 1408 * pc:rb * 5632 + 1408 * (pc + 1),
                                   co:co + 128]
                        .rearrange("(k p) f -> p k f", p=128)), 16)

        @blk.tensor
        def _(tensor):
            tensor.wait_ge(s_hld.h, 64)
            for it in range(33):
                if it < 32:
                    tg, ho = it // 16, it % 16
                    pb = it % 2
                    sl = it % 2
                    tensor.wait_ge(s_wdc[sl].h, 64 * (it // 2 + 1))
                    if it >= 2:
                        tensor.wait_ge(s_yev.h, it - 1)
                    for kc in range(NKC_DN):
                        inst = tensor.matmul(dacc[pb][:], wdnc[sl][:, kc, :],
                                             hT[:, kc, 512 * tg:512 * (tg + 1)],
                                             start=(kc == 0), stop=(kc == NKC_DN - 1))
                    mark("t", inst, s_dmm)
                if it >= 1:
                    m = it - 1
                    mpb = m % 2
                    tensor.wait_ge(s_yev.h, m + 1)
                    if m >= 2:
                        tensor.wait_ge(s_yw.h, m - 1)
                    for st in range(4):
                        inst = tensor.transpose(typs[mpb][:, 128 * st:128 * (st + 1)],
                                                yst[mpb][:, 128 * st:128 * (st + 1)],
                                                s_identf[:])
                    mark("t", inst, s_ty)

        @blk.scalar
        def _(scalar):
            for it in range(32):
                tg, ho = it // 16, it % 16
                pb = it % 2
                if it == 16:
                    for tt8 in range(4):
                        scalar.wait_ge(s_sq5.h, tt8 + 1)
                        mark("s", scalar.activation(srow5[:, 5:6], srow5[:, 4:5],
                                                    AF.Sqrt, bias=EPSRMS,
                                                    scale=1.0 / H), s_sd5)
                    scalar.wait_ge(s_fin.h, 4)
                scalar.wait_ge(s_ty.h, it + 1)
                mark("s", scalar.copy(
                    ybuf[:, 0:4, 128 * ho:128 * (ho + 1)],
                    typs[pb][:].rearrange("p (a t) -> p a t", a=4)), s_yw)
            for tt8 in range(4, 8):
                scalar.wait_ge(s_sq5.h, tt8 + 1)
                mark("s", scalar.activation(srow5[:, 5:6], srow5[:, 4:5],
                                            AF.Sqrt, bias=EPSRMS, scale=1.0 / H), s_sd5)

        @blk.vector
        def _(vector):
            def final(tt8):
                b = tt8 % 2
                tt = tt8 % 4
                vector.wait_ge(s_x2t[b].h, 16 * (tt8 // 2 + 1))
                if tt8 >= 1:
                    vector.wait_ge(s_sd5.h, tt8)
                bar("v", vector)
                mark("v", vector.scalar_tensor_tensor(
                    ysq[:], ybuf[:, tt, :], 1.0, ybuf[:, tt, :],
                    ALU.mult, ALU.mult, accum_out=srow5[:, 4:5]), s_sq5)
                vector.wait_ge(s_sd5.h, tt8 + 1)
                bar("v", vector)
                mark("v", vector.reciprocal(srow5[:, 6:7], srow5[:, 5:6]))
                if tt8 >= 2:
                    vector.wait_ge(s_outst[tt8 % 2].h, 16 * (tt8 // 2))
                bar("v", vector)
                mark("v", vector.scalar_tensor_tensor(
                    outb[b][:], ybuf[:, tt, :], srow5[:, 6:7], x2t[b][:],
                    ALU.mult, ALU.add), s_fin)

            for it in range(32):
                pb = it % 2
                vector.wait_ge(s_dmm.h, it + 1)
                if it >= 2:
                    vector.wait_ge(s_ty.h, it - 1)
                bar("v", vector)
                mark("v", vector.tensor_copy(yst[pb][:], dacc[pb][:]), s_yev)
                if it == 16:
                    vector.wait_ge(s_yw.h, 16)
                    for tt8 in range(4):
                        final(tt8)
            vector.wait_ge(s_yw.h, 32)
            for tt8 in range(4, 8):
                final(tt8)

        @blk.gpsimd
        def _(g):
            g.wait_ge(s_x2sp.h, 16 * 8)
            s_x2t[0].inc(g.dma_start(x2t[0][:], x2_sp[0]), 16)
            s_x2t[1].inc(g.dma_start(x2t[1][:], x2_sp[1]), 16)
            for tt8 in range(8):
                g.wait_ge(s_fin.h, tt8 + 1)
                s_outst[tt8 % 2].inc(g.dma_start(out[tt8], outb[tt8 % 2][:]), 16)
                if tt8 + 2 < 8:
                    s_x2t[tt8 % 2].inc(g.dma_start(x2t[tt8 % 2][:], x2_sp[tt8 + 2]), 16)
            g.wait_ge(s_outst[0].h, 64)
            g.wait_ge(s_outst[1].h, 64)

    p5b.close()


def _host_prep(L0_attn_hidden_states, L0_attn_Wqkv, L0_attn_q_norm_w, L0_attn_k_norm_w,
               L0_attn_Wo, L0_ffn_gate_up_packed, L0_ffn_gate_up_scales,
               L0_ffn_down_packed, L0_ffn_down_scales):
    x = np.ascontiguousarray(L0_attn_hidden_states)
    wqkv = np.ascontiguousarray(
        L0_attn_Wqkv.astype(np.float32).reshape(NH, 128, 48, 128).transpose(2, 1, 0, 3))
    wo_ = np.ascontiguousarray(
        L0_attn_Wo.astype(np.float32).reshape(NH, 128, 16, 128).transpose(2, 1, 0, 3))
    qw_ = np.ascontiguousarray(L0_attn_q_norm_w, dtype=np.float32).reshape(NH, 128).T.copy()
    kw_ = np.ascontiguousarray(L0_attn_k_norm_w, dtype=np.float32).reshape(NH, 128).T.copy()
    gup16 = np.ascontiguousarray(L0_ffn_gate_up_packed).view(np.int16)
    gus_ = np.ascontiguousarray(L0_ffn_gate_up_scales)
    dnp16 = np.ascontiguousarray(L0_ffn_down_packed).view(np.int16)
    dns_ = np.ascontiguousarray(L0_ffn_down_scales)

    gup16m_l, gusm_l, dnp16m_l, dnsm_l = [], [], [], []
    for r in range(8):
        us = [r, r + 8, (r + 16) if r + 16 < 22 else r]
        gup16m_l.append(np.concatenate([gup16[512 * u:512 * (u + 1), :] for u in us], axis=0))
        gusm_l.append(np.ascontiguousarray(
            np.concatenate([gus_[:, 512 * u:512 * (u + 1)] for u in us], axis=1)))
        dnp16m_l.append(np.ascontiguousarray(dnp16[256 * r:256 * (r + 1), :]))
        dnsm_l.append(np.ascontiguousarray(dns_[:, 256 * r:256 * (r + 1)]))

    identb = np.eye(128, dtype=np.float32).astype(ml_dtypes.bfloat16)
    identf = np.eye(128, dtype=np.float32)
    onesrow = np.ones((1, 128), dtype=np.float32)
    onescol = np.ones((128, 1), dtype=np.float32).astype(ml_dtypes.bfloat16)
    bcast4_ = np.zeros((4, 128), dtype=np.float32)
    for g in range(4):
        bcast4_[g, 32 * g:32 * (g + 1)] = 0.5
    bcast4_ = bcast4_.astype(ml_dtypes.bfloat16)
    constv = np.zeros((128, 4), dtype=np.float32)
    constv[:, 0] = EPS_LN
    constv[:, 1] = EPS_RMS

    kl = np.arange(128)[:, None]
    ql = np.arange(128)[None, :]
    ctri_ = np.where(kl <= ql, 1.0, 0.0).astype(ml_dtypes.bfloat16)
    ftri_ = np.where(kl >= ql + 1, 1.0, 0.0).astype(ml_dtypes.bfloat16)
    zeros_ = np.zeros((128, 128), dtype=ml_dtypes.bfloat16)
    ones_ = np.ones((128, 128), dtype=ml_dtypes.bfloat16)
    vones_ = np.ones((128, 9 * NH), dtype=ml_dtypes.bfloat16)

    inv = (1.0 / (10000.0 ** (np.arange(0, HD, 2, dtype=np.float64) / HD)))

    in_maps = []
    for c in range(8):
        b, j = c // 4, c % 4
        own = x[b, 1024 * j:1024 * (j + 1)]
        halo = x[b, 1024 * (j - 1):1024 * j] if j > 0 else own
        xh_ = np.concatenate([halo, own], axis=0).reshape(16, 128, H)
        pos0 = 1024 * j - 1024
        pos = (pos0 + np.arange(TALL, dtype=np.float64))
        ang = pos[None, :] * inv[:, None]
        cosd_ = np.tile(np.cos(ang), (2, 1)).astype(ml_dtypes.bfloat16)
        sind_ = np.tile(np.sin(ang), (2, 1)).astype(ml_dtypes.bfloat16)
        if j == 0:
            fmask_ = zeros_
            vm01_ = zeros_
        else:
            fmask_ = ftri_
            vm01_ = ones_
        in_maps.append(dict(
            xh=xh_, wqkv=wqkv, wo=wo_, qw=qw_, kw=kw_,
            gup16m=gup16m_l[c], gusm=gusm_l[c], dnp16m=dnp16m_l[c], dnsm=dnsm_l[c],
            cosd=cosd_, sind=sind_, identb=identb,
            identf=identf, fmask=fmask_, ctri=ctri_, vm01=vm01_, vones=vones_,
            onesrow=onesrow, onescol=onescol, bcast4=bcast4_, constv=constv))
    return in_maps


def kernel(**inputs):
    in_maps = _host_prep(**inputs)
    nc, _ = build()
    res = run_bass_kernel_spmd(nc, in_maps, core_ids=list(range(8)))
    out = np.zeros((2, 4096, H), dtype=np.float32)
    for c in range(8):
        b, j = c // 4, c % 4
        out[b, 1024 * j:1024 * (j + 1)] = np.asarray(res.results[c]["out"]).reshape(1024, H)
    return out


# version 13
# speedup vs baseline: 1.0317x; 1.0317x over previous
import numpy as np
import ml_dtypes
import concourse.bass as bass
import concourse.mybir as mybir
from concourse.bass_utils import run_bass_kernel_spmd
from contextlib import ExitStack

F32 = mybir.dt.float32
BF16 = mybir.dt.bfloat16
I16 = mybir.dt.int16
F16 = mybir.dt.float16
AF = mybir.ActivationFunctionType
ALU = mybir.AluOpType
AX = mybir.AxisListType

H = 2048
NH = 16
HD = 128
TOWN = 1024
TALL = 2048
IM = 5632
EPS_LN = 1e-5
EPS_RMS = 1e-6
SCALE_S = float(1.0 / np.sqrt(HD))
NEG = -1e30
NKC_DN = 44

BUILD_PHASES = 99


class Sem:
    def __init__(self, nc, name):
        self.h = nc.alloc_semaphore(name=name)
        self.n = 0

    def inc(self, inst, k=1):
        inst.then_inc(self.h, k)
        self.n += k
        return self.n


def build(debug=()):
    nc = bass.Bass("TRN2", target_bir_lowering=False)

    xh = nc.dram_tensor("xh", [16, 128, H], F32, kind="ExternalInput").ap()
    wqkv = nc.dram_tensor("wqkv", [48, 128, NH, 128], F32, kind="ExternalInput").ap()
    wo = nc.dram_tensor("wo", [16, 128, NH, 128], F32, kind="ExternalInput").ap()
    qw = nc.dram_tensor("qw", [128, NH], F32, kind="ExternalInput").ap()
    kw = nc.dram_tensor("kw", [128, NH], F32, kind="ExternalInput").ap()
    gup16m = nc.dram_tensor("gup16m", [1536, 512], I16, kind="ExternalInput").ap()
    gusm = nc.dram_tensor("gusm", [16, 1536], F16, kind="ExternalInput").ap()
    dnp16m = nc.dram_tensor("dnp16m", [256, 1408], I16, kind="ExternalInput").ap()
    dnsm = nc.dram_tensor("dnsm", [44, 256], F16, kind="ExternalInput").ap()
    cosd = nc.dram_tensor("cosd", [128, TALL], BF16, kind="ExternalInput").ap()
    sind = nc.dram_tensor("sind", [128, TALL], BF16, kind="ExternalInput").ap()
    identb = nc.dram_tensor("identb", [128, 128], BF16, kind="ExternalInput").ap()
    identf = nc.dram_tensor("identf", [128, 128], F32, kind="ExternalInput").ap()
    fmask = nc.dram_tensor("fmask", [128, 128], BF16, kind="ExternalInput").ap()
    ctri = nc.dram_tensor("ctri", [128, 128], BF16, kind="ExternalInput").ap()
    vm01 = nc.dram_tensor("vm01", [128, 128], BF16, kind="ExternalInput").ap()
    vones = nc.dram_tensor("vones", [128, 9 * NH], BF16, kind="ExternalInput").ap()
    onesrow = nc.dram_tensor("onesrow", [1, 128], F32, kind="ExternalInput").ap()
    onescol = nc.dram_tensor("onescol", [128, 1], BF16, kind="ExternalInput").ap()
    bcast4 = nc.dram_tensor("bcast4", [4, 128], BF16, kind="ExternalInput").ap()
    constv = nc.dram_tensor("constv", [128, 4], F32, kind="ExternalInput").ap()

    out = nc.dram_tensor("out", [8, 128, H], F32, kind="ExternalOutput").ap()

    v_sp = nc.dram_tensor("v_sp", [16, 128, H], BF16, kind="Internal").ap()
    wgu_cin = nc.dram_tensor("wgu_cin", [3 * 16 * 128, 512], BF16, kind="Internal")
    wgu_g = nc.dram_tensor("wgu_g", [8 * 3 * 16 * 128, 512], BF16, kind="Internal",
                           addr_space="Shared")
    wdn_cin = nc.dram_tensor("wdn_cin", [44 * 128, 256], BF16, kind="Internal")
    wdn_g = nc.dram_tensor("wdn_g", [8 * 44 * 128, 256], BF16, kind="Internal",
                           addr_space="Shared")
    x2_sp = nc.dram_tensor("x2_sp", [8, 128, H], F32, kind="Internal").ap()
    h_sp = nc.dram_tensor("h_sp", [NKC_DN, 128, TOWN], BF16, kind="Internal").ap()

    dbg_outs = {}

    root = ExitStack()
    with root:
        def sb(name, shape, dtype, side="left"):
            return root.enter_context(nc.sbuf_tensor(name, shape, dtype, side=side))

        _phase_sems = []

        def sem(name, persistent=False):
            s = Sem(nc, name)
            if not persistent:
                _phase_sems.append(s)
            return s

        def free_phase_sems():
            if not _phase_sems:
                return
            with nc.Block() as blk:
                @blk.sync
                def _(sync):
                    for s in _phase_sems:
                        sync.sem_clear(s.h)
            for s in _phase_sems:
                nc.release_semaphore(s.h)
            _phase_sems.clear()
            last.clear()

        s_identb = sb("c_identb", [128, 128], BF16)
        s_identf = sb("c_identf", [128, 128], F32)
        s_cv = sb("c_cv", [128, 4], F32)
        s_onesrow = sb("c_onesrow", [1, 128], F32)
        s_onescol = sb("c_onescol", [128, 1], BF16)
        kwsb = sb("c_kwsb", [128, NH], F32)
        qwsb = sb("c_qwsb", [128, NH], F32)
        cossb = sb("c_cossb", [128, TALL], BF16)
        sinsb = sb("c_sinsb", [128, TALL], BF16)

        s_cons = sem("s_cons", persistent=True)
        s_dbg = sem("s_dbg", persistent=True)
        s_vsp = [sem("s_vsp0", persistent=True), sem("s_vsp1", persistent=True)]
        s_x2sp = sem("s_x2sp", persistent=True)
        s_hsp = [sem("s_hsp0", persistent=True), sem("s_hsp1", persistent=True)]
        s_outst = [sem("s_outst0", persistent=True), sem("s_outst1", persistent=True)]

        selfs = {e: sem(f"self_{e}", persistent=True) for e in ("v", "s", "t", "g")}
        last = {}

        def mark(e, inst, s=None, k=1):
            s = s if s is not None else selfs[e]
            s.inc(inst, k)
            last[e] = (s.h, s.n)
            return inst

        def bar(e, eng):
            if e in last:
                eng.wait_ge(*last[e])

        def tap(name, src_ap, shape, dtype):
            if name not in debug:
                return
            t = nc.dram_tensor("dbg_" + name, shape, dtype, kind="ExternalOutput").ap()
            dbg_outs[name] = t
            with nc.Block() as blk:
                @blk.gpsimd
                def _(gp):
                    s_dbg.inc(gp.dma_start(t, src_ap), 16)
                    gp.wait_ge(s_dbg.h, s_dbg.n)

        with nc.Block() as blk:
            @blk.sync
            def _(sync):
                for t, d in [(s_identb, identb), (s_identf, identf), (s_cv, constv),
                             (s_onesrow, onesrow), (s_onescol, onescol),
                             (kwsb, kw), (qwsb, qw), (cossb, cosd), (sinsb, sind)]:
                    s_cons.inc(sync.dma_start(t[:], d), 16)
                sync.wait_ge(s_cons.h, s_cons.n)

        EPSLN = s_cv[:, 0:1]
        EPSRMS = s_cv[:, 1:2]

        s_ccg = sem("s_ccg", persistent=True)
        s_ccd = sem("s_ccd", persistent=True)
        ln1T_ctx = ExitStack()
        ln1T = ln1T_ctx.enter_context(nc.sbuf_tensor("ln1T", [128, NH, TALL], BF16))
        pdq_ctx = ExitStack()
        pdq_env = dict(nc=nc, sem=sem, mark=mark, bar=bar, pdq_ctx=pdq_ctx,
                       gup16m=gup16m, gusm=gusm, dnp16m=dnp16m, dnsm=dnsm,
                       bcast4=bcast4, wgu_cin=wgu_cin, wgu_g=wgu_g,
                       wdn_cin=wdn_cin, wdn_g=wdn_g, s_ccg=s_ccg, s_ccd=s_ccd)
        if BUILD_PHASES >= 5:
            build_pdq(pdq_env)
        PB = pdq_env.get("pdq_bufs") if BUILD_PHASES >= 5 else None
        s_dqg = sem("s_dqg", persistent=True)
        s_dqd = sem("s_dqd", persistent=True)
        s_spq = [sem("s_spq0", persistent=True), sem("s_spq1", persistent=True)]

        def gate_plane(vector, j):
            hb = j // 8
            Pl = (j % 8) // 4
            i_ = j % 4
            b, Ph = hb // 2, (hb % 2) * 2
            P_ = Ph + Pl
            if j % 8 == 0 and hb >= 2:
                vector.wait_ge(s_spq[hb % 2].h, 16 * (hb // 2))
            mark("v", _dequant8(vector, mark, bar,
                                PB["pkq"][:, 4 * b + P_, :],
                                PB["sclq"][:, 4 * b + P_, :],
                                PB["wsl"][:, hb % 2, 4 * Pl + i_, :],
                                PB["dqs"], i_), s_dqg)

        with ExitStack() as pctx:
            psb = lambda n, s, d: pctx.enter_context(nc.sbuf_tensor(n, s, d))
            pps = lambda n, s, d: pctx.enter_context(nc.psum_tensor(n, s, d))
            x_in = [pctx.enter_context(nc.sbuf_tensor(f"p0_x{i}", [128, H], F32,
                                                       side="right")) for i in range(4)]
            xsq = pctx.enter_context(nc.sbuf_tensor("p0_xsq", [128, H], BF16, side="right"))
            ln1tm = [pctx.enter_context(nc.sbuf_tensor(f"p0_ln1tm{i}", [128, H], BF16,
                                                       side="right")) for i in range(2)]
            srow = pctx.enter_context(nc.sbuf_tensor("p0_srow", [128, 16], F32, side="right"))
            tr = [pps(f"p0_tr{i}", [128, 512], BF16) for i in range(2)]

            s_xld = [sem(f"p0_xld{i}") for i in range(4)]
            s_sq = sem("p0_sq")
            s_var = sem("p0_var")
            s_sd = sem("p0_sd")
            s_app = sem("p0_app")
            s_tr = sem("p0_tr")
            s_ev = sem("p0_ev")

            with nc.Block() as blk:
                @blk.sync
                def _(sync):
                    for tt in range(16):
                        b = tt % 4
                        if tt >= 4:
                            sync.wait_ge(s_app.h, tt - 3)
                            sync.wait_ge(s_sq.h, tt - 3)
                        s_xld[b].inc(sync.dma_start(x_in[b][:], xh[tt]), 16)

                @blk.scalar
                def _(scalar):
                    for tt in range(16):
                        b = tt % 4
                        o = 8 * (tt % 2)
                        scalar.wait_ge(s_xld[b].h, 16 * (tt // 4 + 1))
                        if tt >= 2:
                            scalar.wait_ge(s_var.h, tt - 1)
                        mark("s", scalar.activation(xsq[:], x_in[b][:], AF.Square,
                                                    accum_out=srow[:, o + 1:o + 2]), s_sq)
                        scalar.wait_ge(s_var.h, tt + 1)
                        mark("s", scalar.activation(srow[:, o + 5:o + 6], srow[:, o + 4:o + 5],
                                                    AF.Sqrt, bias=EPSLN), s_sd)
                        for g in range(4):
                            scalar.wait_ge(s_tr.h, 4 * tt + g + 1)
                            mark("s", scalar.copy(
                                ln1T[:, 4 * g:4 * g + 4, 128 * tt:128 * (tt + 1)],
                                tr[g % 2][:].rearrange("p (a t) -> p a t", a=4)), s_ev)

                @blk.vector
                def _(vector):
                    ndq = 0
                    if PB is not None:
                        vector.wait_ge(pdq_env["s_pkt"].h, 16 * 23)
                        vector.wait_ge(pdq_env["s_bce"].h, 23)
                    for tt in range(16):
                        b = tt % 4
                        o = 8 * (tt % 2)
                        vector.wait_ge(s_xld[b].h, 16 * (tt // 4 + 1))
                        bar("v", vector)
                        mark("v", vector.reduce_sum(srow[:, o + 0:o + 1], x_in[b][:], axis=AX.X))
                        vector.wait_ge(s_sq.h, tt + 1)
                        bar("v", vector)
                        mark("v", vector.tensor_scalar(srow[:, o + 2:o + 3], srow[:, o + 0:o + 1],
                                                       1.0 / H, None, ALU.mult))
                        bar("v", vector)
                        mark("v", vector.tensor_tensor(srow[:, o + 3:o + 4], srow[:, o + 2:o + 3],
                                                       srow[:, o + 2:o + 3], ALU.mult))
                        bar("v", vector)
                        mark("v", vector.scalar_tensor_tensor(
                            srow[:, o + 4:o + 5], srow[:, o + 1:o + 2], 1.0 / H,
                            srow[:, o + 3:o + 4], ALU.mult, ALU.subtract), s_var)
                        vector.wait_ge(s_sd.h, tt + 1)
                        bar("v", vector)
                        mark("v", vector.reciprocal(srow[:, o + 6:o + 7], srow[:, o + 5:o + 6]))
                        bar("v", vector)
                        mark("v", vector.scalar_tensor_tensor(
                            srow[:, o + 7:o + 8], srow[:, o + 2:o + 3], -1.0,
                            srow[:, o + 6:o + 7], ALU.mult, ALU.mult))
                        if tt >= 2:
                            vector.wait_ge(s_tr.h, 4 * (tt - 1))
                        bar("v", vector)
                        mark("v", vector.tensor_scalar(ln1tm[tt % 2][:], x_in[b][:],
                                                       srow[:, o + 6:o + 7], srow[:, o + 7:o + 8],
                                                       ALU.mult, ALU.add), s_app)
                        if PB is not None:
                            want = (tt + 1) * 24 // 16
                            while ndq < want:
                                gate_plane(vector, ndq)
                                ndq += 1

                @blk.tensor
                def _(tensor):
                    for tt in range(16):
                        b = tt % 2
                        tensor.wait_ge(s_app.h, tt + 1)
                        for g in range(4):
                            gi = 4 * tt + g
                            if gi >= 2:
                                tensor.wait_ge(s_ev.h, gi - 1)
                            for j in range(4):
                                ht = 4 * g + j
                                inst = tensor.transpose(
                                    tr[gi % 2][:, 128 * j:128 * (j + 1)],
                                    ln1tm[b][:, 128 * ht:128 * (ht + 1)], s_identb[:])
                            mark("t", inst, s_tr)

                @blk.gpsimd
                def _(g):
                    if PB is not None:
                        for hb in range(2):
                            g.wait_ge(s_dqg.h, 8 * (hb + 1))
                            s_spq[hb % 2].inc(g.dma_start(
                                wgu_cin.ap()[1024 * hb:1024 * (hb + 1), :]
                                .rearrange("(k p) f -> p k f", p=128),
                                PB["wsl"][:, hb % 2, :, :]), 16)
                    g.wait_ge(s_ev.h, 64)

        free_phase_sems()
        tap("ln1T", ln1T[:], [128, NH, TALL], BF16)
        if BUILD_PHASES < 1:
            ln1T_ctx.close()
            return nc, dbg_outs

        with ExitStack() as pctx:
            psb = lambda n, s, d: pctx.enter_context(nc.sbuf_tensor(n, s, d))
            pps = lambda n, s, d: pctx.enter_context(nc.psum_tensor(n, s, d))
            wvb = [pctx.enter_context(nc.sbuf_tensor(f"p1a_wvb{i}", [128, NH, 512], BF16,
                                                      side="right")) for i in range(2)]
            vstage = [pctx.enter_context(nc.sbuf_tensor(f"p1a_vst{i}", [128, NH, 512], BF16,
                                                        side="right")) for i in range(2)]
            acc = [pps(f"p1a_acc{i}", [128, 512], F32) for i in range(2)]

            s_wvb = [sem(f"p1a_wvb{i}") for i in range(2)]
            s_mm = sem("p1a_mm")
            s_evv = sem("p1a_evv")

            with nc.Block() as blk:
                @blk.gpsimd
                def _(g):
                    for fg in range(4):
                        b = fg % 2
                        if fg >= 2:
                            g.wait_ge(s_mm.h, 16 * (fg - 1))
                        for j in range(4):
                            s_wvb[b].inc(g.dma_start(
                                wvb[b][:].rearrange("p a (j f) -> p a j f", j=4)[:, :, j, :],
                                wqkv[32 + 4 * fg + j]), 16)
                    if PB is not None:
                        for hb in range(2, 6):
                            g.wait_ge(s_dqg.h, 8 * (hb + 1))
                            s_spq[hb % 2].inc(g.dma_start(
                                wgu_cin.ap()[1024 * hb:1024 * (hb + 1), :]
                                .rearrange("(k p) f -> p k f", p=128),
                                PB["wsl"][:, hb % 2, :, :]), 16)
                        for tp in range(11):
                            g.wait_ge(s_dqd.h, 4 * (tp + 1))
                            it = 6 + tp
                            s_spq[it % 2].inc(g.dma_start(
                                wdn_cin.ap()[512 * tp:512 * (tp + 1), :]
                                .rearrange("(k p) f -> p k f", p=128),
                                PB["wsl"][:, it % 2, 0:4, 0:256]), 16)
                        g.wait_ge(s_spq[0].h, s_spq[0].n)
                        g.wait_ge(s_spq[1].h, s_spq[1].n)

                @blk.sync
                def _(sync):
                    for fg in range(4):
                        sync.wait_ge(s_evv.h, 16 * (fg + 1))
                        sbf = fg % 2
                        s_vsp[sbf].inc(sync.dma_start(
                            v_sp[:, :, 512 * fg:512 * (fg + 1)].rearrange("t p f -> p t f"),
                            vstage[sbf][:]), 16)
                    sync.wait_ge(s_vsp[0].h, s_vsp[0].n)
                    sync.wait_ge(s_vsp[1].h, s_vsp[1].n)

                @blk.vector
                def _(vector):
                    if PB is not None:
                        for j in range(24, 48):
                            gate_plane(vector, j)
                        for tp in range(11):
                            it = 6 + tp
                            vector.wait_ge(s_spq[it % 2].h, 16 * (it // 2))
                            for i_ in range(4):
                                mark("v", _dequant8(vector, mark, bar,
                                                    PB["dpkq"][:, tp, :],
                                                    PB["scld"][:, tp, :],
                                                    PB["wsl"][:, it % 2, i_, 0:256],
                                                    PB["dqs"], i_), s_dqd)

                @blk.tensor
                def _(tensor):
                    nmm = 0
                    for fg in range(4):
                        b = fg % 2
                        tensor.wait_ge(s_wvb[b].h, 64 * (fg // 2 + 1))
                        for tt in range(16):
                            pb = nmm % 2
                            if nmm >= 2:
                                tensor.wait_ge(s_evv.h, nmm - 1)
                            for hc in range(NH):
                                inst = tensor.matmul(acc[pb][:],
                                                     ln1T[:, hc, 128 * tt:128 * (tt + 1)],
                                                     wvb[b][:, hc, :],
                                                     start=(hc == 0), stop=(hc == NH - 1))
                            mark("t", inst, s_mm)
                            nmm += 1

                @blk.scalar
                def _(scalar):
                    nmm = 0
                    for fg in range(4):
                        sbf = fg % 2
                        for tt in range(16):
                            pb = nmm % 2
                            scalar.wait_ge(s_mm.h, nmm + 1)
                            if fg >= 2 and tt == 0:
                                scalar.wait_ge(s_vsp[fg % 2].h, 16 * (fg // 2))
                            mark("s", scalar.copy(vstage[sbf][:, tt, :], acc[pb][:]), s_evv)
                            nmm += 1

        free_phase_sems()
        pdq_ctx.close()
        if BUILD_PHASES < 2:
            ln1T_ctx.close()
            return nc, dbg_outs

        kq_ctx = ExitStack()
        kT = kq_ctx.enter_context(nc.sbuf_tensor("kT", [128, NH, TALL], BF16, side="right"))
        qT = kq_ctx.enter_context(nc.sbuf_tensor("qT", [128, NH, TOWN], BF16, side="right"))
        rsbk = kq_ctx.enter_context(nc.sbuf_tensor("rsbk", [128, TALL], F32, side="right"))
        rsbq = kq_ctx.enter_context(nc.sbuf_tensor("rsbq", [128, TOWN], F32, side="right"))

        p1b_ctx = ExitStack()
        psb = lambda n, s, d: p1b_ctx.enter_context(nc.sbuf_tensor(n, s, d))
        pps = lambda n, s, d: p1b_ctx.enter_context(nc.psum_tensor(n, s, d))
        wkb = [psb(f"p1b_wkb{i}", [128, NH, 128], BF16) for i in range(2)]
        wkf = [psb(f"p1b_wkf{i}", [128, 8, 128], F32) for i in range(2)]
        sqv = psb("p1b_sqv", [128, TALL], BF16)
        tmpA = psb("p1b_tmpA", [128, 1024], BF16)
        tmpB = psb("p1b_tmpB", [128, 1024], BF16)
        acc = [pps(f"p1b_acc{i}", [128, 512], F32) for i in range(2)]
        ssqk = pps("p1b_ssqk", [1, TALL], F32)
        ssqq = pps("p1b_ssqq", [1, TOWN], F32)

        wl = [("k", ft) for ft in range(16)] + [("q", ft) for ft in range(16)]
        items = []
        for kind, ft in wl:
            for g in range(4 if kind == "k" else 2):
                items.append((kind, ft, g))
        NIT = len(items)

        s_wkb = [sem(f"p1b_wkb{i}") for i in range(2)]
        s_cvt2 = sem("p1b_cvt")
        s_mm2 = sem("p1b_mm")
        s_evk = sem("p1b_evk")
        s_sq2 = sem("p1b_sq2")
        s_ones = sem("p1b_ones")
        s_ropeK = sem("p1b_ropeK")
        s_ropeQ = sem("p1b_ropeQ")

        def groups_done_through(li):
            return sum(4 if k == "k" else 2 for k, _ in wl[:li + 1])

        def sqv_slice(k2, g2, ft2):
            s = g2 if k2 == "k" else (g2 + 2 * (ft2 % 2))
            return sqv[:, 512 * s:512 * (s + 1)]

        def ones_mm(tensor, lag):
            k2, f2, g2 = items[lag]
            tensor.wait_ge(s_sq2.h, lag + 1)
            tgt = ssqk[:, 512 * g2:512 * (g2 + 1)] if k2 == "k" \
                else ssqq[:, 512 * g2:512 * (g2 + 1)]
            inst = tensor.matmul(tgt, s_onescol[:], sqv_slice(k2, g2, f2),
                                 start=(f2 == 0), stop=(f2 == 15))
            if f2 == 15:
                mark("t", inst, s_ones)

        with nc.Block() as blk:
            @blk.sync
            def _(sync):
                if BUILD_PHASES >= 5:
                    sync.wait_ge(pdq_env["s_pkt"].h, 16 * 23)
                for li, (kind, ft) in enumerate(wl):
                    tile = (16 if kind == "k" else 0) + ft
                    for hh in range(2):
                        j = 2 * li + hh
                        if j >= 2:
                            sync.wait_ge(s_cvt2.h, j - 1)
                        s_wkb[j % 2].inc(sync.dma_start(
                            wkf[j % 2][:], wqkv[tile, :, 8 * hh:8 * (hh + 1), :]), 16)

            @blk.tensor
            def _(tensor):
                ni = 0
                for li, (kind, ft) in enumerate(wl):
                    b = li % 2
                    tensor.wait_ge(s_cvt2.h, 2 * (li + 1))
                    ngr = 4 if kind == "k" else 2
                    for g in range(ngr):
                        pb = ni % 2
                        if ni >= 2:
                            tensor.wait_ge(s_evk.h, ni - 1)
                        src0 = 0 if kind == "k" else TOWN
                        for hc in range(NH):
                            inst = tensor.matmul(
                                acc[pb][:], wkb[b][:, hc, :],
                                ln1T[:, hc, src0 + 512 * g:src0 + 512 * (g + 1)],
                                start=(hc == 0), stop=(hc == NH - 1))
                        mark("t", inst, s_mm2)
                        ni += 1
                        if ni - 4 >= 0:
                            ones_mm(tensor, ni - 4)
                for lag in range(NIT - 3, NIT):
                    ones_mm(tensor, lag)

            @blk.scalar
            def _(scalar):
                ni = 0
                for li, (kind, ft) in enumerate(wl):
                    b = li % 2
                    for hh in range(2):
                        j = 2 * li + hh
                        scalar.wait_ge(s_wkb[j % 2].h, 16 * (j // 2 + 1))
                        if li >= 2 and hh == 0:
                            scalar.wait_ge(s_mm2.h, groups_done_through(li - 2))
                        mark("s", scalar.copy(
                            wkb[b][:, 8 * hh:8 * (hh + 1), :].rearrange("p a f -> p (a f)"),
                            wkf[j % 2][:].rearrange("p a f -> p (a f)")), s_cvt2)
                    ngr = 4 if kind == "k" else 2
                    for g in range(ngr):
                        pb = ni % 2
                        scalar.wait_ge(s_mm2.h, ni + 1)
                        dst = kT[:, ft, 512 * g:512 * (g + 1)] if kind == "k" \
                            else qT[:, ft, 512 * g:512 * (g + 1)]
                        wcol = kwsb[:, ft:ft + 1] if kind == "k" else qwsb[:, ft:ft + 1]
                        mark("s", scalar.activation(sqv_slice(kind, g, ft), acc[pb][:],
                                                    AF.Square), s_sq2)
                        mark("s", scalar.activation(dst, acc[pb][:], AF.Copy,
                                                    scale=wcol), s_evk)
                        ni += 1

            @blk.vector
            def _(vector):
                def rope(kind2, ft2):
                    if kind2 == "k":
                        kTT, W, c0 = kT, TALL, 0
                        done = 4 * ft2 + 4
                        rsem = s_ropeK
                    else:
                        kTT, W, c0 = qT, TOWN, TOWN
                        done = 64 + 2 * ft2 + 2
                        rsem = s_ropeQ
                    vector.wait_ge(s_evk.h, done)
                    for hf in range(W // 1024):
                        c1 = c0 + 1024 * hf
                        x1 = kTT[0:64, ft2, 1024 * hf:1024 * (hf + 1)]
                        x2 = kTT[64:128, ft2, 1024 * hf:1024 * (hf + 1)]
                        cl = cossb[0:64, c1:c1 + 1024]
                        ch = cossb[64:128, c1:c1 + 1024]
                        sl = sinsb[0:64, c1:c1 + 1024]
                        sh = sinsb[64:128, c1:c1 + 1024]
                        bar("v", vector)
                        mark("v", vector.tensor_mul(tmpA[0:64, :], x1, cl))
                        mark("v", vector.tensor_mul(tmpB[0:64, :], x2, sh))
                        mark("v", vector.tensor_mul(tmpA[64:128, :], x2, ch))
                        mark("v", vector.tensor_mul(tmpB[64:128, :], x1, sl))
                        bar("v", vector)
                        mark("v", vector.tensor_sub(x1, tmpA[0:64, :], tmpB[0:64, :]))
                        mark("v", vector.tensor_add(x2, tmpA[64:128, :], tmpB[64:128, :]))
                    rsem.inc(vector.engine_nop())

                for li, (kind, ft) in enumerate(wl):
                    if 0 <= li - 2 < 16:
                        rope("k", li - 2)
                    elif li - 2 >= 16:
                        rope("q", li - 18)
                for ft2 in (14, 15):
                    rope("q", ft2)

            @blk.sync
            def _(sync):
                sync.wait_ge(s_ones.h, 6)

        c_row = sem("p1c_row")
        c_rec = sem("p1c_rec")
        c_bc = sem("p1c_bc")
        c_bev = sem("p1c_bev")
        rrowk = rsbk[0:1, :]
        rrowq = rsbq[0:1, :]
        with nc.Block() as blk:
            @blk.scalar
            def _(scalar):
                mark("s", scalar.activation(rrowk, ssqk[0:1, :], AF.Sqrt,
                                            bias=EPSRMS[0:1, :], scale=1.0 / H), c_row)
                mark("s", scalar.activation(rrowq, ssqq[0:1, :], AF.Sqrt,
                                            bias=EPSRMS[0:1, :], scale=1.0 / H), c_row)
                for gi in range(6):
                    scalar.wait_ge(c_bc.h, gi + 1)
                    dst = rsbk[:, 512 * gi:512 * (gi + 1)] if gi < 4 \
                        else rsbq[:, 512 * (gi - 4):512 * (gi - 3)]
                    mark("s", scalar.copy(dst, acc[0][:]), c_bev)

            @blk.vector
            def _(vector):
                vector.wait_ge(c_row.h, 2)
                bar("v", vector)
                mark("v", vector.reciprocal(rrowk, rrowk))
                bar("v", vector)
                mark("v", vector.reciprocal(rrowq, rrowq))
                bar("v", vector)
                mark("v", vector.tensor_scalar(rrowq, rrowq, SCALE_S, None, ALU.mult), c_rec)

            @blk.tensor
            def _(tensor):
                tensor.wait_ge(c_rec.h, 1)
                for gi in range(6):
                    if gi >= 1:
                        tensor.wait_ge(c_bev.h, gi)
                    row = rsbk[0:1, 512 * gi:512 * (gi + 1)] if gi < 4 \
                        else rsbq[0:1, 512 * (gi - 4):512 * (gi - 3)]
                    mark("t", tensor.matmul(acc[0][:], s_onesrow[:], row,
                                            start=True, stop=True), c_bc)

            @blk.gpsimd
            def _(g):
                g.wait_ge(c_bev.h, 6)

        free_phase_sems()
        p1b_ctx.close()
        ln1T_ctx.close()

        tap("kTraw", kT[:], [128, NH, TALL], BF16)
        tap("qTraw", qT[:], [128, NH, TOWN], BF16)
        tap("rsbk", rsbk[:], [128, TALL], F32)
        tap("rsbq", rsbq[:], [128, TOWN], F32)

        if BUILD_PHASES < 3:
            kq_ctx.close()
            return nc, dbg_outs

        env = dict(nc=nc, sem=sem, mark=mark, bar=bar, root=root,
                   kT=kT, qT=qT, rsbk=rsbk, rsbq=rsbq,
                   v_sp=v_sp, s_identb=s_identb, s_identf=s_identf,
                   fmask=fmask, ctri=ctri, vm01=vm01, vones=vones,
                   s_vsp=s_vsp, EPSRMS=EPSRMS,
                   wgu_cin=wgu_cin, wgu_g=wgu_g, wdn_cin=wdn_cin, wdn_g=wdn_g,
                   s_ccg=s_ccg, s_ccd=s_ccd, do_cc=(BUILD_PHASES >= 5))
        oT_ctx = ExitStack()
        oT = oT_ctx.enter_context(nc.sbuf_tensor("oT", [128, NH, TOWN], BF16))
        env["oT"] = oT
        build_p2(env)
        free_phase_sems()

        tap("kT", kT[:], [128, NH, TALL], BF16)
        tap("qT", qT[:], [128, NH, TOWN], BF16)
        tap("oT", oT[:], [128, NH, TOWN], BF16)

        kq_ctx.close()
        if BUILD_PHASES < 4:
            oT_ctx.close()
            return nc, dbg_outs

        ln2_ctx = ExitStack()
        ln2pT = ln2_ctx.enter_context(nc.sbuf_tensor("ln2pT", [128, NH, TOWN], BF16, side="right"))
        env.update(dict(wo=wo, xh=xh, x2_sp=x2_sp, s_x2sp=s_x2sp, ln2pT=ln2pT,
                        EPSLN=EPSLN))
        build_p3(env)
        free_phase_sems()
        oT_ctx.close()

        tap("ln2pT", ln2pT[:], [128, NH, TOWN], BF16)

        if BUILD_PHASES < 5:
            ln2_ctx.close()
            return nc, dbg_outs

        env.update(dict(wgu_g=wgu_g, wdn_g=wdn_g, s_ccg=s_ccg, s_ccd=s_ccd,
                        h_sp=h_sp, s_hsp=s_hsp, out=out, s_outst=s_outst))
        build_p4(env)
        free_phase_sems()
        ln2_ctx.close()
        tap("h_sp", h_sp[:], [NKC_DN, 128, TOWN], BF16)
        build_p5(env)

    return nc, dbg_outs


def build_p2(env):
    nc = env["nc"]
    sem, mark, bar, root = env["sem"], env["mark"], env["bar"], env["root"]
    kT, qT, rsbk, rsbq = env["kT"], env["qT"], env["rsbk"], env["rsbq"]
    v_sp, s_identb = env["v_sp"], env["s_identb"]
    fmask, ctri, vm01, vones = env["fmask"], env["ctri"], env["vm01"], env["vones"]
    s_vsp, oT = env["s_vsp"], env["oT"]

    p2 = ExitStack()
    psb = lambda n, s, d: p2.enter_context(nc.sbuf_tensor(n, s, d))
    pps = lambda n, s, d: p2.enter_context(nc.psum_tensor(n, s, d))
    vroll = psb("p2_vroll", [128, 9, NH * 129], BF16)
    PT_sb = [psb(f"p2_P{i}", [128, 1152], BF16) for i in range(2)]
    o_sb = psb("p2_osb", [128, 2, 128], BF16)
    rrq = psb("p2_rr", [128, 2], F32)
    fmask_sb = psb("p2_fmask", [128, 128], BF16)
    ctri_sb = psb("p2_ctri", [128, 128], BF16)
    vm_sb = psb("p2_vm", [128, 128], BF16)
    vones_sb = psb("p2_vones", [128, 9 * NH], BF16)
    S_ps = [pps(f"p2_S{i}", [128, 1152], F32) for i in range(2)]
    o_ps = pps("p2_ops", [128, 129], F32)
    oTr_ps = pps("p2_otrp", [128, 128], BF16)

    s_msk_ld = sem("p2_mskld")
    s_von = sem("p2_von")
    s_vl = [sem(f"p2_vl{i}") for i in range(9)]
    s_kap = sem("p2_kap")
    s_qap = sem("p2_qap")
    s_S = sem("p2_S")
    s_exp = sem("p2_exp")
    s_trim = sem("p2_trim")
    s_pv = sem("p2_pv")
    s_rr = sem("p2_rr")
    s_oev = sem("p2_oev")
    s_otr = sem("p2_otr")
    s_ote = sem("p2_ote")

    NT = 128

    def qi_h(n):
        return n // 16, n % 16

    def vre(sl):
        return vroll[:, sl, :].rearrange("p (h c) -> p h c", c=129)[:, :, 0:128]

    with nc.Block() as blk:
        @blk.sync
        def _(sync):
            sync.dma_start(fmask_sb[:], fmask).then_inc(s_msk_ld.h, 16)
            sync.dma_start(ctri_sb[:], ctri).then_inc(s_msk_ld.h, 16)
            sync.dma_start(vm_sb[:], vm01).then_inc(s_msk_ld.h, 16)
            sync.dma_start(vones_sb[:], vones).then_inc(s_msk_ld.h, 16)
            s_msk_ld.n += 64
            sync.wait_ge(s_vsp[0].h, s_vsp[0].n)
            sync.wait_ge(s_vsp[1].h, s_vsp[1].n)
            for t in range(9):
                s_vl[t % 9].inc(sync.dma_start(
                    vre(t % 9), v_sp[t].rearrange("p (h c) -> p h c", c=128)), 16)
            for qi in range(1, 8):
                t = qi + 8
                sync.wait_ge(s_pv.h, 16 * qi)
                s_vl[t % 9].inc(sync.dma_start(
                    vre(t % 9), v_sp[t].rearrange("p (h c) -> p h c", c=128)), 16)

        @blk.vector
        def _(vector):
            def rs_apply(h):
                bar("v", vector)
                mark("v", vector.tensor_mul(kT[:, h, :], kT[:, h, :], rsbk[:]), s_kap)
                bar("v", vector)
                mark("v", vector.tensor_mul(qT[:, h, :], qT[:, h, :], rsbq[:]), s_qap)

            vector.wait_ge(s_msk_ld.h, 64)
            mark("v", vector.tensor_scalar(
                vroll[:].rearrange("p s (h c) -> p (s h) c", c=129)[:, :, 128:129],
                vones_sb[:].rearrange("p (x c) -> p x c", c=1),
                1.0, None, ALU.mult), s_von)
            rs_apply(0)
            for n in range(NT):
                if n + 1 < 16:
                    rs_apply(n + 1)
                mpar = n % 2
                vector.wait_ge(s_pv.h, n + 1)
                if n >= 2:
                    vector.wait_ge(s_oev.h, n - 1)
                bar("v", vector)
                mark("v", vector.reciprocal(rrq[:, mpar:mpar + 1], o_ps[:, 128:129]), s_rr)

        @blk.tensor
        def _(tensor):
            tensor.wait_ge(s_von.h, 1)
            for n in range(NT + 2):
                if n < NT:
                    qi, h = qi_h(n)
                    par = n % 2
                    if n < 16:
                        tensor.wait_ge(s_kap.h, h + 1)
                        tensor.wait_ge(s_qap.h, h + 1)
                    if n >= 2:
                        tensor.wait_ge(s_exp.h, n - 1)
                    rhs_q = qT[:, h, 128 * qi:128 * (qi + 1)]
                    for b in range(9):
                        tb = qi + b
                        inst = tensor.matmul(S_ps[par][:, 128 * b:128 * (b + 1)],
                                             kT[:, h, 128 * tb:128 * (tb + 1)], rhs_q,
                                             start=True, stop=True)
                    mark("t", inst, s_S)
                if n >= 2:
                    j = n - 2
                    tensor.wait_ge(s_oev.h, j + 1)
                    if j >= 1:
                        tensor.wait_ge(s_ote.h, j)
                    mark("t", tensor.transpose(oTr_ps[:], o_sb[:, j % 2, :],
                                               s_identb[:]), s_otr)
                if 1 <= n <= NT:
                    m = n - 1
                    mq, mh = qi_h(m)
                    mpar = m % 2
                    if mh == 0:
                        if mq == 0:
                            for t in range(9):
                                tensor.wait_ge(s_vl[t % 9].h, 16)
                        else:
                            tensor.wait_ge(s_vl[(mq + 8) % 9].h, 32)
                    tensor.wait_ge(s_trim.h, m + 1)
                    if m >= 1:
                        tensor.wait_ge(s_oev.h, m)
                    for b in range(9):
                        sl = (mq + b) % 9
                        inst = tensor.matmul(o_ps[:, 0:129],
                                             PT_sb[mpar][:, 128 * b:128 * (b + 1)],
                                             vroll[:, sl, 129 * mh:129 * mh + 129],
                                             start=(b == 0), stop=(b == 8))
                    mark("t", inst, s_pv)

        @blk.scalar
        def _(scalar):
            def oev(j):
                ppar = j % 2
                scalar.wait_ge(s_rr.h, j + 1)
                if j >= 2:
                    scalar.wait_ge(s_otr.h, j - 1)
                mark("s", scalar.activation(o_sb[:, ppar, :], o_ps[:, 0:128], AF.Copy,
                                            scale=rrq[:, ppar:ppar + 1]), s_oev)

            def otev(j):
                pq, ph = qi_h(j)
                scalar.wait_ge(s_otr.h, j + 1)
                mark("s", scalar.copy(oT[:, ph, 128 * pq:128 * (pq + 1)], oTr_ps[:]),
                     s_ote)

            for m in range(NT):
                mpar = m % 2
                scalar.wait_ge(s_S.h, m + 1)
                if m >= 2:
                    scalar.wait_ge(s_pv.h, m - 1)
                mark("s", scalar.activation(PT_sb[mpar][:], S_ps[mpar][:], AF.Exp), s_exp)
                if m >= 1:
                    oev(m - 1)
                if m >= 2:
                    otev(m - 2)
            oev(NT - 1)
            otev(NT - 2)
            otev(NT - 1)

        @blk.gpsimd
        def _(g):
            if env.get("do_cc"):
                g.collective_compute("AllGather", ALU.bypass,
                                     replica_groups=[list(range(8))],
                                     ins=[env["wgu_cin"].ap().opt()],
                                     outs=[env["wgu_g"].ap().opt()]).then_inc(env["s_ccg"].h, 1)
                env["s_ccg"].n += 1
                g.collective_compute("AllGather", ALU.bypass,
                                     replica_groups=[list(range(8))],
                                     ins=[env["wdn_cin"].ap().opt()],
                                     outs=[env["wdn_g"].ap().opt()]).then_inc(env["s_ccd"].h, 1)
                env["s_ccd"].n += 1
            g.wait_ge(s_msk_ld.h, 64)
            for m in range(NT):
                mq, mh = qi_h(m)
                mpar = m % 2
                g.wait_ge(s_exp.h, m + 1)
                bar("g", g)
                mark("g", g.tensor_mul(PT_sb[mpar][:, 0:128], PT_sb[mpar][:, 0:128],
                                       fmask_sb[:]))
                for b in range(1, 8):
                    if mq + b < 8:
                        mark("g", g.tensor_mul(PT_sb[mpar][:, 128 * b:128 * (b + 1)],
                                               PT_sb[mpar][:, 128 * b:128 * (b + 1)],
                                               vm_sb[:]))
                mark("g", g.tensor_mul(PT_sb[mpar][:, 1024:1152],
                                       PT_sb[mpar][:, 1024:1152], ctri_sb[:]), s_trim)
            g.wait_ge(s_ote.h, NT)

    p2.close()


def build_p3(env):
    nc = env["nc"]
    sem, mark, bar = env["sem"], env["mark"], env["bar"]
    oT, wo, xh = env["oT"], env["wo"], env["xh"]
    x2_sp, s_x2sp, ln2pT = env["x2_sp"], env["s_x2sp"], env["ln2pT"]
    s_identb, s_identf = env["s_identb"], env["s_identf"]
    EPSLN = env["EPSLN"]

    p3 = ExitStack()
    psb = lambda n, s, d: p3.enter_context(nc.sbuf_tensor(n, s, d))
    pps = lambda n, s, d: p3.enter_context(nc.psum_tensor(n, s, d))
    wob = [psb(f"p3_wob{i}", [128, NH, 128], BF16) for i in range(2)]
    y1T = [psb(f"p3_y1T{i}", [128, 512], F32) for i in range(2)]
    x2b = psb("p3_x2b", [128, 8, H], F32)
    xtile = [psb(f"p3_xt{i}", [128, H], F32) for i in range(2)]
    ln2tm = [psb(f"p3_ln2tm{i}", [128, H], BF16) for i in range(2)]
    srow3 = psb("p3_srow", [128, 16], F32)
    xsq3 = psb("p3_xsq", [128, H], BF16)
    wo_acc = [pps(f"p3_acc{i}", [128, 512], F32) for i in range(2)]
    tr_ps = [pps(f"p3_tr{i}", [128, 512], F32) for i in range(2)]
    pt3 = [pps(f"p3_pt{i}", [128, 512], BF16) for i in range(2)]

    s_wob = [sem(f"p3_wob{i}") for i in range(2)]
    s_mm3 = sem("p3_mm")
    s_y1ev = sem("p3_y1ev")
    s_tr3 = sem("p3_tr")
    s_x2w = sem("p3_x2w")
    s_xt = [sem(f"p3_xt{i}") for i in range(2)]
    s_add = sem("p3_add")
    s_sq3 = sem("p3_sq3")
    s_var3 = sem("p3_var")
    s_sd3 = sem("p3_sd")
    s_app3 = sem("p3_app")
    s_ptr = sem("p3_ptr")
    s_pev = sem("p3_pev")

    NIT3 = 32

    with nc.Block() as blk:
        @blk.gpsimd
        def _(g):
            for ft in range(16):
                b = ft % 2
                if ft >= 2:
                    g.wait_ge(s_mm3.h, 2 * (ft - 1))
                s_wob[b].inc(g.dma_start(wob[b][:], wo[ft]), 16)
            for tt in range(8):
                g.wait_ge(s_add.h, tt + 1)
                s_x2sp.inc(g.dma_start(x2_sp[tt], x2b[:, tt, :]), 16)
            g.wait_ge(s_x2sp.h, s_x2sp.n)

        @blk.tensor
        def _(tensor):
            for it in range(NIT3 + 1):
                if it < NIT3:
                    ft, tg = it // 2, it % 2
                    pb = it % 2
                    if tg == 0:
                        tensor.wait_ge(s_wob[ft % 2].h, 16 * (ft // 2 + 1))
                    if it >= 2:
                        tensor.wait_ge(s_y1ev.h, it - 1)
                    for hc in range(NH):
                        inst = tensor.matmul(wo_acc[pb][:], wob[ft % 2][:, hc, :],
                                             oT[:, hc, 512 * tg:512 * (tg + 1)],
                                             start=(hc == 0), stop=(hc == NH - 1))
                    mark("t", inst, s_mm3)
                if it >= 1:
                    m = it - 1
                    mpb = m % 2
                    tensor.wait_ge(s_y1ev.h, m + 1)
                    if m >= 2:
                        tensor.wait_ge(s_x2w.h, m - 1)
                    for st in range(4):
                        inst = tensor.transpose(tr_ps[mpb][:, 128 * st:128 * (st + 1)],
                                                y1T[mpb][:, 128 * st:128 * (st + 1)],
                                                s_identf[:])
                    mark("t", inst, s_tr3)
            for tt in range(8):
                tensor.wait_ge(s_app3.h, tt + 1)
                for gg in range(4):
                    gi = 4 * tt + gg
                    if gi >= 2:
                        tensor.wait_ge(s_pev.h, gi - 1)
                    for jj in range(4):
                        c4 = 4 * gg + jj
                        P_, i_ = c4 // 4, c4 % 4
                        pv = ln2tm[tt % 2][:].rearrange("p (P c b i) -> p P i (c b)",
                                                        P=4, c=64, b=2, i=4)
                        inst = tensor.transpose(pt3[gi % 2][:, 128 * jj:128 * (jj + 1)],
                                                pv[:, P_, i_, :], s_identb[:])
                    mark("t", inst, s_ptr)

        @blk.vector
        def _(vector):
            for it in range(NIT3):
                pb = it % 2
                vector.wait_ge(s_mm3.h, it + 1)
                if it >= 2:
                    vector.wait_ge(s_tr3.h, it - 1)
                bar("v", vector)
                mark("v", vector.tensor_copy(y1T[pb][:], wo_acc[pb][:]), s_y1ev)
            for tt in range(8):
                b = tt % 2
                o = 8 * (tt % 2)
                vector.wait_ge(s_xt[b].h, 16 * (tt // 2 + 1))
                if tt == 0:
                    vector.wait_ge(s_x2w.h, NIT3)
                bar("v", vector)
                mark("v", vector.tensor_add(x2b[:, tt, :], x2b[:, tt, :], xtile[b][:]), s_add)
                bar("v", vector)
                mark("v", vector.reduce_sum(srow3[:, o + 0:o + 1], x2b[:, tt, :], axis=AX.X))
                vector.wait_ge(s_sq3.h, tt + 1)
                bar("v", vector)
                mark("v", vector.tensor_scalar(srow3[:, o + 2:o + 3], srow3[:, o + 0:o + 1],
                                               1.0 / H, None, ALU.mult))
                bar("v", vector)
                mark("v", vector.tensor_tensor(srow3[:, o + 3:o + 4], srow3[:, o + 2:o + 3],
                                               srow3[:, o + 2:o + 3], ALU.mult))
                bar("v", vector)
                mark("v", vector.scalar_tensor_tensor(
                    srow3[:, o + 4:o + 5], srow3[:, o + 1:o + 2], 1.0 / H,
                    srow3[:, o + 3:o + 4], ALU.mult, ALU.subtract), s_var3)
                vector.wait_ge(s_sd3.h, tt + 1)
                bar("v", vector)
                mark("v", vector.reciprocal(srow3[:, o + 6:o + 7], srow3[:, o + 5:o + 6]))
                bar("v", vector)
                mark("v", vector.scalar_tensor_tensor(
                    srow3[:, o + 7:o + 8], srow3[:, o + 2:o + 3], -1.0,
                    srow3[:, o + 6:o + 7], ALU.mult, ALU.mult))
                if tt >= 2:
                    vector.wait_ge(s_ptr.h, 4 * (tt - 1))
                bar("v", vector)
                mark("v", vector.tensor_scalar(ln2tm[b][:], x2b[:, tt, :],
                                               srow3[:, o + 6:o + 7], srow3[:, o + 7:o + 8],
                                               ALU.mult, ALU.add), s_app3)

        @blk.scalar
        def _(scalar):
            for it in range(NIT3):
                mit = it % 2
                ft, tg = it // 2, it % 2
                scalar.wait_ge(s_tr3.h, it + 1)
                mark("s", scalar.copy(
                    x2b[:, 4 * tg:4 * tg + 4, 128 * ft:128 * (ft + 1)],
                    tr_ps[mit][:].rearrange("p (a t) -> p a t", a=4)), s_x2w)
            for tt in range(8):
                b = tt % 2
                o = 8 * (tt % 2)
                scalar.wait_ge(s_add.h, tt + 1)
                if tt >= 2:
                    scalar.wait_ge(s_var3.h, tt - 1)
                mark("s", scalar.activation(xsq3[:], x2b[:, tt, :], AF.Square,
                                            accum_out=srow3[:, o + 1:o + 2]), s_sq3)
                scalar.wait_ge(s_var3.h, tt + 1)
                mark("s", scalar.activation(srow3[:, o + 5:o + 6], srow3[:, o + 4:o + 5],
                                            AF.Sqrt, bias=EPSLN), s_sd3)
                for gg in range(4):
                    gi = 4 * tt + gg
                    scalar.wait_ge(s_ptr.h, gi + 1)
                    mark("s", scalar.copy(
                        ln2pT[:, 4 * gg:4 * gg + 4, 128 * tt:128 * (tt + 1)],
                        pt3[gi % 2][:].rearrange("p (a t) -> p a t", a=4)), s_pev)

        @blk.sync
        def _(sync):
            for tt in range(8):
                b = tt % 2
                if tt >= 2:
                    sync.wait_ge(s_app3.h, tt - 1)
                    sync.wait_ge(s_sq3.h, tt - 1)
                s_xt[b].inc(sync.dma_start(xtile[b][:], xh[8 + tt]), 16)
            sync.wait_ge(s_pev.h, 32)

    p3.close()


def build_pdq(env):
    nc = env["nc"]
    sem, mark, bar = env["sem"], env["mark"], env["bar"]
    gup16m, gusm, dnp16m, dnsm = env["gup16m"], env["gusm"], env["dnp16m"], env["dnsm"]
    bcast4 = env["bcast4"]

    pdq = env["pdq_ctx"]
    stg = ExitStack()
    psb = lambda n, s, d: pdq.enter_context(nc.sbuf_tensor(n, s, d))
    ssb = lambda n, s, d: stg.enter_context(nc.sbuf_tensor(n, s, d, side="right"))
    pps = lambda n, s, d: stg.enter_context(nc.psum_tensor(n, s, d))
    B = env["pdq_bufs"] = {}
    B["pkq"] = psb("pdq_pkq", [128, 12, 512], I16)
    B["dpkq"] = psb("pdq_dpkq", [128, 11, 256], I16)
    gsc = ssb("pdq_gsc", [4, 4, 1536], BF16)
    dsc = ssb("pdq_dsc", [4, 11, 256], BF16)
    bcsb = ssb("pdq_bcsb", [4, 128], BF16)
    B["sclq"] = psb("pdq_sclq", [128, 12, 512], BF16)
    B["scld"] = psb("pdq_scld", [128, 11, 256], BF16)
    B["wsl"] = psb("pdq_wsl", [128, 2, 8, 512], BF16)
    B["dqs"] = psb("pdq_dqs", [128, 5, 512], I16)
    bcps = pps("pdq_bc", [128, 512], F32)

    s_pkt = env["s_pkt"] = sem("pdq_pkt", persistent=True)
    s_sc = sem("pdq_sc", persistent=True)
    s_bcm = sem("pdq_bcm", persistent=True)
    s_bce = env["s_bce"] = sem("pdq_bce", persistent=True)

    gus_v = gusm.rearrange("(P j) f -> j P f", j=4)
    dns_v = dnsm.rearrange("(t j) f -> j t f", j=4)

    with nc.Block() as blk:
        @blk.sync
        def _(sync):
            for b in range(3):
                for P_ in range(4):
                    s_pkt.inc(sync.dma_start(
                        B["pkq"][:, 4 * b + P_, :],
                        gup16m[512 * b:512 * (b + 1), 128 * P_:128 * (P_ + 1)],
                        transpose=True), 16)
            for tp in range(11):
                s_pkt.inc(sync.dma_start(
                    B["dpkq"][:, tp, :], dnp16m[:, 128 * tp:128 * (tp + 1)],
                    transpose=True), 16)

        @blk.gpsimd
        def _(g):
            s_sc.inc(g.dma_start(bcsb[:], bcast4), 16)
            s_sc.inc(g.dma_start(gsc[:], gus_v), 16)
            s_sc.inc(g.dma_start(dsc[:], dns_v), 16)

        @blk.tensor
        def _(tensor):
            tensor.wait_ge(s_sc.h, 48)
            for s in range(12):
                if s >= 1:
                    tensor.wait_ge(s_bce.h, s)
                b, P_ = s // 4, s % 4
                mark("t", tensor.matmul(bcps[:], bcsb[:],
                                        gsc[0:4, P_, 512 * b:512 * (b + 1)],
                                        start=True, stop=True), s_bcm)
            for tp in range(11):
                s = 12 + tp
                tensor.wait_ge(s_bce.h, s)
                mark("t", tensor.matmul(bcps[:, 0:256], bcsb[:],
                                        dsc[0:4, tp, :],
                                        start=True, stop=True), s_bcm)

        @blk.scalar
        def _(scalar):
            for s in range(12):
                scalar.wait_ge(s_bcm.h, s + 1)
                mark("s", scalar.copy(B["sclq"][:, s, :], bcps[:]), s_bce)
            for tp in range(11):
                scalar.wait_ge(s_bcm.h, 12 + tp + 1)
                mark("s", scalar.copy(B["scld"][:, tp, :], bcps[:, 0:256]), s_bce)

    stg.close()


def _dequant8(vector, mark, bar, v, scl, dst, scr, i):
    ALUm = ALU
    W = v.shape[-1]
    m = scr[:, 0, 0:W]
    a = scr[:, 1, 0:W]
    b2 = scr[:, 2, 0:W]
    c = scr[:, 3, 0:W]
    s01 = scr[:, 4, 0:W]
    bar("v", vector)
    if i == 0:
        mark("v", vector.tensor_scalar(m, v, 7, None, ALUm.bitwise_and))
    else:
        mark("v", vector.tensor_scalar(m, v, 4 * i, 7, ALUm.logical_shift_right,
                                       ALUm.bitwise_and))
    mark("v", vector.tensor_scalar(s01, v, 4 * i + 3, 1, ALUm.logical_shift_right,
                                   ALUm.bitwise_and))
    bar("v", vector)
    mark("v", vector.scalar_tensor_tensor(a, m, 4, m, ALUm.max, ALUm.add))
    mark("v", vector.tensor_scalar(b2, m, 2, 12, ALUm.mult, ALUm.max))
    bar("v", vector)
    mark("v", vector.scalar_tensor_tensor(c, b2, -16, a, ALUm.add, ALUm.add))
    bar("v", vector)
    mark("v", vector.tensor_tensor(a, c, s01, ALUm.mult))
    bar("v", vector)
    mark("v", vector.scalar_tensor_tensor(m, a, -2, c, ALUm.mult, ALUm.add))
    bar("v", vector)
    return vector.tensor_tensor(dst, m, scl, ALUm.mult)


def build_p4(env):
    nc = env["nc"]
    sem, mark, bar = env["sem"], env["mark"], env["bar"]
    ln2pT = env["ln2pT"]
    wgu_g = env["wgu_g"]
    h_sp, s_hsp = env["h_sp"], env["s_hsp"]
    s_ccg = env["s_ccg"]

    p4 = ExitStack()
    psb = lambda n, s, d: p4.enter_context(nc.sbuf_tensor(n, s, d))
    pps = lambda n, s, d: p4.enter_context(nc.psum_tensor(n, s, d))
    wgu = [psb(f"p4_wgu{i}", [128, 2, 16, 512], BF16) for i in range(2)]
    sgate = psb("p4_sgate", [128, 2, 512], BF16)
    hstage = [psb(f"p4_hst{i}", [128, 4, TOWN], BF16) for i in range(2)]
    gps = [pps(f"p4_g{i}", [128, 512], F32) for i in range(2)]
    ups = [pps(f"p4_u{i}", [128, 512], F32) for i in range(2)]

    s_wg = [sem(f"p4_wg{i}") for i in range(2)]
    s_gu = sem("p4_gu")
    s_sil = sem("p4_sil")
    s_hh = sem("p4_hh")

    def gpos(u):
        return 3 * (u % 8) + u // 8


    def _p4_mms(tensor, tp):
        sl = tp % 2
        tensor.wait_ge(s_wg[sl].h, 32 * (tp // 2 + 1))
        for ip in range(4):
            for tg in range(2):
                it = 8 * tp + 2 * ip + tg
                pb = it % 2
                if it >= 2:
                    tensor.wait_ge(s_sil.h, it - 1)
                    tensor.wait_ge(s_hh.h, it - 1)
                for kc in range(16):
                    wv = wgu[sl][:, 0, kc, :].rearrange("p (c b i) -> p i (c b)",
                                                        c=64, b=2, i=4)[:, ip, :]
                    inst = tensor.matmul(gps[pb][:], wv,
                                         ln2pT[:, kc, 512 * tg:512 * (tg + 1)],
                                         start=(kc == 0), stop=(kc == 15))
                mark("t", inst, s_gu)
                for kc in range(16):
                    wv = wgu[sl][:, 1, kc, :].rearrange("p (c b i) -> p i (c b)",
                                                        c=64, b=2, i=4)[:, ip, :]
                    inst = tensor.matmul(ups[pb][:], wv,
                                         ln2pT[:, kc, 512 * tg:512 * (tg + 1)],
                                         start=(kc == 0), stop=(kc == 15))
                mark("t", inst, s_gu)

    def _p4_hcomb(vector, tp):
        hs = tp % 2
        for ii in range(8):
            it = 8 * tp + ii
            ip, tg = ii // 2, ii % 2
            pb = it % 2
            vector.wait_ge(s_sil.h, it + 1)
            vector.wait_ge(s_gu.h, 2 * it + 2)
            if tp >= 2 and ii == 0:
                vector.wait_ge(s_hsp[hs].h, 16 * (tp // 2))
            bar("v", vector)
            mark("v", vector.tensor_mul(hstage[hs][:, ip, 512 * tg:512 * (tg + 1)],
                                        sgate[:, it % 2, :], ups[pb][:]), s_hh)

    with nc.Block() as blk:
        @blk.sync
        def _(sync):
            sync.wait_ge(s_ccg.h, 1)
            for tp in range(11):
                sl = tp % 2
                if tp >= 2:
                    sync.wait_ge(s_gu.h, 16 * (tp - 1))
                for bb in range(2):
                    u = tp + 11 * bb
                    r0 = gpos(u) * 2048
                    s_wg[sl].inc(sync.dma_start(
                        wgu[sl][:, bb, :, :],
                        wgu_g.ap()[r0:r0 + 2048, :].rearrange("(k p) f -> p k f", p=128)),
                        16)

        @blk.gpsimd
        def _(g):
            for tp in range(11):
                if tp >= 2:
                    g.wait_ge(s_hh.h, 8 * (tp - 1))
                    hs = (tp - 2) % 2
                    s_hsp[hs].inc(g.dma_start(
                        h_sp[4 * (tp - 2):4 * (tp - 2) + 4].rearrange("k p t -> p k t"),
                        hstage[hs][:]), 16)
            for tp in (9, 10):
                g.wait_ge(s_hh.h, 8 * (tp + 1))
                hs = tp % 2
                s_hsp[hs].inc(g.dma_start(
                    h_sp[4 * tp:4 * tp + 4].rearrange("k p t -> p k t"),
                    hstage[hs][:]), 16)
            g.wait_ge(s_hsp[0].h, s_hsp[0].n)
            g.wait_ge(s_hsp[1].h, s_hsp[1].n)

        @blk.tensor
        def _(tensor):
            for tp in range(11):
                _p4_mms(tensor, tp)

        @blk.scalar
        def _(scalar):
            for tp in range(11):
                for ii in range(8):
                    it = 8 * tp + ii
                    pb = it % 2
                    scalar.wait_ge(s_gu.h, 2 * it + 1)
                    if it >= 2:
                        scalar.wait_ge(s_hh.h, it - 1)
                    mark("s", scalar.activation(sgate[:, it % 2, :], gps[pb][:],
                                                AF.Silu), s_sil)

        @blk.vector
        def _(vector):
            for tp in range(11):
                _p4_hcomb(vector, tp)

    p4.close()


def build_p5(env):
    nc = env["nc"]
    sem, mark, bar = env["sem"], env["mark"], env["bar"]
    wdn_g = env["wdn_g"]
    h_sp = env["h_sp"]
    s_hsp, s_x2sp = env["s_hsp"], env["s_x2sp"]
    x2_sp, out, s_outst = env["x2_sp"], env["out"], env["s_outst"]
    s_identf, EPSRMS = env["s_identf"], env["EPSRMS"]
    s_ccd = env["s_ccd"]

    p5b = ExitStack()
    psb = lambda n, s, d: p5b.enter_context(nc.sbuf_tensor(n, s, d))
    pps = lambda n, s, d: p5b.enter_context(nc.psum_tensor(n, s, d))
    hT = psb("p5_hT", [128, NKC_DN, TOWN], BF16)
    wdnc = [psb(f"p5_wdnc{i}", [128, NKC_DN, 128], BF16) for i in range(2)]
    yst = [psb(f"p5_yst{i}", [128, 512], F32) for i in range(2)]
    ybuf = psb("p5_ybuf", [128, 4, H], F32)
    x2t = [psb(f"p5_x2t{i}", [128, H], F32) for i in range(2)]
    outb = [psb(f"p5_outb{i}", [128, H], F32) for i in range(2)]
    ysq = psb("p5_ysq", [128, H], BF16)
    srow5 = psb("p5_srow", [128, 8], F32)
    dacc = [pps(f"p5_dacc{i}", [128, 512], F32) for i in range(2)]
    typs = [pps(f"p5_ty{i}", [128, 512], F32) for i in range(2)]

    s_hld = sem("p5_hld")
    s_wdc = [sem(f"p5_wdc{i}") for i in range(2)]
    s_dmm = sem("p5_dmm")
    s_yev = sem("p5_yev")
    s_ty = sem("p5_ty")
    s_yw = sem("p5_yw")
    s_x2t = [sem(f"p5_x2t{i}") for i in range(2)]
    s_sq5 = sem("p5_sq5")
    s_sd5 = sem("p5_sd5")
    s_fin = sem("p5_fin")

    with nc.Block() as blk:
        @blk.sync
        def _(sync):
            sync.wait_ge(s_hsp[0].h, 96)
            sync.wait_ge(s_hsp[1].h, 80)
            for pc in range(4):
                s_hld.inc(sync.dma_start(
                    hT[:, 11 * pc:11 * (pc + 1), :],
                    h_sp[11 * pc:11 * (pc + 1)].rearrange("k p t -> p k t")), 16)
            sync.wait_ge(s_ccd.h, 1)
            for it in range(32):
                ho = it % 16
                sl = it % 2
                rb, co = ho // 2, (ho % 2) * 128
                if it >= 2:
                    sync.wait_ge(s_dmm.h, it - 1)
                for pc in range(4):
                    s_wdc[sl].inc(sync.dma_start(
                        wdnc[sl][:, 11 * pc:11 * (pc + 1), :],
                        wdn_g.ap()[rb * 5632 + 1408 * pc:rb * 5632 + 1408 * (pc + 1),
                                   co:co + 128]
                        .rearrange("(k p) f -> p k f", p=128)), 16)

        @blk.tensor
        def _(tensor):
            tensor.wait_ge(s_hld.h, 64)
            for it in range(33):
                if it < 32:
                    tg, ho = it // 16, it % 16
                    pb = it % 2
                    sl = it % 2
                    tensor.wait_ge(s_wdc[sl].h, 64 * (it // 2 + 1))
                    if it >= 2:
                        tensor.wait_ge(s_yev.h, it - 1)
                    for kc in range(NKC_DN):
                        inst = tensor.matmul(dacc[pb][:], wdnc[sl][:, kc, :],
                                             hT[:, kc, 512 * tg:512 * (tg + 1)],
                                             start=(kc == 0), stop=(kc == NKC_DN - 1))
                    mark("t", inst, s_dmm)
                if it >= 1:
                    m = it - 1
                    mpb = m % 2
                    tensor.wait_ge(s_yev.h, m + 1)
                    if m >= 2:
                        tensor.wait_ge(s_yw.h, m - 1)
                    for st in range(4):
                        inst = tensor.transpose(typs[mpb][:, 128 * st:128 * (st + 1)],
                                                yst[mpb][:, 128 * st:128 * (st + 1)],
                                                s_identf[:])
                    mark("t", inst, s_ty)

        @blk.scalar
        def _(scalar):
            for it in range(32):
                tg, ho = it // 16, it % 16
                pb = it % 2
                if it == 16:
                    for tt8 in range(4):
                        scalar.wait_ge(s_sq5.h, tt8 + 1)
                        mark("s", scalar.activation(srow5[:, 5:6], srow5[:, 4:5],
                                                    AF.Sqrt, bias=EPSRMS,
                                                    scale=1.0 / H), s_sd5)
                    scalar.wait_ge(s_fin.h, 4)
                scalar.wait_ge(s_ty.h, it + 1)
                mark("s", scalar.copy(
                    ybuf[:, 0:4, 128 * ho:128 * (ho + 1)],
                    typs[pb][:].rearrange("p (a t) -> p a t", a=4)), s_yw)
            for tt8 in range(4, 8):
                scalar.wait_ge(s_sq5.h, tt8 + 1)
                mark("s", scalar.activation(srow5[:, 5:6], srow5[:, 4:5],
                                            AF.Sqrt, bias=EPSRMS, scale=1.0 / H), s_sd5)

        @blk.vector
        def _(vector):
            def final(tt8):
                b = tt8 % 2
                tt = tt8 % 4
                vector.wait_ge(s_x2t[b].h, 16 * (tt8 // 2 + 1))
                if tt8 >= 1:
                    vector.wait_ge(s_sd5.h, tt8)
                bar("v", vector)
                mark("v", vector.scalar_tensor_tensor(
                    ysq[:], ybuf[:, tt, :], 1.0, ybuf[:, tt, :],
                    ALU.mult, ALU.mult, accum_out=srow5[:, 4:5]), s_sq5)
                vector.wait_ge(s_sd5.h, tt8 + 1)
                bar("v", vector)
                mark("v", vector.reciprocal(srow5[:, 6:7], srow5[:, 5:6]))
                if tt8 >= 2:
                    vector.wait_ge(s_outst[tt8 % 2].h, 16 * (tt8 // 2))
                bar("v", vector)
                mark("v", vector.scalar_tensor_tensor(
                    outb[b][:], ybuf[:, tt, :], srow5[:, 6:7], x2t[b][:],
                    ALU.mult, ALU.add), s_fin)

            for it in range(32):
                pb = it % 2
                vector.wait_ge(s_dmm.h, it + 1)
                if it >= 2:
                    vector.wait_ge(s_ty.h, it - 1)
                bar("v", vector)
                mark("v", vector.tensor_copy(yst[pb][:], dacc[pb][:]), s_yev)
                if it == 16:
                    vector.wait_ge(s_yw.h, 16)
                    for tt8 in range(4):
                        final(tt8)
            vector.wait_ge(s_yw.h, 32)
            for tt8 in range(4, 8):
                final(tt8)

        @blk.gpsimd
        def _(g):
            g.wait_ge(s_x2sp.h, 16 * 8)
            s_x2t[0].inc(g.dma_start(x2t[0][:], x2_sp[0]), 16)
            s_x2t[1].inc(g.dma_start(x2t[1][:], x2_sp[1]), 16)
            for tt8 in range(8):
                g.wait_ge(s_fin.h, tt8 + 1)
                s_outst[tt8 % 2].inc(g.dma_start(out[tt8], outb[tt8 % 2][:]), 16)
                if tt8 + 2 < 8:
                    s_x2t[tt8 % 2].inc(g.dma_start(x2t[tt8 % 2][:], x2_sp[tt8 + 2]), 16)
            g.wait_ge(s_outst[0].h, 64)
            g.wait_ge(s_outst[1].h, 64)

    p5b.close()


def _host_prep(L0_attn_hidden_states, L0_attn_Wqkv, L0_attn_q_norm_w, L0_attn_k_norm_w,
               L0_attn_Wo, L0_ffn_gate_up_packed, L0_ffn_gate_up_scales,
               L0_ffn_down_packed, L0_ffn_down_scales):
    x = np.ascontiguousarray(L0_attn_hidden_states)
    wqkv = np.ascontiguousarray(
        L0_attn_Wqkv.astype(np.float32).reshape(NH, 128, 48, 128).transpose(2, 1, 0, 3))
    wo_ = np.ascontiguousarray(
        L0_attn_Wo.astype(np.float32).reshape(NH, 128, 16, 128).transpose(2, 1, 0, 3))
    qw_ = np.ascontiguousarray(L0_attn_q_norm_w, dtype=np.float32).reshape(NH, 128).T.copy()
    kw_ = np.ascontiguousarray(L0_attn_k_norm_w, dtype=np.float32).reshape(NH, 128).T.copy()
    gup16 = np.ascontiguousarray(L0_ffn_gate_up_packed).view(np.int16)
    gus_ = np.ascontiguousarray(L0_ffn_gate_up_scales)
    dnp16 = np.ascontiguousarray(L0_ffn_down_packed).view(np.int16)
    dns_ = np.ascontiguousarray(L0_ffn_down_scales)

    gup16m_l, gusm_l, dnp16m_l, dnsm_l = [], [], [], []
    for r in range(8):
        us = [r, r + 8, (r + 16) if r + 16 < 22 else r]
        gup16m_l.append(np.concatenate([gup16[512 * u:512 * (u + 1), :] for u in us], axis=0))
        gusm_l.append(np.ascontiguousarray(
            np.concatenate([gus_[:, 512 * u:512 * (u + 1)] for u in us], axis=1)))
        dnp16m_l.append(np.ascontiguousarray(dnp16[256 * r:256 * (r + 1), :]))
        dnsm_l.append(np.ascontiguousarray(dns_[:, 256 * r:256 * (r + 1)]))

    identb = np.eye(128, dtype=np.float32).astype(ml_dtypes.bfloat16)
    identf = np.eye(128, dtype=np.float32)
    onesrow = np.ones((1, 128), dtype=np.float32)
    onescol = np.ones((128, 1), dtype=np.float32).astype(ml_dtypes.bfloat16)
    bcast4_ = np.zeros((4, 128), dtype=np.float32)
    for g in range(4):
        bcast4_[g, 32 * g:32 * (g + 1)] = 0.5
    bcast4_ = bcast4_.astype(ml_dtypes.bfloat16)
    constv = np.zeros((128, 4), dtype=np.float32)
    constv[:, 0] = EPS_LN
    constv[:, 1] = EPS_RMS

    kl = np.arange(128)[:, None]
    ql = np.arange(128)[None, :]
    ctri_ = np.where(kl <= ql, 1.0, 0.0).astype(ml_dtypes.bfloat16)
    ftri_ = np.where(kl >= ql + 1, 1.0, 0.0).astype(ml_dtypes.bfloat16)
    zeros_ = np.zeros((128, 128), dtype=ml_dtypes.bfloat16)
    ones_ = np.ones((128, 128), dtype=ml_dtypes.bfloat16)
    vones_ = np.ones((128, 9 * NH), dtype=ml_dtypes.bfloat16)

    inv = (1.0 / (10000.0 ** (np.arange(0, HD, 2, dtype=np.float64) / HD)))

    in_maps = []
    for c in range(8):
        b, j = c // 4, c % 4
        own = x[b, 1024 * j:1024 * (j + 1)]
        halo = x[b, 1024 * (j - 1):1024 * j] if j > 0 else own
        xh_ = np.concatenate([halo, own], axis=0).reshape(16, 128, H)
        pos0 = 1024 * j - 1024
        pos = (pos0 + np.arange(TALL, dtype=np.float64))
        ang = pos[None, :] * inv[:, None]
        cosd_ = np.tile(np.cos(ang), (2, 1)).astype(ml_dtypes.bfloat16)
        sind_ = np.tile(np.sin(ang), (2, 1)).astype(ml_dtypes.bfloat16)
        if j == 0:
            fmask_ = zeros_
            vm01_ = zeros_
        else:
            fmask_ = ftri_
            vm01_ = ones_
        in_maps.append(dict(
            xh=xh_, wqkv=wqkv, wo=wo_, qw=qw_, kw=kw_,
            gup16m=gup16m_l[c], gusm=gusm_l[c], dnp16m=dnp16m_l[c], dnsm=dnsm_l[c],
            cosd=cosd_, sind=sind_, identb=identb,
            identf=identf, fmask=fmask_, ctri=ctri_, vm01=vm01_, vones=vones_,
            onesrow=onesrow, onescol=onescol, bcast4=bcast4_, constv=constv))
    return in_maps


def kernel(**inputs):
    in_maps = _host_prep(**inputs)
    nc, _ = build()
    res = run_bass_kernel_spmd(nc, in_maps, core_ids=list(range(8)))
    out = np.zeros((2, 4096, H), dtype=np.float32)
    for c in range(8):
        b, j = c // 4, c % 4
        out[b, 1024 * j:1024 * (j + 1)] = np.asarray(res.results[c]["out"]).reshape(1024, H)
    return out


# version 14
# speedup vs baseline: 1.0793x; 1.0462x over previous
import numpy as np
import ml_dtypes
import concourse.bass as bass
import concourse.mybir as mybir
from concourse.bass_utils import run_bass_kernel_spmd
from contextlib import ExitStack

F32 = mybir.dt.float32
BF16 = mybir.dt.bfloat16
I16 = mybir.dt.int16
F16 = mybir.dt.float16
AF = mybir.ActivationFunctionType
ALU = mybir.AluOpType
AX = mybir.AxisListType

H = 2048
NH = 16
HD = 128
TOWN = 1024
TALL = 2048
IM = 5632
EPS_LN = 1e-5
EPS_RMS = 1e-6
SCALE_S = float(1.0 / np.sqrt(HD))
NEG = -1e30
NKC_DN = 44

BUILD_PHASES = 99


class Sem:
    def __init__(self, nc, name):
        self.h = nc.alloc_semaphore(name=name)
        self.n = 0

    def inc(self, inst, k=1):
        inst.then_inc(self.h, k)
        self.n += k
        return self.n


def build(debug=()):
    nc = bass.Bass("TRN2", target_bir_lowering=False)

    xh = nc.dram_tensor("xh", [16, 128, H], F32, kind="ExternalInput").ap()
    wqkv = nc.dram_tensor("wqkv", [48, 128, NH, 128], F32, kind="ExternalInput").ap()
    wo = nc.dram_tensor("wo", [16, 128, NH, 128], F32, kind="ExternalInput").ap()
    qw = nc.dram_tensor("qw", [128, NH], F32, kind="ExternalInput").ap()
    kw = nc.dram_tensor("kw", [128, NH], F32, kind="ExternalInput").ap()
    gup16m = nc.dram_tensor("gup16m", [1536, 512], I16, kind="ExternalInput").ap()
    gusm = nc.dram_tensor("gusm", [16, 1536], F16, kind="ExternalInput").ap()
    dnp16m = nc.dram_tensor("dnp16m", [256, 1408], I16, kind="ExternalInput").ap()
    dnsm = nc.dram_tensor("dnsm", [44, 256], F16, kind="ExternalInput").ap()
    cosd = nc.dram_tensor("cosd", [128, TALL], BF16, kind="ExternalInput").ap()
    sind = nc.dram_tensor("sind", [128, TALL], BF16, kind="ExternalInput").ap()
    identb = nc.dram_tensor("identb", [128, 128], BF16, kind="ExternalInput").ap()
    identf = nc.dram_tensor("identf", [128, 128], F32, kind="ExternalInput").ap()
    fmask = nc.dram_tensor("fmask", [128, 128], BF16, kind="ExternalInput").ap()
    ctri = nc.dram_tensor("ctri", [128, 128], BF16, kind="ExternalInput").ap()
    vm01 = nc.dram_tensor("vm01", [128, 128], BF16, kind="ExternalInput").ap()
    vones = nc.dram_tensor("vones", [128, 9 * NH], BF16, kind="ExternalInput").ap()
    onesrow = nc.dram_tensor("onesrow", [1, 128], F32, kind="ExternalInput").ap()
    onescol = nc.dram_tensor("onescol", [128, 1], BF16, kind="ExternalInput").ap()
    bcast4 = nc.dram_tensor("bcast4", [4, 128], BF16, kind="ExternalInput").ap()
    constv = nc.dram_tensor("constv", [128, 4], F32, kind="ExternalInput").ap()

    out = nc.dram_tensor("out", [8, 128, H], F32, kind="ExternalOutput").ap()

    v_sp = nc.dram_tensor("v_sp", [4, 128, 16, 512], BF16, kind="Internal").ap()
    wgu_cin = nc.dram_tensor("wgu_cin", [3 * 16 * 128, 512], BF16, kind="Internal")
    wgu_g = nc.dram_tensor("wgu_g", [8 * 3 * 16 * 128, 512], BF16, kind="Internal",
                           addr_space="Shared")
    wdn_cin = nc.dram_tensor("wdn_cin", [44 * 128, 256], BF16, kind="Internal")
    wdn_g = nc.dram_tensor("wdn_g", [8 * 44 * 128, 256], BF16, kind="Internal",
                           addr_space="Shared")
    x2_sp = nc.dram_tensor("x2_sp", [8, 128, H], F32, kind="Internal").ap()
    h_sp = nc.dram_tensor("h_sp", [NKC_DN, 128, TOWN], BF16, kind="Internal").ap()

    dbg_outs = {}

    root = ExitStack()
    with root:
        def sb(name, shape, dtype, side="left"):
            return root.enter_context(nc.sbuf_tensor(name, shape, dtype, side=side))

        _phase_sems = []

        def sem(name, persistent=False):
            s = Sem(nc, name)
            if not persistent:
                _phase_sems.append(s)
            return s

        def free_phase_sems():
            if not _phase_sems:
                return
            with nc.Block() as blk:
                @blk.sync
                def _(sync):
                    for s in _phase_sems:
                        sync.sem_clear(s.h)
            for s in _phase_sems:
                nc.release_semaphore(s.h)
            _phase_sems.clear()
            last.clear()

        s_identb = sb("c_identb", [128, 128], BF16)
        s_identf = sb("c_identf", [128, 128], F32)
        s_cv = sb("c_cv", [128, 4], F32)
        s_onesrow = sb("c_onesrow", [1, 128], F32)
        s_onescol = sb("c_onescol", [128, 1], BF16)
        kwsb = sb("c_kwsb", [128, NH], F32)
        qwsb = sb("c_qwsb", [128, NH], F32)
        cossb = sb("c_cossb", [128, TALL], BF16)
        sinsb = sb("c_sinsb", [128, TALL], BF16)

        s_cons = sem("s_cons", persistent=True)
        s_dbg = sem("s_dbg", persistent=True)
        s_vsp = [sem("s_vsp0", persistent=True), sem("s_vsp1", persistent=True)]
        s_x2sp = sem("s_x2sp", persistent=True)
        s_hsp = [sem("s_hsp0", persistent=True), sem("s_hsp1", persistent=True)]
        s_outst = [sem("s_outst0", persistent=True), sem("s_outst1", persistent=True)]

        selfs = {e: sem(f"self_{e}", persistent=True) for e in ("v", "s", "t", "g")}
        last = {}

        def mark(e, inst, s=None, k=1):
            s = s if s is not None else selfs[e]
            s.inc(inst, k)
            last[e] = (s.h, s.n)
            return inst

        def bar(e, eng):
            if e in last:
                eng.wait_ge(*last[e])

        def tap(name, src_ap, shape, dtype):
            if name not in debug:
                return
            t = nc.dram_tensor("dbg_" + name, shape, dtype, kind="ExternalOutput").ap()
            dbg_outs[name] = t
            with nc.Block() as blk:
                @blk.gpsimd
                def _(gp):
                    s_dbg.inc(gp.dma_start(t, src_ap), 16)
                    gp.wait_ge(s_dbg.h, s_dbg.n)

        with nc.Block() as blk:
            @blk.sync
            def _(sync):
                for t, d in [(s_identb, identb), (s_identf, identf), (s_cv, constv),
                             (s_onesrow, onesrow), (s_onescol, onescol),
                             (kwsb, kw), (qwsb, qw), (cossb, cosd), (sinsb, sind)]:
                    s_cons.inc(sync.dma_start(t[:], d), 16)
                sync.wait_ge(s_cons.h, s_cons.n)

        EPSLN = s_cv[:, 0:1]
        EPSRMS = s_cv[:, 1:2]

        s_ccg = sem("s_ccg", persistent=True)
        s_ccd = sem("s_ccd", persistent=True)
        ln1T_ctx = ExitStack()
        ln1T = ln1T_ctx.enter_context(nc.sbuf_tensor("ln1T", [128, NH, TALL], BF16))
        pdq_ctx = ExitStack()
        pdq_env = dict(nc=nc, sem=sem, mark=mark, bar=bar, pdq_ctx=pdq_ctx,
                       gup16m=gup16m, gusm=gusm, dnp16m=dnp16m, dnsm=dnsm,
                       bcast4=bcast4, wgu_cin=wgu_cin, wgu_g=wgu_g,
                       wdn_cin=wdn_cin, wdn_g=wdn_g, s_ccg=s_ccg, s_ccd=s_ccd)
        if BUILD_PHASES >= 5:
            build_pdq(pdq_env)
        PB = pdq_env.get("pdq_bufs") if BUILD_PHASES >= 5 else None
        s_dqg = sem("s_dqg", persistent=True)
        s_dqd = sem("s_dqd", persistent=True)
        s_spq = [sem("s_spq0", persistent=True), sem("s_spq1", persistent=True)]

        def gate_plane(vector, j):
            hb = j // 8
            Pl = (j % 8) // 4
            i_ = j % 4
            b, Ph = hb // 2, (hb % 2) * 2
            P_ = Ph + Pl
            if j % 8 == 0 and hb >= 2:
                vector.wait_ge(s_spq[hb % 2].h, 16 * (hb // 2))
            mark("v", _dequant8(vector, mark, bar,
                                PB["pkq"][:, 4 * b + P_, :],
                                PB["sclq"][:, 4 * b + P_, :],
                                PB["wsl"][:, hb % 2, 4 * Pl + i_, :],
                                PB["dqs"], i_), s_dqg)

        with ExitStack() as pctx:
            psb = lambda n, s, d: pctx.enter_context(nc.sbuf_tensor(n, s, d))
            pps = lambda n, s, d: pctx.enter_context(nc.psum_tensor(n, s, d))
            x_in = [pctx.enter_context(nc.sbuf_tensor(f"p0_x{i}", [128, H], F32,
                                                       side="right")) for i in range(4)]
            xsq = pctx.enter_context(nc.sbuf_tensor("p0_xsq", [128, H], BF16, side="right"))
            ln1tm = [pctx.enter_context(nc.sbuf_tensor(f"p0_ln1tm{i}", [128, H], BF16,
                                                       side="right")) for i in range(2)]
            srow = pctx.enter_context(nc.sbuf_tensor("p0_srow", [128, 16], F32, side="right"))
            tr = [pps(f"p0_tr{i}", [128, 512], BF16) for i in range(2)]

            s_xld = [sem(f"p0_xld{i}") for i in range(4)]
            s_sq = sem("p0_sq")
            s_var = sem("p0_var")
            s_sd = sem("p0_sd")
            s_app = sem("p0_app")
            s_tr = sem("p0_tr")
            s_ev = sem("p0_ev")

            with nc.Block() as blk:
                @blk.sync
                def _(sync):
                    for tt in range(16):
                        b = tt % 4
                        if tt >= 4:
                            sync.wait_ge(s_app.h, tt - 3)
                            sync.wait_ge(s_sq.h, tt - 3)
                        s_xld[b].inc(sync.dma_start(x_in[b][:], xh[tt]), 16)

                @blk.scalar
                def _(scalar):
                    for tt in range(16):
                        b = tt % 4
                        o = 8 * (tt % 2)
                        scalar.wait_ge(s_xld[b].h, 16 * (tt // 4 + 1))
                        if tt >= 2:
                            scalar.wait_ge(s_var.h, tt - 1)
                        mark("s", scalar.activation(xsq[:], x_in[b][:], AF.Square,
                                                    accum_out=srow[:, o + 1:o + 2]), s_sq)
                        scalar.wait_ge(s_var.h, tt + 1)
                        mark("s", scalar.activation(srow[:, o + 5:o + 6], srow[:, o + 4:o + 5],
                                                    AF.Sqrt, bias=EPSLN), s_sd)
                        for g in range(4):
                            scalar.wait_ge(s_tr.h, 4 * tt + g + 1)
                            mark("s", scalar.copy(
                                ln1T[:, 4 * g:4 * g + 4, 128 * tt:128 * (tt + 1)],
                                tr[g % 2][:].rearrange("p (a t) -> p a t", a=4)), s_ev)

                @blk.vector
                def _(vector):
                    ndq = 0
                    if PB is not None:
                        vector.wait_ge(pdq_env["s_pkt"].h, 16 * 23)
                        vector.wait_ge(pdq_env["s_bce"].h, 23)
                    for tt in range(16):
                        b = tt % 4
                        o = 8 * (tt % 2)
                        vector.wait_ge(s_xld[b].h, 16 * (tt // 4 + 1))
                        bar("v", vector)
                        mark("v", vector.reduce_sum(srow[:, o + 0:o + 1], x_in[b][:], axis=AX.X))
                        vector.wait_ge(s_sq.h, tt + 1)
                        bar("v", vector)
                        mark("v", vector.tensor_scalar(srow[:, o + 2:o + 3], srow[:, o + 0:o + 1],
                                                       1.0 / H, None, ALU.mult))
                        bar("v", vector)
                        mark("v", vector.tensor_tensor(srow[:, o + 3:o + 4], srow[:, o + 2:o + 3],
                                                       srow[:, o + 2:o + 3], ALU.mult))
                        bar("v", vector)
                        mark("v", vector.scalar_tensor_tensor(
                            srow[:, o + 4:o + 5], srow[:, o + 1:o + 2], 1.0 / H,
                            srow[:, o + 3:o + 4], ALU.mult, ALU.subtract), s_var)
                        vector.wait_ge(s_sd.h, tt + 1)
                        bar("v", vector)
                        mark("v", vector.reciprocal(srow[:, o + 6:o + 7], srow[:, o + 5:o + 6]))
                        bar("v", vector)
                        mark("v", vector.scalar_tensor_tensor(
                            srow[:, o + 7:o + 8], srow[:, o + 2:o + 3], -1.0,
                            srow[:, o + 6:o + 7], ALU.mult, ALU.mult))
                        if tt >= 2:
                            vector.wait_ge(s_tr.h, 4 * (tt - 1))
                        bar("v", vector)
                        mark("v", vector.tensor_scalar(ln1tm[tt % 2][:], x_in[b][:],
                                                       srow[:, o + 6:o + 7], srow[:, o + 7:o + 8],
                                                       ALU.mult, ALU.add), s_app)
                        if PB is not None:
                            want = (tt + 1) * 24 // 16
                            while ndq < want:
                                gate_plane(vector, ndq)
                                ndq += 1

                @blk.tensor
                def _(tensor):
                    for tt in range(16):
                        b = tt % 2
                        tensor.wait_ge(s_app.h, tt + 1)
                        for g in range(4):
                            gi = 4 * tt + g
                            if gi >= 2:
                                tensor.wait_ge(s_ev.h, gi - 1)
                            for j in range(4):
                                ht = 4 * g + j
                                inst = tensor.transpose(
                                    tr[gi % 2][:, 128 * j:128 * (j + 1)],
                                    ln1tm[b][:, 128 * ht:128 * (ht + 1)], s_identb[:])
                            mark("t", inst, s_tr)

                @blk.gpsimd
                def _(g):
                    if PB is not None:
                        for hb in range(2):
                            g.wait_ge(s_dqg.h, 8 * (hb + 1))
                            s_spq[hb % 2].inc(g.dma_start(
                                wgu_cin.ap()[1024 * hb:1024 * (hb + 1), :]
                                .rearrange("(k p) f -> p k f", p=128),
                                PB["wsl"][:, hb % 2, :, :]), 16)
                    g.wait_ge(s_ev.h, 64)

        free_phase_sems()
        tap("ln1T", ln1T[:], [128, NH, TALL], BF16)
        if BUILD_PHASES < 1:
            ln1T_ctx.close()
            return nc, dbg_outs

        with ExitStack() as pctx:
            psb = lambda n, s, d: pctx.enter_context(nc.sbuf_tensor(n, s, d))
            pps = lambda n, s, d: pctx.enter_context(nc.psum_tensor(n, s, d))
            wvb = [pctx.enter_context(nc.sbuf_tensor(f"p1a_wvb{i}", [128, NH, 512], BF16,
                                                      side="right")) for i in range(2)]
            vstage = [pctx.enter_context(nc.sbuf_tensor(f"p1a_vst{i}", [128, NH, 512], BF16,
                                                        side="right")) for i in range(2)]
            acc = [pps(f"p1a_acc{i}", [128, 512], F32) for i in range(2)]

            s_wvb = [sem(f"p1a_wvb{i}") for i in range(2)]
            s_mm = sem("p1a_mm")
            s_evv = sem("p1a_evv")

            with nc.Block() as blk:
                @blk.gpsimd
                def _(g):
                    for fg in range(4):
                        b = fg % 2
                        if fg >= 2:
                            g.wait_ge(s_mm.h, 16 * (fg - 1))
                        for j in range(4):
                            s_wvb[b].inc(g.dma_start(
                                wvb[b][:].rearrange("p a (j f) -> p a j f", j=4)[:, :, j, :],
                                wqkv[32 + 4 * fg + j]), 16)
                    if PB is not None:
                        for hb in range(2, 6):
                            g.wait_ge(s_dqg.h, 8 * (hb + 1))
                            s_spq[hb % 2].inc(g.dma_start(
                                wgu_cin.ap()[1024 * hb:1024 * (hb + 1), :]
                                .rearrange("(k p) f -> p k f", p=128),
                                PB["wsl"][:, hb % 2, :, :]), 16)
                        for tp in range(11):
                            g.wait_ge(s_dqd.h, 4 * (tp + 1))
                            it = 6 + tp
                            s_spq[it % 2].inc(g.dma_start(
                                wdn_cin.ap()[512 * tp:512 * (tp + 1), :]
                                .rearrange("(k p) f -> p k f", p=128),
                                PB["wsl"][:, it % 2, 0:4, 0:256]), 16)
                        g.wait_ge(s_spq[0].h, s_spq[0].n)
                        g.wait_ge(s_spq[1].h, s_spq[1].n)

                @blk.sync
                def _(sync):
                    for fg in range(4):
                        sync.wait_ge(s_evv.h, 16 * (fg + 1))
                        sbf = fg % 2
                        s_vsp[sbf].inc(sync.dma_start(
                            v_sp[fg], vstage[sbf][:]), 16)
                    sync.wait_ge(s_vsp[0].h, s_vsp[0].n)
                    sync.wait_ge(s_vsp[1].h, s_vsp[1].n)

                @blk.vector
                def _(vector):
                    if PB is not None:
                        for j in range(24, 48):
                            gate_plane(vector, j)
                        for tp in range(11):
                            it = 6 + tp
                            vector.wait_ge(s_spq[it % 2].h, 16 * (it // 2))
                            for i_ in range(4):
                                mark("v", _dequant8(vector, mark, bar,
                                                    PB["dpkq"][:, tp, :],
                                                    PB["scld"][:, tp, :],
                                                    PB["wsl"][:, it % 2, i_, 0:256],
                                                    PB["dqs"], i_), s_dqd)

                @blk.tensor
                def _(tensor):
                    nmm = 0
                    for fg in range(4):
                        b = fg % 2
                        tensor.wait_ge(s_wvb[b].h, 64 * (fg // 2 + 1))
                        for tt in range(16):
                            pb = nmm % 2
                            if nmm >= 2:
                                tensor.wait_ge(s_evv.h, nmm - 1)
                            for hc in range(NH):
                                inst = tensor.matmul(acc[pb][:],
                                                     ln1T[:, hc, 128 * tt:128 * (tt + 1)],
                                                     wvb[b][:, hc, :],
                                                     start=(hc == 0), stop=(hc == NH - 1))
                            mark("t", inst, s_mm)
                            nmm += 1

                @blk.scalar
                def _(scalar):
                    nmm = 0
                    for fg in range(4):
                        sbf = fg % 2
                        for tt in range(16):
                            pb = nmm % 2
                            scalar.wait_ge(s_mm.h, nmm + 1)
                            if fg >= 2 and tt == 0:
                                scalar.wait_ge(s_vsp[fg % 2].h, 16 * (fg // 2))
                            mark("s", scalar.copy(vstage[sbf][:, tt, :], acc[pb][:]), s_evv)
                            nmm += 1

        free_phase_sems()
        pdq_ctx.close()
        if BUILD_PHASES < 2:
            ln1T_ctx.close()
            return nc, dbg_outs

        kq_ctx = ExitStack()
        kT = kq_ctx.enter_context(nc.sbuf_tensor("kT", [128, NH, TALL], BF16, side="right"))
        qT = kq_ctx.enter_context(nc.sbuf_tensor("qT", [128, NH, TOWN], BF16, side="right"))
        rsbk = kq_ctx.enter_context(nc.sbuf_tensor("rsbk", [128, TALL], F32, side="right"))
        rsbq = kq_ctx.enter_context(nc.sbuf_tensor("rsbq", [128, TOWN], F32, side="right"))

        p1b_ctx = ExitStack()
        psb = lambda n, s, d: p1b_ctx.enter_context(nc.sbuf_tensor(n, s, d))
        pps = lambda n, s, d: p1b_ctx.enter_context(nc.psum_tensor(n, s, d))
        wkb = [psb(f"p1b_wkb{i}", [128, NH, 128], BF16) for i in range(2)]
        wkf = [psb(f"p1b_wkf{i}", [128, 8, 128], F32) for i in range(2)]
        sqv = psb("p1b_sqv", [128, TALL], BF16)
        tmpA = psb("p1b_tmpA", [128, 1024], BF16)
        tmpB = psb("p1b_tmpB", [128, 1024], BF16)
        acc = [pps(f"p1b_acc{i}", [128, 512], F32) for i in range(2)]
        ssqk = pps("p1b_ssqk", [1, TALL], F32)
        ssqq = pps("p1b_ssqq", [1, TOWN], F32)

        wl = [("k", ft) for ft in range(16)] + [("q", ft) for ft in range(16)]
        items = []
        for kind, ft in wl:
            for g in range(4 if kind == "k" else 2):
                items.append((kind, ft, g))
        NIT = len(items)

        s_wkb = [sem(f"p1b_wkb{i}") for i in range(2)]
        s_cvt2 = sem("p1b_cvt")
        s_mm2 = sem("p1b_mm")
        s_evk = sem("p1b_evk")
        s_sq2 = sem("p1b_sq2")
        s_ones = sem("p1b_ones")
        s_ropeK = sem("p1b_ropeK")
        s_ropeQ = sem("p1b_ropeQ")

        def groups_done_through(li):
            return sum(4 if k == "k" else 2 for k, _ in wl[:li + 1])

        def sqv_slice(k2, g2, ft2):
            s = g2 if k2 == "k" else (g2 + 2 * (ft2 % 2))
            return sqv[:, 512 * s:512 * (s + 1)]

        def ones_mm(tensor, lag):
            k2, f2, g2 = items[lag]
            tensor.wait_ge(s_sq2.h, lag + 1)
            tgt = ssqk[:, 512 * g2:512 * (g2 + 1)] if k2 == "k" \
                else ssqq[:, 512 * g2:512 * (g2 + 1)]
            inst = tensor.matmul(tgt, s_onescol[:], sqv_slice(k2, g2, f2),
                                 start=(f2 == 0), stop=(f2 == 15))
            if f2 == 15:
                mark("t", inst, s_ones)

        with nc.Block() as blk:
            @blk.sync
            def _(sync):
                if BUILD_PHASES >= 5:
                    sync.wait_ge(pdq_env["s_pkt"].h, 16 * 23)
                for li, (kind, ft) in enumerate(wl):
                    tile = (16 if kind == "k" else 0) + ft
                    for hh in range(2):
                        j = 2 * li + hh
                        if j >= 2:
                            sync.wait_ge(s_cvt2.h, j - 1)
                        s_wkb[j % 2].inc(sync.dma_start(
                            wkf[j % 2][:], wqkv[tile, :, 8 * hh:8 * (hh + 1), :]), 16)

            @blk.tensor
            def _(tensor):
                ni = 0
                for li, (kind, ft) in enumerate(wl):
                    b = li % 2
                    tensor.wait_ge(s_cvt2.h, 2 * (li + 1))
                    ngr = 4 if kind == "k" else 2
                    for g in range(ngr):
                        pb = ni % 2
                        if ni >= 2:
                            tensor.wait_ge(s_evk.h, ni - 1)
                        src0 = 0 if kind == "k" else TOWN
                        for hc in range(NH):
                            inst = tensor.matmul(
                                acc[pb][:], wkb[b][:, hc, :],
                                ln1T[:, hc, src0 + 512 * g:src0 + 512 * (g + 1)],
                                start=(hc == 0), stop=(hc == NH - 1))
                        mark("t", inst, s_mm2)
                        ni += 1
                        if ni - 4 >= 0:
                            ones_mm(tensor, ni - 4)
                for lag in range(NIT - 3, NIT):
                    ones_mm(tensor, lag)

            @blk.scalar
            def _(scalar):
                ni = 0
                for li, (kind, ft) in enumerate(wl):
                    ngr = 4 if kind == "k" else 2
                    for g in range(ngr):
                        pb = ni % 2
                        scalar.wait_ge(s_mm2.h, ni + 1)
                        dst = kT[:, ft, 512 * g:512 * (g + 1)] if kind == "k" \
                            else qT[:, ft, 512 * g:512 * (g + 1)]
                        wcol = kwsb[:, ft:ft + 1] if kind == "k" else qwsb[:, ft:ft + 1]
                        mark("s", scalar.activation(sqv_slice(kind, g, ft), acc[pb][:],
                                                    AF.Square), s_sq2)
                        mark("s", scalar.activation(dst, acc[pb][:], AF.Copy,
                                                    scale=wcol), s_evk)
                        ni += 1

            @blk.vector
            def _(vector):
                def rope(kind2, ft2):
                    if kind2 == "k":
                        kTT, W, c0 = kT, TALL, 0
                        done = 4 * ft2 + 4
                        rsem = s_ropeK
                    else:
                        kTT, W, c0 = qT, TOWN, TOWN
                        done = 64 + 2 * ft2 + 2
                        rsem = s_ropeQ
                    vector.wait_ge(s_evk.h, done)
                    for hf in range(W // 1024):
                        c1 = c0 + 1024 * hf
                        x1 = kTT[0:64, ft2, 1024 * hf:1024 * (hf + 1)]
                        x2 = kTT[64:128, ft2, 1024 * hf:1024 * (hf + 1)]
                        cl = cossb[0:64, c1:c1 + 1024]
                        ch = cossb[64:128, c1:c1 + 1024]
                        sl = sinsb[0:64, c1:c1 + 1024]
                        sh = sinsb[64:128, c1:c1 + 1024]
                        bar("v", vector)
                        mark("v", vector.tensor_mul(tmpA[0:64, :], x1, cl))
                        mark("v", vector.tensor_mul(tmpB[0:64, :], x2, sh))
                        mark("v", vector.tensor_mul(tmpA[64:128, :], x2, ch))
                        mark("v", vector.tensor_mul(tmpB[64:128, :], x1, sl))
                        bar("v", vector)
                        mark("v", vector.tensor_sub(x1, tmpA[0:64, :], tmpB[0:64, :]))
                        mark("v", vector.tensor_add(x2, tmpA[64:128, :], tmpB[64:128, :]))
                    rsem.inc(vector.engine_nop())

                for li, (kind, ft) in enumerate(wl):
                    b = li % 2
                    for hh in range(2):
                        j = 2 * li + hh
                        vector.wait_ge(s_wkb[j % 2].h, 16 * (j // 2 + 1))
                        if li >= 2 and hh == 0:
                            vector.wait_ge(s_mm2.h, groups_done_through(li - 2))
                        bar("v", vector)
                        mark("v", vector.tensor_copy(
                            wkb[b][:, 8 * hh:8 * (hh + 1), :].rearrange("p a f -> p (a f)"),
                            wkf[j % 2][:].rearrange("p a f -> p (a f)")), s_cvt2)
                    if 0 <= li - 2 < 16:
                        rope("k", li - 2)
                    elif li - 2 >= 16:
                        rope("q", li - 18)
                for ft2 in (14, 15):
                    rope("q", ft2)

            @blk.sync
            def _(sync):
                sync.wait_ge(s_ones.h, 6)

        c_row = sem("p1c_row")
        c_rec = sem("p1c_rec")
        c_bc = sem("p1c_bc")
        c_bev = sem("p1c_bev")
        rrowk = rsbk[0:1, :]
        rrowq = rsbq[0:1, :]
        with nc.Block() as blk:
            @blk.scalar
            def _(scalar):
                mark("s", scalar.activation(rrowk, ssqk[0:1, :], AF.Sqrt,
                                            bias=EPSRMS[0:1, :], scale=1.0 / H), c_row)
                mark("s", scalar.activation(rrowq, ssqq[0:1, :], AF.Sqrt,
                                            bias=EPSRMS[0:1, :], scale=1.0 / H), c_row)
                for gi in range(6):
                    scalar.wait_ge(c_bc.h, gi + 1)
                    dst = rsbk[:, 512 * gi:512 * (gi + 1)] if gi < 4 \
                        else rsbq[:, 512 * (gi - 4):512 * (gi - 3)]
                    mark("s", scalar.copy(dst, acc[0][:]), c_bev)

            @blk.vector
            def _(vector):
                vector.wait_ge(c_row.h, 2)
                bar("v", vector)
                mark("v", vector.reciprocal(rrowk, rrowk))
                bar("v", vector)
                mark("v", vector.reciprocal(rrowq, rrowq))
                bar("v", vector)
                mark("v", vector.tensor_scalar(rrowq, rrowq, SCALE_S, None, ALU.mult), c_rec)

            @blk.tensor
            def _(tensor):
                tensor.wait_ge(c_rec.h, 1)
                for gi in range(6):
                    if gi >= 1:
                        tensor.wait_ge(c_bev.h, gi)
                    row = rsbk[0:1, 512 * gi:512 * (gi + 1)] if gi < 4 \
                        else rsbq[0:1, 512 * (gi - 4):512 * (gi - 3)]
                    mark("t", tensor.matmul(acc[0][:], s_onesrow[:], row,
                                            start=True, stop=True), c_bc)

            @blk.gpsimd
            def _(g):
                g.wait_ge(c_bev.h, 6)

        free_phase_sems()
        p1b_ctx.close()
        ln1T_ctx.close()

        tap("kTraw", kT[:], [128, NH, TALL], BF16)
        tap("qTraw", qT[:], [128, NH, TOWN], BF16)
        tap("rsbk", rsbk[:], [128, TALL], F32)
        tap("rsbq", rsbq[:], [128, TOWN], F32)

        if BUILD_PHASES < 3:
            kq_ctx.close()
            return nc, dbg_outs

        env = dict(nc=nc, sem=sem, mark=mark, bar=bar, root=root,
                   kT=kT, qT=qT, rsbk=rsbk, rsbq=rsbq,
                   v_sp=v_sp, s_identb=s_identb, s_identf=s_identf,
                   fmask=fmask, ctri=ctri, vm01=vm01, vones=vones,
                   s_vsp=s_vsp, EPSRMS=EPSRMS,
                   wgu_cin=wgu_cin, wgu_g=wgu_g, wdn_cin=wdn_cin, wdn_g=wdn_g,
                   s_ccg=s_ccg, s_ccd=s_ccd, do_cc=(BUILD_PHASES >= 5))
        oT_ctx = ExitStack()
        oT = oT_ctx.enter_context(nc.sbuf_tensor("oT", [128, NH, TOWN], BF16))
        env["oT"] = oT
        build_p2(env)
        free_phase_sems()

        tap("kT", kT[:], [128, NH, TALL], BF16)
        tap("qT", qT[:], [128, NH, TOWN], BF16)
        tap("oT", oT[:], [128, NH, TOWN], BF16)

        kq_ctx.close()
        if BUILD_PHASES < 4:
            oT_ctx.close()
            return nc, dbg_outs

        ln2_ctx = ExitStack()
        ln2pT = ln2_ctx.enter_context(nc.sbuf_tensor("ln2pT", [128, NH, TOWN], BF16, side="right"))
        env.update(dict(wo=wo, xh=xh, x2_sp=x2_sp, s_x2sp=s_x2sp, ln2pT=ln2pT,
                        EPSLN=EPSLN))
        build_p3(env)
        free_phase_sems()
        oT_ctx.close()

        tap("ln2pT", ln2pT[:], [128, NH, TOWN], BF16)

        if BUILD_PHASES < 5:
            ln2_ctx.close()
            return nc, dbg_outs

        env.update(dict(wgu_g=wgu_g, wdn_g=wdn_g, s_ccg=s_ccg, s_ccd=s_ccd,
                        h_sp=h_sp, s_hsp=s_hsp, out=out, s_outst=s_outst))
        build_p4(env)
        free_phase_sems()
        ln2_ctx.close()
        tap("h_sp", h_sp[:], [NKC_DN, 128, TOWN], BF16)
        build_p5(env)

    return nc, dbg_outs


def build_p2(env):
    nc = env["nc"]
    sem, mark, bar, root = env["sem"], env["mark"], env["bar"], env["root"]
    kT, qT, rsbk, rsbq = env["kT"], env["qT"], env["rsbk"], env["rsbq"]
    v_sp, s_identb = env["v_sp"], env["s_identb"]
    fmask, ctri, vm01, vones = env["fmask"], env["ctri"], env["vm01"], env["vones"]
    s_vsp, oT = env["s_vsp"], env["oT"]

    p2 = ExitStack()
    psb = lambda n, s, d: p2.enter_context(nc.sbuf_tensor(n, s, d))
    pps = lambda n, s, d: p2.enter_context(nc.psum_tensor(n, s, d))
    vroll = psb("p2_vroll", [128, 9, NH * 129], BF16)
    PT_sb = [psb(f"p2_P{i}", [128, 1152], BF16) for i in range(2)]
    o_sb = psb("p2_osb", [128, 2, 128], BF16)
    rrq = psb("p2_rr", [128, 2], F32)
    fmask_sb = psb("p2_fmask", [128, 128], BF16)
    ctri_sb = psb("p2_ctri", [128, 128], BF16)
    vm_sb = psb("p2_vm", [128, 128], BF16)
    vones_sb = psb("p2_vones", [128, 9 * NH], BF16)
    S_ps = [pps(f"p2_S{i}", [128, 1152], F32) for i in range(2)]
    o_ps = pps("p2_ops", [128, 129], F32)
    oTr_ps = pps("p2_otrp", [128, 128], BF16)

    s_msk_ld = sem("p2_mskld")
    s_von = sem("p2_von")
    s_vl = [sem(f"p2_vl{i}") for i in range(9)]
    s_kap = sem("p2_kap")
    s_qap = sem("p2_qap")
    s_S = sem("p2_S")
    s_exp = sem("p2_exp")
    s_trim = sem("p2_trim")
    s_pv = sem("p2_pv")
    s_rr = sem("p2_rr")
    s_oev = sem("p2_oev")
    s_otr = sem("p2_otr")
    s_ote = sem("p2_ote")

    NT = 128

    def qi_h(n):
        return n // 16, n % 16

    def vre(sl):
        return vroll[:, sl, :].rearrange("p (h c) -> p h c", c=129)[:, :, 0:128]

    with nc.Block() as blk:
        @blk.sync
        def _(sync):
            sync.dma_start(fmask_sb[:], fmask).then_inc(s_msk_ld.h, 16)
            sync.dma_start(ctri_sb[:], ctri).then_inc(s_msk_ld.h, 16)
            sync.dma_start(vm_sb[:], vm01).then_inc(s_msk_ld.h, 16)
            sync.dma_start(vones_sb[:], vones).then_inc(s_msk_ld.h, 16)
            s_msk_ld.n += 64
            sync.wait_ge(s_vsp[0].h, s_vsp[0].n)
            sync.wait_ge(s_vsp[1].h, s_vsp[1].n)

            def vload(t):
                sl = t % 9
                for fg in range(4):
                    dst = vroll[:, sl, :].rearrange("p (h c) -> p h c",
                                                    c=129)[:, 4 * fg:4 * (fg + 1), 0:128]
                    s_vl[sl].inc(sync.dma_start(
                        dst, v_sp[fg][:, t, :].rearrange("p (a c) -> p a c", c=128)), 16)

            for t in range(9):
                vload(t)
            for qi in range(1, 8):
                sync.wait_ge(s_pv.h, 16 * qi)
                vload(qi + 8)

        @blk.vector
        def _(vector):
            def rs_apply(h):
                bar("v", vector)
                mark("v", vector.tensor_mul(kT[:, h, :], kT[:, h, :], rsbk[:]), s_kap)
                bar("v", vector)
                mark("v", vector.tensor_mul(qT[:, h, :], qT[:, h, :], rsbq[:]), s_qap)

            vector.wait_ge(s_msk_ld.h, 64)
            mark("v", vector.tensor_scalar(
                vroll[:].rearrange("p s (h c) -> p (s h) c", c=129)[:, :, 128:129],
                vones_sb[:].rearrange("p (x c) -> p x c", c=1),
                1.0, None, ALU.mult), s_von)
            rs_apply(0)
            for n in range(NT):
                if n + 1 < 16:
                    rs_apply(n + 1)
                mpar = n % 2
                vector.wait_ge(s_pv.h, n + 1)
                if n >= 2:
                    vector.wait_ge(s_oev.h, n - 1)
                bar("v", vector)
                mark("v", vector.reciprocal(rrq[:, mpar:mpar + 1], o_ps[:, 128:129]), s_rr)

        @blk.tensor
        def _(tensor):
            tensor.wait_ge(s_von.h, 1)
            for n in range(NT + 2):
                if n < NT:
                    qi, h = qi_h(n)
                    par = n % 2
                    if n < 16:
                        tensor.wait_ge(s_kap.h, h + 1)
                        tensor.wait_ge(s_qap.h, h + 1)
                    if n >= 2:
                        tensor.wait_ge(s_exp.h, n - 1)
                    rhs_q = qT[:, h, 128 * qi:128 * (qi + 1)]
                    for b in range(9):
                        tb = qi + b
                        inst = tensor.matmul(S_ps[par][:, 128 * b:128 * (b + 1)],
                                             kT[:, h, 128 * tb:128 * (tb + 1)], rhs_q,
                                             start=True, stop=True)
                    mark("t", inst, s_S)
                if n >= 2:
                    j = n - 2
                    tensor.wait_ge(s_oev.h, j + 1)
                    if j >= 1:
                        tensor.wait_ge(s_ote.h, j)
                    mark("t", tensor.transpose(oTr_ps[:], o_sb[:, j % 2, :],
                                               s_identb[:]), s_otr)
                if 1 <= n <= NT:
                    m = n - 1
                    mq, mh = qi_h(m)
                    mpar = m % 2
                    if mh == 0:
                        if mq == 0:
                            for t in range(9):
                                tensor.wait_ge(s_vl[t % 9].h, 64)
                        else:
                            tensor.wait_ge(s_vl[(mq + 8) % 9].h, 128)
                    tensor.wait_ge(s_trim.h, m + 1)
                    if m >= 1:
                        tensor.wait_ge(s_oev.h, m)
                    for b in range(9):
                        sl = (mq + b) % 9
                        inst = tensor.matmul(o_ps[:, 0:129],
                                             PT_sb[mpar][:, 128 * b:128 * (b + 1)],
                                             vroll[:, sl, 129 * mh:129 * mh + 129],
                                             start=(b == 0), stop=(b == 8))
                    mark("t", inst, s_pv)

        @blk.scalar
        def _(scalar):
            def oev(j):
                ppar = j % 2
                scalar.wait_ge(s_rr.h, j + 1)
                if j >= 2:
                    scalar.wait_ge(s_otr.h, j - 1)
                mark("s", scalar.activation(o_sb[:, ppar, :], o_ps[:, 0:128], AF.Copy,
                                            scale=rrq[:, ppar:ppar + 1]), s_oev)

            def otev(j):
                pq, ph = qi_h(j)
                scalar.wait_ge(s_otr.h, j + 1)
                mark("s", scalar.copy(oT[:, ph, 128 * pq:128 * (pq + 1)], oTr_ps[:]),
                     s_ote)

            for m in range(NT):
                mpar = m % 2
                scalar.wait_ge(s_S.h, m + 1)
                if m >= 2:
                    scalar.wait_ge(s_pv.h, m - 1)
                mark("s", scalar.activation(PT_sb[mpar][:], S_ps[mpar][:], AF.Exp), s_exp)
                if m >= 1:
                    oev(m - 1)
                if m >= 2:
                    otev(m - 2)
            oev(NT - 1)
            otev(NT - 2)
            otev(NT - 1)

        @blk.gpsimd
        def _(g):
            if env.get("do_cc"):
                g.collective_compute("AllGather", ALU.bypass,
                                     replica_groups=[list(range(8))],
                                     ins=[env["wgu_cin"].ap().opt()],
                                     outs=[env["wgu_g"].ap().opt()]).then_inc(env["s_ccg"].h, 1)
                env["s_ccg"].n += 1
                g.collective_compute("AllGather", ALU.bypass,
                                     replica_groups=[list(range(8))],
                                     ins=[env["wdn_cin"].ap().opt()],
                                     outs=[env["wdn_g"].ap().opt()]).then_inc(env["s_ccd"].h, 1)
                env["s_ccd"].n += 1
            g.wait_ge(s_msk_ld.h, 64)
            for m in range(NT):
                mq, mh = qi_h(m)
                mpar = m % 2
                g.wait_ge(s_exp.h, m + 1)
                bar("g", g)
                mark("g", g.tensor_mul(PT_sb[mpar][:, 0:128], PT_sb[mpar][:, 0:128],
                                       fmask_sb[:]))
                for b in range(1, 8):
                    if mq + b < 8:
                        mark("g", g.tensor_mul(PT_sb[mpar][:, 128 * b:128 * (b + 1)],
                                               PT_sb[mpar][:, 128 * b:128 * (b + 1)],
                                               vm_sb[:]))
                mark("g", g.tensor_mul(PT_sb[mpar][:, 1024:1152],
                                       PT_sb[mpar][:, 1024:1152], ctri_sb[:]), s_trim)
            g.wait_ge(s_ote.h, NT)

    p2.close()


def build_p3(env):
    nc = env["nc"]
    sem, mark, bar = env["sem"], env["mark"], env["bar"]
    oT, wo, xh = env["oT"], env["wo"], env["xh"]
    x2_sp, s_x2sp, ln2pT = env["x2_sp"], env["s_x2sp"], env["ln2pT"]
    s_identb, s_identf = env["s_identb"], env["s_identf"]
    EPSLN = env["EPSLN"]

    p3 = ExitStack()
    psb = lambda n, s, d: p3.enter_context(nc.sbuf_tensor(n, s, d))
    pps = lambda n, s, d: p3.enter_context(nc.psum_tensor(n, s, d))
    wob = [psb(f"p3_wob{i}", [128, NH, 128], BF16) for i in range(2)]
    y1T = [psb(f"p3_y1T{i}", [128, 512], F32) for i in range(2)]
    x2b = psb("p3_x2b", [128, 8, H], F32)
    xtile = [psb(f"p3_xt{i}", [128, H], F32) for i in range(2)]
    ln2tm = [psb(f"p3_ln2tm{i}", [128, H], BF16) for i in range(2)]
    srow3 = psb("p3_srow", [128, 16], F32)
    xsq3 = psb("p3_xsq", [128, H], BF16)
    wo_acc = [pps(f"p3_acc{i}", [128, 512], F32) for i in range(2)]
    tr_ps = [pps(f"p3_tr{i}", [128, 512], F32) for i in range(2)]
    pt3 = [pps(f"p3_pt{i}", [128, 512], BF16) for i in range(2)]

    s_wob = [sem(f"p3_wob{i}") for i in range(2)]
    s_mm3 = sem("p3_mm")
    s_y1ev = sem("p3_y1ev")
    s_tr3 = sem("p3_tr")
    s_x2w = sem("p3_x2w")
    s_xt = [sem(f"p3_xt{i}") for i in range(2)]
    s_add = sem("p3_add")
    s_sq3 = sem("p3_sq3")
    s_var3 = sem("p3_var")
    s_sd3 = sem("p3_sd")
    s_app3 = sem("p3_app")
    s_ptr = sem("p3_ptr")
    s_pev = sem("p3_pev")

    NIT3 = 32

    with nc.Block() as blk:
        @blk.gpsimd
        def _(g):
            for ft in range(16):
                b = ft % 2
                if ft >= 2:
                    g.wait_ge(s_mm3.h, 2 * (ft - 1))
                s_wob[b].inc(g.dma_start(wob[b][:], wo[ft]), 16)
            for tt in range(8):
                g.wait_ge(s_add.h, tt + 1)
                s_x2sp.inc(g.dma_start(x2_sp[tt], x2b[:, tt, :]), 16)
            g.wait_ge(s_x2sp.h, s_x2sp.n)

        @blk.tensor
        def _(tensor):
            for it in range(NIT3 + 1):
                if it < NIT3:
                    ft, tg = it // 2, it % 2
                    pb = it % 2
                    if tg == 0:
                        tensor.wait_ge(s_wob[ft % 2].h, 16 * (ft // 2 + 1))
                    if it >= 2:
                        tensor.wait_ge(s_y1ev.h, it - 1)
                    for hc in range(NH):
                        inst = tensor.matmul(wo_acc[pb][:], wob[ft % 2][:, hc, :],
                                             oT[:, hc, 512 * tg:512 * (tg + 1)],
                                             start=(hc == 0), stop=(hc == NH - 1))
                    mark("t", inst, s_mm3)
                if it >= 1:
                    m = it - 1
                    mpb = m % 2
                    tensor.wait_ge(s_y1ev.h, m + 1)
                    if m >= 2:
                        tensor.wait_ge(s_x2w.h, m - 1)
                    for st in range(4):
                        inst = tensor.transpose(tr_ps[mpb][:, 128 * st:128 * (st + 1)],
                                                y1T[mpb][:, 128 * st:128 * (st + 1)],
                                                s_identf[:])
                    mark("t", inst, s_tr3)
            for tt in range(8):
                tensor.wait_ge(s_app3.h, tt + 1)
                for gg in range(4):
                    gi = 4 * tt + gg
                    if gi >= 2:
                        tensor.wait_ge(s_pev.h, gi - 1)
                    for jj in range(4):
                        c4 = 4 * gg + jj
                        P_, i_ = c4 // 4, c4 % 4
                        pv = ln2tm[tt % 2][:].rearrange("p (P c b i) -> p P i (c b)",
                                                        P=4, c=64, b=2, i=4)
                        inst = tensor.transpose(pt3[gi % 2][:, 128 * jj:128 * (jj + 1)],
                                                pv[:, P_, i_, :], s_identb[:])
                    mark("t", inst, s_ptr)

        @blk.vector
        def _(vector):
            for it in range(NIT3):
                pb = it % 2
                vector.wait_ge(s_mm3.h, it + 1)
                if it >= 2:
                    vector.wait_ge(s_tr3.h, it - 1)
                bar("v", vector)
                mark("v", vector.tensor_copy(y1T[pb][:], wo_acc[pb][:]), s_y1ev)
            for tt in range(8):
                b = tt % 2
                o = 8 * (tt % 2)
                vector.wait_ge(s_xt[b].h, 16 * (tt // 2 + 1))
                if tt == 0:
                    vector.wait_ge(s_x2w.h, NIT3)
                bar("v", vector)
                mark("v", vector.tensor_add(x2b[:, tt, :], x2b[:, tt, :], xtile[b][:]), s_add)
                bar("v", vector)
                mark("v", vector.reduce_sum(srow3[:, o + 0:o + 1], x2b[:, tt, :], axis=AX.X))
                vector.wait_ge(s_sq3.h, tt + 1)
                bar("v", vector)
                mark("v", vector.tensor_scalar(srow3[:, o + 2:o + 3], srow3[:, o + 0:o + 1],
                                               1.0 / H, None, ALU.mult))
                bar("v", vector)
                mark("v", vector.tensor_tensor(srow3[:, o + 3:o + 4], srow3[:, o + 2:o + 3],
                                               srow3[:, o + 2:o + 3], ALU.mult))
                bar("v", vector)
                mark("v", vector.scalar_tensor_tensor(
                    srow3[:, o + 4:o + 5], srow3[:, o + 1:o + 2], 1.0 / H,
                    srow3[:, o + 3:o + 4], ALU.mult, ALU.subtract), s_var3)
                vector.wait_ge(s_sd3.h, tt + 1)
                bar("v", vector)
                mark("v", vector.reciprocal(srow3[:, o + 6:o + 7], srow3[:, o + 5:o + 6]))
                bar("v", vector)
                mark("v", vector.scalar_tensor_tensor(
                    srow3[:, o + 7:o + 8], srow3[:, o + 2:o + 3], -1.0,
                    srow3[:, o + 6:o + 7], ALU.mult, ALU.mult))
                if tt >= 2:
                    vector.wait_ge(s_ptr.h, 4 * (tt - 1))
                bar("v", vector)
                mark("v", vector.tensor_scalar(ln2tm[b][:], x2b[:, tt, :],
                                               srow3[:, o + 6:o + 7], srow3[:, o + 7:o + 8],
                                               ALU.mult, ALU.add), s_app3)

        @blk.scalar
        def _(scalar):
            for it in range(NIT3):
                mit = it % 2
                ft, tg = it // 2, it % 2
                scalar.wait_ge(s_tr3.h, it + 1)
                mark("s", scalar.copy(
                    x2b[:, 4 * tg:4 * tg + 4, 128 * ft:128 * (ft + 1)],
                    tr_ps[mit][:].rearrange("p (a t) -> p a t", a=4)), s_x2w)
            for tt in range(8):
                b = tt % 2
                o = 8 * (tt % 2)
                scalar.wait_ge(s_add.h, tt + 1)
                if tt >= 2:
                    scalar.wait_ge(s_var3.h, tt - 1)
                mark("s", scalar.activation(xsq3[:], x2b[:, tt, :], AF.Square,
                                            accum_out=srow3[:, o + 1:o + 2]), s_sq3)
                scalar.wait_ge(s_var3.h, tt + 1)
                mark("s", scalar.activation(srow3[:, o + 5:o + 6], srow3[:, o + 4:o + 5],
                                            AF.Sqrt, bias=EPSLN), s_sd3)
                for gg in range(4):
                    gi = 4 * tt + gg
                    scalar.wait_ge(s_ptr.h, gi + 1)
                    mark("s", scalar.copy(
                        ln2pT[:, 4 * gg:4 * gg + 4, 128 * tt:128 * (tt + 1)],
                        pt3[gi % 2][:].rearrange("p (a t) -> p a t", a=4)), s_pev)

        @blk.sync
        def _(sync):
            for tt in range(8):
                b = tt % 2
                if tt >= 2:
                    sync.wait_ge(s_app3.h, tt - 1)
                    sync.wait_ge(s_sq3.h, tt - 1)
                s_xt[b].inc(sync.dma_start(xtile[b][:], xh[8 + tt]), 16)
            sync.wait_ge(s_pev.h, 32)

    p3.close()


def build_pdq(env):
    nc = env["nc"]
    sem, mark, bar = env["sem"], env["mark"], env["bar"]
    gup16m, gusm, dnp16m, dnsm = env["gup16m"], env["gusm"], env["dnp16m"], env["dnsm"]
    bcast4 = env["bcast4"]

    pdq = env["pdq_ctx"]
    stg = ExitStack()
    psb = lambda n, s, d: pdq.enter_context(nc.sbuf_tensor(n, s, d))
    ssb = lambda n, s, d: stg.enter_context(nc.sbuf_tensor(n, s, d, side="right"))
    pps = lambda n, s, d: stg.enter_context(nc.psum_tensor(n, s, d))
    B = env["pdq_bufs"] = {}
    B["pkq"] = psb("pdq_pkq", [128, 12, 512], I16)
    B["dpkq"] = psb("pdq_dpkq", [128, 11, 256], I16)
    gsc = ssb("pdq_gsc", [4, 4, 1536], BF16)
    dsc = ssb("pdq_dsc", [4, 11, 256], BF16)
    bcsb = ssb("pdq_bcsb", [4, 128], BF16)
    B["sclq"] = psb("pdq_sclq", [128, 12, 512], BF16)
    B["scld"] = psb("pdq_scld", [128, 11, 256], BF16)
    B["wsl"] = psb("pdq_wsl", [128, 2, 8, 512], BF16)
    B["dqs"] = psb("pdq_dqs", [128, 5, 512], I16)
    bcps = pps("pdq_bc", [128, 512], F32)

    s_pkt = env["s_pkt"] = sem("pdq_pkt", persistent=True)
    s_sc = sem("pdq_sc", persistent=True)
    s_bcm = sem("pdq_bcm", persistent=True)
    s_bce = env["s_bce"] = sem("pdq_bce", persistent=True)

    gus_v = gusm.rearrange("(P j) f -> j P f", j=4)
    dns_v = dnsm.rearrange("(t j) f -> j t f", j=4)

    with nc.Block() as blk:
        @blk.sync
        def _(sync):
            for b in range(3):
                for P_ in range(4):
                    s_pkt.inc(sync.dma_start(
                        B["pkq"][:, 4 * b + P_, :],
                        gup16m[512 * b:512 * (b + 1), 128 * P_:128 * (P_ + 1)],
                        transpose=True), 16)
            for tp in range(11):
                s_pkt.inc(sync.dma_start(
                    B["dpkq"][:, tp, :], dnp16m[:, 128 * tp:128 * (tp + 1)],
                    transpose=True), 16)

        @blk.gpsimd
        def _(g):
            s_sc.inc(g.dma_start(bcsb[:], bcast4), 16)
            s_sc.inc(g.dma_start(gsc[:], gus_v), 16)
            s_sc.inc(g.dma_start(dsc[:], dns_v), 16)

        @blk.tensor
        def _(tensor):
            tensor.wait_ge(s_sc.h, 48)
            for s in range(12):
                if s >= 1:
                    tensor.wait_ge(s_bce.h, s)
                b, P_ = s // 4, s % 4
                mark("t", tensor.matmul(bcps[:], bcsb[:],
                                        gsc[0:4, P_, 512 * b:512 * (b + 1)],
                                        start=True, stop=True), s_bcm)
            for tp in range(11):
                s = 12 + tp
                tensor.wait_ge(s_bce.h, s)
                mark("t", tensor.matmul(bcps[:, 0:256], bcsb[:],
                                        dsc[0:4, tp, :],
                                        start=True, stop=True), s_bcm)

        @blk.scalar
        def _(scalar):
            for s in range(12):
                scalar.wait_ge(s_bcm.h, s + 1)
                mark("s", scalar.copy(B["sclq"][:, s, :], bcps[:]), s_bce)
            for tp in range(11):
                scalar.wait_ge(s_bcm.h, 12 + tp + 1)
                mark("s", scalar.copy(B["scld"][:, tp, :], bcps[:, 0:256]), s_bce)

    stg.close()


def _dequant8(vector, mark, bar, v, scl, dst, scr, i):
    ALUm = ALU
    W = v.shape[-1]
    m = scr[:, 0, 0:W]
    a = scr[:, 1, 0:W]
    b2 = scr[:, 2, 0:W]
    c = scr[:, 3, 0:W]
    s01 = scr[:, 4, 0:W]
    bar("v", vector)
    if i == 0:
        mark("v", vector.tensor_scalar(m, v, 7, None, ALUm.bitwise_and))
    else:
        mark("v", vector.tensor_scalar(m, v, 4 * i, 7, ALUm.logical_shift_right,
                                       ALUm.bitwise_and))
    mark("v", vector.tensor_scalar(s01, v, 4 * i + 3, 1, ALUm.logical_shift_right,
                                   ALUm.bitwise_and))
    bar("v", vector)
    mark("v", vector.scalar_tensor_tensor(a, m, 4, m, ALUm.max, ALUm.add))
    mark("v", vector.tensor_scalar(b2, m, 2, 12, ALUm.mult, ALUm.max))
    bar("v", vector)
    mark("v", vector.scalar_tensor_tensor(c, b2, -16, a, ALUm.add, ALUm.add))
    bar("v", vector)
    mark("v", vector.tensor_tensor(a, c, s01, ALUm.mult))
    bar("v", vector)
    mark("v", vector.scalar_tensor_tensor(m, a, -2, c, ALUm.mult, ALUm.add))
    bar("v", vector)
    return vector.tensor_tensor(dst, m, scl, ALUm.mult)


def build_p4(env):
    nc = env["nc"]
    sem, mark, bar = env["sem"], env["mark"], env["bar"]
    ln2pT = env["ln2pT"]
    wgu_g = env["wgu_g"]
    h_sp, s_hsp = env["h_sp"], env["s_hsp"]
    s_ccg = env["s_ccg"]

    p4 = ExitStack()
    psb = lambda n, s, d: p4.enter_context(nc.sbuf_tensor(n, s, d))
    pps = lambda n, s, d: p4.enter_context(nc.psum_tensor(n, s, d))
    wgu = [psb(f"p4_wgu{i}", [128, 2, 16, 512], BF16) for i in range(2)]
    sgate = psb("p4_sgate", [128, 2, 512], BF16)
    hstage = [psb(f"p4_hst{i}", [128, 4, TOWN], BF16) for i in range(2)]
    gps = [pps(f"p4_g{i}", [128, 512], F32) for i in range(2)]
    ups = [pps(f"p4_u{i}", [128, 512], F32) for i in range(2)]

    s_wg = [sem(f"p4_wg{i}") for i in range(2)]
    s_gu = sem("p4_gu")
    s_sil = sem("p4_sil")
    s_hh = sem("p4_hh")

    def gpos(u):
        return 3 * (u % 8) + u // 8


    def _p4_mms(tensor, tp):
        sl = tp % 2
        tensor.wait_ge(s_wg[sl].h, 32 * (tp // 2 + 1))
        for ip in range(4):
            for tg in range(2):
                it = 8 * tp + 2 * ip + tg
                pb = it % 2
                if it >= 2:
                    tensor.wait_ge(s_sil.h, it - 1)
                    tensor.wait_ge(s_hh.h, it - 1)
                for kc in range(16):
                    wv = wgu[sl][:, 0, kc, :].rearrange("p (c b i) -> p i (c b)",
                                                        c=64, b=2, i=4)[:, ip, :]
                    inst = tensor.matmul(gps[pb][:], wv,
                                         ln2pT[:, kc, 512 * tg:512 * (tg + 1)],
                                         start=(kc == 0), stop=(kc == 15))
                mark("t", inst, s_gu)
                for kc in range(16):
                    wv = wgu[sl][:, 1, kc, :].rearrange("p (c b i) -> p i (c b)",
                                                        c=64, b=2, i=4)[:, ip, :]
                    inst = tensor.matmul(ups[pb][:], wv,
                                         ln2pT[:, kc, 512 * tg:512 * (tg + 1)],
                                         start=(kc == 0), stop=(kc == 15))
                mark("t", inst, s_gu)

    def _p4_hcomb(vector, tp):
        hs = tp % 2
        for ii in range(8):
            it = 8 * tp + ii
            ip, tg = ii // 2, ii % 2
            pb = it % 2
            vector.wait_ge(s_sil.h, it + 1)
            vector.wait_ge(s_gu.h, 2 * it + 2)
            if tp >= 2 and ii == 0:
                vector.wait_ge(s_hsp[hs].h, 16 * (tp // 2))
            bar("v", vector)
            mark("v", vector.tensor_mul(hstage[hs][:, ip, 512 * tg:512 * (tg + 1)],
                                        sgate[:, it % 2, :], ups[pb][:]), s_hh)

    with nc.Block() as blk:
        @blk.sync
        def _(sync):
            sync.wait_ge(s_ccg.h, 1)
            for tp in range(11):
                sl = tp % 2
                if tp >= 2:
                    sync.wait_ge(s_gu.h, 16 * (tp - 1))
                for bb in range(2):
                    u = tp + 11 * bb
                    r0 = gpos(u) * 2048
                    s_wg[sl].inc(sync.dma_start(
                        wgu[sl][:, bb, :, :],
                        wgu_g.ap()[r0:r0 + 2048, :].rearrange("(k p) f -> p k f", p=128)),
                        16)

        @blk.gpsimd
        def _(g):
            for tp in range(11):
                if tp >= 2:
                    g.wait_ge(s_hh.h, 8 * (tp - 1))
                    hs = (tp - 2) % 2
                    s_hsp[hs].inc(g.dma_start(
                        h_sp[4 * (tp - 2):4 * (tp - 2) + 4].rearrange("k p t -> p k t"),
                        hstage[hs][:]), 16)
            for tp in (9, 10):
                g.wait_ge(s_hh.h, 8 * (tp + 1))
                hs = tp % 2
                s_hsp[hs].inc(g.dma_start(
                    h_sp[4 * tp:4 * tp + 4].rearrange("k p t -> p k t"),
                    hstage[hs][:]), 16)
            g.wait_ge(s_hsp[0].h, s_hsp[0].n)
            g.wait_ge(s_hsp[1].h, s_hsp[1].n)

        @blk.tensor
        def _(tensor):
            for tp in range(11):
                _p4_mms(tensor, tp)

        @blk.scalar
        def _(scalar):
            for tp in range(11):
                for ii in range(8):
                    it = 8 * tp + ii
                    pb = it % 2
                    scalar.wait_ge(s_gu.h, 2 * it + 1)
                    if it >= 2:
                        scalar.wait_ge(s_hh.h, it - 1)
                    mark("s", scalar.activation(sgate[:, it % 2, :], gps[pb][:],
                                                AF.Silu), s_sil)

        @blk.vector
        def _(vector):
            for tp in range(11):
                _p4_hcomb(vector, tp)

    p4.close()


def build_p5(env):
    nc = env["nc"]
    sem, mark, bar = env["sem"], env["mark"], env["bar"]
    wdn_g = env["wdn_g"]
    h_sp = env["h_sp"]
    s_hsp, s_x2sp = env["s_hsp"], env["s_x2sp"]
    x2_sp, out, s_outst = env["x2_sp"], env["out"], env["s_outst"]
    s_identf, EPSRMS = env["s_identf"], env["EPSRMS"]
    s_ccd = env["s_ccd"]

    p5b = ExitStack()
    psb = lambda n, s, d: p5b.enter_context(nc.sbuf_tensor(n, s, d))
    pps = lambda n, s, d: p5b.enter_context(nc.psum_tensor(n, s, d))
    hT = psb("p5_hT", [128, NKC_DN, TOWN], BF16)
    wdnc = [psb(f"p5_wdnc{i}", [128, NKC_DN, 128], BF16) for i in range(2)]
    yst = [psb(f"p5_yst{i}", [128, 512], F32) for i in range(2)]
    ybuf = psb("p5_ybuf", [128, 4, H], F32)
    x2t = [psb(f"p5_x2t{i}", [128, H], F32) for i in range(2)]
    outb = [psb(f"p5_outb{i}", [128, H], F32) for i in range(2)]
    ysq = psb("p5_ysq", [128, H], BF16)
    srow5 = psb("p5_srow", [128, 8], F32)
    dacc = [pps(f"p5_dacc{i}", [128, 512], F32) for i in range(2)]
    typs = [pps(f"p5_ty{i}", [128, 512], F32) for i in range(2)]

    s_hld = sem("p5_hld")
    s_wdc = [sem(f"p5_wdc{i}") for i in range(2)]
    s_dmm = sem("p5_dmm")
    s_yev = sem("p5_yev")
    s_ty = sem("p5_ty")
    s_yw = sem("p5_yw")
    s_x2t = [sem(f"p5_x2t{i}") for i in range(2)]
    s_sq5 = sem("p5_sq5")
    s_sd5 = sem("p5_sd5")
    s_fin = sem("p5_fin")

    with nc.Block() as blk:
        @blk.sync
        def _(sync):
            sync.wait_ge(s_hsp[0].h, 96)
            sync.wait_ge(s_hsp[1].h, 80)
            for pc in range(4):
                s_hld.inc(sync.dma_start(
                    hT[:, 11 * pc:11 * (pc + 1), :],
                    h_sp[11 * pc:11 * (pc + 1)].rearrange("k p t -> p k t")), 16)
            sync.wait_ge(s_ccd.h, 1)
            for it in range(32):
                ho = it % 16
                sl = it % 2
                rb, co = ho // 2, (ho % 2) * 128
                if it >= 2:
                    sync.wait_ge(s_dmm.h, it - 1)
                for pc in range(4):
                    s_wdc[sl].inc(sync.dma_start(
                        wdnc[sl][:, 11 * pc:11 * (pc + 1), :],
                        wdn_g.ap()[rb * 5632 + 1408 * pc:rb * 5632 + 1408 * (pc + 1),
                                   co:co + 128]
                        .rearrange("(k p) f -> p k f", p=128)), 16)

        @blk.tensor
        def _(tensor):
            tensor.wait_ge(s_hld.h, 64)
            for it in range(33):
                if it < 32:
                    tg, ho = it // 16, it % 16
                    pb = it % 2
                    sl = it % 2
                    tensor.wait_ge(s_wdc[sl].h, 64 * (it // 2 + 1))
                    if it >= 2:
                        tensor.wait_ge(s_yev.h, it - 1)
                    for kc in range(NKC_DN):
                        inst = tensor.matmul(dacc[pb][:], wdnc[sl][:, kc, :],
                                             hT[:, kc, 512 * tg:512 * (tg + 1)],
                                             start=(kc == 0), stop=(kc == NKC_DN - 1))
                    mark("t", inst, s_dmm)
                if it >= 1:
                    m = it - 1
                    mpb = m % 2
                    tensor.wait_ge(s_yev.h, m + 1)
                    if m >= 2:
                        tensor.wait_ge(s_yw.h, m - 1)
                    for st in range(4):
                        inst = tensor.transpose(typs[mpb][:, 128 * st:128 * (st + 1)],
                                                yst[mpb][:, 128 * st:128 * (st + 1)],
                                                s_identf[:])
                    mark("t", inst, s_ty)

        @blk.scalar
        def _(scalar):
            for it in range(32):
                tg, ho = it // 16, it % 16
                pb = it % 2
                if it == 16:
                    for tt8 in range(4):
                        scalar.wait_ge(s_sq5.h, tt8 + 1)
                        mark("s", scalar.activation(srow5[:, 5:6], srow5[:, 4:5],
                                                    AF.Sqrt, bias=EPSRMS,
                                                    scale=1.0 / H), s_sd5)
                    scalar.wait_ge(s_fin.h, 4)
                scalar.wait_ge(s_ty.h, it + 1)
                mark("s", scalar.copy(
                    ybuf[:, 0:4, 128 * ho:128 * (ho + 1)],
                    typs[pb][:].rearrange("p (a t) -> p a t", a=4)), s_yw)
            for tt8 in range(4, 8):
                scalar.wait_ge(s_sq5.h, tt8 + 1)
                mark("s", scalar.activation(srow5[:, 5:6], srow5[:, 4:5],
                                            AF.Sqrt, bias=EPSRMS, scale=1.0 / H), s_sd5)

        @blk.vector
        def _(vector):
            def final(tt8):
                b = tt8 % 2
                tt = tt8 % 4
                vector.wait_ge(s_x2t[b].h, 16 * (tt8 // 2 + 1))
                if tt8 >= 1:
                    vector.wait_ge(s_sd5.h, tt8)
                bar("v", vector)
                mark("v", vector.scalar_tensor_tensor(
                    ysq[:], ybuf[:, tt, :], 1.0, ybuf[:, tt, :],
                    ALU.mult, ALU.mult, accum_out=srow5[:, 4:5]), s_sq5)
                vector.wait_ge(s_sd5.h, tt8 + 1)
                bar("v", vector)
                mark("v", vector.reciprocal(srow5[:, 6:7], srow5[:, 5:6]))
                if tt8 >= 2:
                    vector.wait_ge(s_outst[tt8 % 2].h, 16 * (tt8 // 2))
                bar("v", vector)
                mark("v", vector.scalar_tensor_tensor(
                    outb[b][:], ybuf[:, tt, :], srow5[:, 6:7], x2t[b][:],
                    ALU.mult, ALU.add), s_fin)

            for it in range(32):
                pb = it % 2
                vector.wait_ge(s_dmm.h, it + 1)
                if it >= 2:
                    vector.wait_ge(s_ty.h, it - 1)
                bar("v", vector)
                mark("v", vector.tensor_copy(yst[pb][:], dacc[pb][:]), s_yev)
                if it == 16:
                    vector.wait_ge(s_yw.h, 16)
                    for tt8 in range(4):
                        final(tt8)
            vector.wait_ge(s_yw.h, 32)
            for tt8 in range(4, 8):
                final(tt8)

        @blk.gpsimd
        def _(g):
            g.wait_ge(s_x2sp.h, 16 * 8)
            s_x2t[0].inc(g.dma_start(x2t[0][:], x2_sp[0]), 16)
            s_x2t[1].inc(g.dma_start(x2t[1][:], x2_sp[1]), 16)
            for tt8 in range(8):
                g.wait_ge(s_fin.h, tt8 + 1)
                s_outst[tt8 % 2].inc(g.dma_start(out[tt8], outb[tt8 % 2][:]), 16)
                if tt8 + 2 < 8:
                    s_x2t[tt8 % 2].inc(g.dma_start(x2t[tt8 % 2][:], x2_sp[tt8 + 2]), 16)
            g.wait_ge(s_outst[0].h, 64)
            g.wait_ge(s_outst[1].h, 64)

    p5b.close()


def _host_prep(L0_attn_hidden_states, L0_attn_Wqkv, L0_attn_q_norm_w, L0_attn_k_norm_w,
               L0_attn_Wo, L0_ffn_gate_up_packed, L0_ffn_gate_up_scales,
               L0_ffn_down_packed, L0_ffn_down_scales):
    x = np.ascontiguousarray(L0_attn_hidden_states)
    wqkv = np.ascontiguousarray(
        L0_attn_Wqkv.astype(np.float32).reshape(NH, 128, 48, 128).transpose(2, 1, 0, 3))
    wo_ = np.ascontiguousarray(
        L0_attn_Wo.astype(np.float32).reshape(NH, 128, 16, 128).transpose(2, 1, 0, 3))
    qw_ = np.ascontiguousarray(L0_attn_q_norm_w, dtype=np.float32).reshape(NH, 128).T.copy()
    kw_ = np.ascontiguousarray(L0_attn_k_norm_w, dtype=np.float32).reshape(NH, 128).T.copy()
    gup16 = np.ascontiguousarray(L0_ffn_gate_up_packed).view(np.int16)
    gus_ = np.ascontiguousarray(L0_ffn_gate_up_scales)
    dnp16 = np.ascontiguousarray(L0_ffn_down_packed).view(np.int16)
    dns_ = np.ascontiguousarray(L0_ffn_down_scales)

    gup16m_l, gusm_l, dnp16m_l, dnsm_l = [], [], [], []
    for r in range(8):
        us = [r, r + 8, (r + 16) if r + 16 < 22 else r]
        gup16m_l.append(np.concatenate([gup16[512 * u:512 * (u + 1), :] for u in us], axis=0))
        gusm_l.append(np.ascontiguousarray(
            np.concatenate([gus_[:, 512 * u:512 * (u + 1)] for u in us], axis=1)))
        dnp16m_l.append(np.ascontiguousarray(dnp16[256 * r:256 * (r + 1), :]))
        dnsm_l.append(np.ascontiguousarray(dns_[:, 256 * r:256 * (r + 1)]))

    identb = np.eye(128, dtype=np.float32).astype(ml_dtypes.bfloat16)
    identf = np.eye(128, dtype=np.float32)
    onesrow = np.ones((1, 128), dtype=np.float32)
    onescol = np.ones((128, 1), dtype=np.float32).astype(ml_dtypes.bfloat16)
    bcast4_ = np.zeros((4, 128), dtype=np.float32)
    for g in range(4):
        bcast4_[g, 32 * g:32 * (g + 1)] = 0.5
    bcast4_ = bcast4_.astype(ml_dtypes.bfloat16)
    constv = np.zeros((128, 4), dtype=np.float32)
    constv[:, 0] = EPS_LN
    constv[:, 1] = EPS_RMS

    kl = np.arange(128)[:, None]
    ql = np.arange(128)[None, :]
    ctri_ = np.where(kl <= ql, 1.0, 0.0).astype(ml_dtypes.bfloat16)
    ftri_ = np.where(kl >= ql + 1, 1.0, 0.0).astype(ml_dtypes.bfloat16)
    zeros_ = np.zeros((128, 128), dtype=ml_dtypes.bfloat16)
    ones_ = np.ones((128, 128), dtype=ml_dtypes.bfloat16)
    vones_ = np.ones((128, 9 * NH), dtype=ml_dtypes.bfloat16)

    inv = (1.0 / (10000.0 ** (np.arange(0, HD, 2, dtype=np.float64) / HD)))

    in_maps = []
    for c in range(8):
        b, j = c // 4, c % 4
        own = x[b, 1024 * j:1024 * (j + 1)]
        halo = x[b, 1024 * (j - 1):1024 * j] if j > 0 else own
        xh_ = np.concatenate([halo, own], axis=0).reshape(16, 128, H)
        pos0 = 1024 * j - 1024
        pos = (pos0 + np.arange(TALL, dtype=np.float64))
        ang = pos[None, :] * inv[:, None]
        cosd_ = np.tile(np.cos(ang), (2, 1)).astype(ml_dtypes.bfloat16)
        sind_ = np.tile(np.sin(ang), (2, 1)).astype(ml_dtypes.bfloat16)
        if j == 0:
            fmask_ = zeros_
            vm01_ = zeros_
        else:
            fmask_ = ftri_
            vm01_ = ones_
        in_maps.append(dict(
            xh=xh_, wqkv=wqkv, wo=wo_, qw=qw_, kw=kw_,
            gup16m=gup16m_l[c], gusm=gusm_l[c], dnp16m=dnp16m_l[c], dnsm=dnsm_l[c],
            cosd=cosd_, sind=sind_, identb=identb,
            identf=identf, fmask=fmask_, ctri=ctri_, vm01=vm01_, vones=vones_,
            onesrow=onesrow, onescol=onescol, bcast4=bcast4_, constv=constv))
    return in_maps


def kernel(**inputs):
    in_maps = _host_prep(**inputs)
    nc, _ = build()
    res = run_bass_kernel_spmd(nc, in_maps, core_ids=list(range(8)))
    out = np.zeros((2, 4096, H), dtype=np.float32)
    for c in range(8):
        b, j = c // 4, c % 4
        out[b, 1024 * j:1024 * (j + 1)] = np.asarray(res.results[c]["out"]).reshape(1024, H)
    return out
